# version 05
# baseline (speedup 1.0000x reference)
import numpy as np
import ml_dtypes
import concourse.bass as bass
import concourse.mybir as mybir

F32 = mybir.dt.float32
BF16 = mybir.dt.bfloat16
AF = mybir.ActivationFunctionType
ALU = mybir.AluOpType
AX = mybir.AxisListType

NCORES = 8
DBG_CAPTURE = None
P1_LIMIT = None
SKIP_HEADLOOP = False
HL_PARTS = ("bld1", "bld2", "bld4", "ktr", "x0", "powers", "applies", "post", "compose")
HL_PARTS2 = None
H, DK, DV = 16, 64, 128
KEY_DIM, VAL_DIM = H * DK, H * DV
C = 2048
HID = 8192
L = 128
CMAGIC = 12582912.0
ATANH_HALF = 0.5493061443340549
EPS_ORMS = 1.1920929e-07


_DRAIN_OPS = {
    "tensor_scalar", "tensor_single_scalar", "tensor_scalar_add", "tensor_scalar_sub",
    "tensor_scalar_mul", "tensor_scalar_max", "tensor_scalar_min", "tensor_tensor",
    "tensor_add", "tensor_sub", "tensor_mul", "tensor_max", "tensor_copy", "tensor_relu",
    "scalar_tensor_tensor", "tensor_tensor_scan", "tensor_reduce", "reduce_sum",
    "reduce_max", "reciprocal", "memset", "memzero", "select", "copy_predicated",
    "activation", "copy", "add", "mul", "square", "sqrt", "sign", "transpose",
    "stream_shuffle",
}


class DrainingEngine:

    def __init__(self, eng):
        object.__setattr__(self, "_e", eng)

    def xcopy(self, out, in_):
        e = self._e
        e.drain()
        if hasattr(e, "tensor_copy"):
            return e.tensor_copy(out, in_)
        return e.copy(out, in_)

    def xmul(self, out, in_, s):
        e = self._e
        e.drain()
        if hasattr(e, "tensor_scalar_mul"):
            return e.tensor_scalar_mul(out, in_, s)
        return e.mul(out, in_, s)

    def __getattr__(self, name):
        attr = getattr(self._e, name)
        if name in _DRAIN_OPS and callable(attr):
            e = self._e

            def call(*a, **k):
                e.drain()
                return attr(*a, **k)
            return call
        return attr


class Sched:
    ENGINES = ("sync", "tensor", "vector", "scalar", "gpsimd")

    def __init__(self, nc):
        self.nc = nc
        self.events = []
        self.sems = {}
        self.cnt = {}
        self.cur_phase = "init"

    def phase(self, name):
        self.cur_phase = name

    def sem(self, name):
        if name not in self.sems:
            self.sems[name] = self.nc.semaphore(name).__enter__()
            self.cnt[name] = 0
        return self.sems[name]

    def bump(self, name, amt):
        self.sem(name)
        self.cnt[name] += amt
        return self.cnt[name]

    def val(self, name):
        self.sem(name)
        return self.cnt[name]

    def emit(self, engine, fn, waits=(), inc=None):
        assert engine in self.ENGINES, engine
        wl = [(self.sems[s], v) for (s, v) in waits if v > 0]
        ih = None
        if inc is not None:
            self.bump(inc[0], inc[1])
            ih = (self.sems[inc[0]], inc[1])

        def wrapped(eng, fn=fn, wl=wl, ih=ih):
            for (s, v) in wl:
                eng.wait_ge(s, v)
            ins = fn(eng)
            if ih is not None:
                assert ins is not None, "emit with inc: fn must return an instruction"
                ins.then_inc(ih[0], ih[1])
        self.events.append((engine, wrapped, self.cur_phase))

    def flush_dma(self):
        pend = [(nm, self.cnt[nm]) for nm in list(self.cnt)
                if nm.startswith("q_") and self.cnt[nm] > 0]

        def fl(eng, pend=pend):
            for nm, v in pend:
                eng.wait_ge(self.sems[nm], v)
        self.events.append(("sync", fl, self.cur_phase))

    def barrier(self):
        self.flush_dma()
        vals = {}
        for e in self.ENGINES:
            sn = f"bar_{e}"
            vals[sn] = self.bump(sn, 1)
            sem = self.sems[sn]

            def post(eng, sem=sem):
                eng.drain()
                return eng.nop().then_inc(sem, 1)
            self.events.append((e, post, self.cur_phase))
        allv = dict(vals)

        def waitall(eng, allv=allv):
            for sn, v in allv.items():
                eng.wait_ge(self.sems[sn], v)
        for e in self.ENGINES:
            self.events.append((e, waitall, self.cur_phase))
        return allv

    def replay(self, block):
        evs = self.events
        nc = self.nc

        def run(eng, sel, wrap=False):
            d = DrainingEngine(eng) if wrap else eng
            cur = None
            sid = None
            for (e, fn, ph) in evs:
                if e != sel:
                    continue
                if ph != cur:
                    if cur is not None:
                        nc.leave_named_scope(cur, sid, False)
                    cur = ph
                    sid, _ = nc.enter_named_scope(cur, False)
                fn(d)
            if cur is not None:
                nc.leave_named_scope(cur, sid, False)

        @block.sync
        def _(eng):
            run(eng, "sync")

        @block.tensor
        def _(eng):
            run(eng, "tensor")

        @block.vector
        def _(eng):
            run(eng, "vector", wrap=True)

        @block.scalar
        def _(eng):
            run(eng, "scalar", wrap=True)

        @block.gpsimd
        def _(eng):
            run(eng, "gpsimd", wrap=True)


class Alloc:

    def __init__(self, nc):
        self.nc = nc
        self.groups = {}

    def sb(self, group, name, shape, dt=F32):
        g = self.groups.setdefault(group, [])
        cm = self.nc.sbuf_tensor(name, list(shape), dt)
        h = cm.__enter__()
        nb = shape[1] * (2 if dt == BF16 else 4)
        if not hasattr(self, "cur"):
            self.cur, self.peak, self.peak_at = 0, 0, ""
        self.cur += nb
        if self.cur > self.peak:
            self.peak, self.peak_at = self.cur, name
        g.append((cm, nb))
        return h

    def free(self, group):
        for cm, nb in reversed(self.groups.get(group, [])):
            cm.__exit__(None, None, None)
            self.cur -= nb
        self.groups[group] = []


def host_inputs(inp, ntok=2048):
    nslice = 4
    x = inp["L0_attn_x"]
    B, T, _ = x.shape
    f32 = np.float32

    def cc(a):
        return np.ascontiguousarray(a.astype(f32))

    convall = np.concatenate([
        inp["L0_attn_q_conv_w"][:, 0, :],
        inp["L0_attn_k_conv_w"][:, 0, :],
        inp["L0_attn_v_conv_w"][:, 0, :],
    ], axis=0)
    convw = cc(convall.reshape(32, 128, 4).transpose(1, 0, 2).reshape(128, 128))
    cm = inp["L0_c"]
    c_fm = cc(cm.reshape(2, 16, 128).transpose(2, 1, 0).reshape(128, 32))
    ident = np.eye(128, dtype=f32)
    identbf = np.eye(128, dtype=ml_dtypes.bfloat16)
    ar = np.arange(L)
    mneg_s = np.where(ar[:, None] < ar[None, :], 0.0, -1e30).astype(f32)
    mneg_i = np.where(ar[:, None] <= ar[None, :], 0.0, -1e30).astype(f32)
    mneg_sT = np.ascontiguousarray(mneg_s.T)

    def mask4(m):
        return np.ascontiguousarray(np.tile(m, (1, 4)))
    headmask = np.zeros((128, 2), f32)
    headmask[:64, 0] = 1.0
    headmask[64:, 1] = 1.0
    maskT_q = np.zeros((2, 128), f32)
    maskT_q[0, :64] = DK ** -0.5
    maskT_q[1, 64:] = DK ** -0.5
    maskT_k = (maskT_q > 0).astype(f32)
    ohsel = np.zeros((16, 2048), f32)
    for hh in range(16):
        ohsel[hh, hh * 128:(hh + 1) * 128] = 1.0

    shared = {
        "qkT": cc(inp["L0_attn_qk_w"].T), "vgT": cc(inp["L0_attn_vg_w"].T),
        "owT": cc(inp["L0_attn_o_w"].T),
        "abT": cc(np.concatenate([inp["L0_attn_a_w"].T, inp["L0_attn_b_w"].T], axis=1)),
        "convw": convw, "c_fm": c_fm,
        "b_bias": cc(inp["L0_attn_b_bias"].reshape(16, 1)),
        "beta_bias": cc(inp["L0_attn_beta_bias"].reshape(16, 1)),
        "dt_bias": cc(inp["L0_attn_dt_bias"].reshape(16, 1)),
        "A_log": cc(inp["L0_attn_A_log"].reshape(16, 1)),
        "g1v": cc(inp["L0_ffn_g1"].reshape(1, C)),
        "g2v": cc(inp["L0_ffn_g2"].reshape(1, C)),
        "g3v": cc(inp["L0_ffn_g3"].reshape(1, HID)),
        "ident": ident, "identbf": identbf,
        "mneg_s4": mask4(mneg_s), "mneg_i4": mask4(mneg_i), "mneg_sT4": mask4(mneg_sT),
        "headmask": headmask, "maskT_q": maskT_q, "maskT_k": maskT_k,
        "ones128x1": np.ones((128, 1), f32), "onesrow": np.ones((16, L), f32),
        "ohsel": ohsel,
    }
    in_maps = []
    for core in range(NCORES):
        b, p = core // nslice, core % nslice
        t0 = p * ntok
        m = dict(shared)
        xs = np.zeros((ntok + 3, C), f32)
        if t0 >= 3:
            xs[:] = x[b, t0 - 3:t0 + ntok]
        else:
            xs[3 - t0:] = x[b, 0:t0 + ntok]
        m["x_sl"] = cc(xs)
        m["haloflag"] = np.array([[0.0 if t0 == 0 else 1.0]], f32)
        m["wmod_sl"] = cc(inp["L0_wmod"][:, core * 1536:(core + 1) * 1536])
        selb = np.zeros((1, 2), f32)
        selb[0, b] = 1.0
        m["selb"] = selb
        selp = np.zeros((1, 24), f32)
        for j in range(3):
            r = core - 1 - j
            if r >= b * nslice:
                selp[0, j * 8 + r] = 1.0
        m["selp"] = selp
        m["w1T_sl"] = cc(inp["L0_ffn_w1"].T[core * 256:(core + 1) * 256])
        m["w2T_sl"] = cc(inp["L0_ffn_w2"].T[core * 256:(core + 1) * 256])
        m["w3T_sl"] = cc(inp["L0_ffn_w3"].T[core * 1024:(core + 1) * 1024])
        in_maps.append(m)
    return in_maps


CONSTS_CORE = [
    ("identbf", [128, 128], BF16),
    ("ones128x1", [128, 1], F32), ("onesrow", [16, L], F32),
    ("selb", [1, 2], F32), ("headmask", [128, 2], F32),
    ("b_bias", [16, 1], F32), ("beta_bias", [16, 1], F32),
    ("dt_bias", [16, 1], F32), ("A_log", [16, 1], F32),
]
CONSTS_P1 = [
    ("convw", [128, 128], F32), ("c_fm", [128, 32], F32),
    ("ident", [128, 128], F32),
    ("mneg_s4", [128, 512], F32), ("mneg_i4", [128, 512], F32), ("mneg_sT4", [128, 512], F32),
    ("maskT_q", [2, 128], F32), ("maskT_k", [2, 128], F32),
    ("ohsel", [16, 2048], F32),
]
CONSTS = CONSTS_CORE + CONSTS_P1


def build(ntok=2048, debug_taps=(), maxphase=99):
    nch = ntok // L
    NTP = ntok + 4
    nc = bass.Bass("TRN2", target_bir_lowering=False, debug=False)

    def din(name, shape, dt=F32):
        return nc.dram_tensor(name, list(shape), dt, kind="ExternalInput").ap()

    def dout(name, shape, dt=F32):
        return nc.dram_tensor(name, list(shape), dt, kind="ExternalOutput").ap()

    I = {}
    I["x_sl"] = din("x_sl", [ntok + 3, C])
    I["qkT"] = din("qkT", [C, C])
    I["vgT"] = din("vgT", [C, 2 * VAL_DIM])
    I["owT"] = din("owT", [VAL_DIM, C])
    I["abT"] = din("abT", [C, 32])
    I["g1v"] = din("g1v", [1, C])
    I["g2v"] = din("g2v", [1, C])
    I["g3v"] = din("g3v", [1, HID])
    I["wmod_sl"] = din("wmod_sl", [C, 1536])
    I["w1T_sl"] = din("w1T_sl", [256, HID])
    I["w2T_sl"] = din("w2T_sl", [256, HID])
    I["w3T_sl"] = din("w3T_sl", [1024, C])
    for (nm, shape, dt) in CONSTS:
        I[nm] = din(nm, shape, dt)
    I["haloflag"] = din("haloflag", [1, 1])
    I["selp"] = din("selp", [1, 24])

    O = {"y": dout("y", [ntok, C])}
    taps = {}

    def mktap(name, shape, dt=F32):
        if name in debug_taps:
            taps[name] = dout("tap_" + name, list(shape), dt)

    mktap("h_fm", [128, 16 * NTP], BF16)
    mktap("modvec", [8, C])
    mktap("q_fm", [128, 8 * ntok], BF16)
    mktap("k_fm", [128, 8 * ntok], BF16)
    mktap("vsp", [nch * 16 * 128, 128], BF16)
    mktap("gsp", [VAL_DIM, ntok], BF16)
    mktap("cum", [128, nch * 16])
    mktap("beta", [128, nch * 16])
    mktap("bg", [128, nch * 16])
    mktap("kbs", [128, nch * 16])
    mktap("glrow", [1, nch * 16])
    mktap("negT", [64, nch * 16 * 64], BF16)
    mktap("Yc", [64, nch * 16 * 128], BF16)
    mktap("ouT", [nch * 16 * 128, 128], BF16)
    mktap("zT", [nch * 16 * 64, 128], BF16)
    mktap("pq", [H * 64, 192], BF16)
    mktap("sinit", [64, 16 * 128])
    mktap("x1", [ntok, C])
    mktap("i1", [16 * nch * 128, 128], BF16)
    mktap("hmid", [ntok, HID], BF16)
    mktap("f2sc", [128, 6 * 16])
    mktap("i3h", [128, 64 * 128], BF16)

    Dmodpart = nc.dram_tensor("Dmodpart", [2, 1536], F32)
    Dmodgath = nc.dram_tensor("Dmodgath", [16, 1536], F32)
    Dmodvec = nc.dram_tensor("Dmodvec", [8, C], F32)
    Dwsum_in = nc.dram_tensor("Dwsum_in", [1, 4], F32)
    Dwsum_out = nc.dram_tensor("Dwsum_out", [1, 4], F32)
    Dwq_in = nc.dram_tensor("Dwq_in", [768, HID], BF16)
    Dwq_gath = nc.dram_tensor("Dwq_gath", [NCORES * 768, HID], BF16)
    Dpq_in = nc.dram_tensor("Dpq_in", [H * 64, 192], BF16)
    Dpq_gath = nc.dram_tensor("Dpq_gath", [NCORES * H * 64, 192], BF16)
    Dvsp = nc.dram_tensor("Dvsp", [nch * 16 * 128, 128], BF16)
    Dgsp = nc.dram_tensor("Dgsp", [VAL_DIM, ntok], BF16)
    Dou = nc.dram_tensor("Dou", [nch * 16 * 128, 128], BF16)
    Dzt = nc.dram_tensor("Dzt", [nch * 16 * 64, 128], BF16)
    Dx1 = nc.dram_tensor("Dx1", [ntok, C], F32)
    Di1 = nc.dram_tensor("Di1", [16 * nch * 128, 128], BF16)
    Di2 = nc.dram_tensor("Di2", [16 * nch * 128, 128], BF16)
    Dh = nc.dram_tensor("Dh", [ntok, HID], BF16)
    Dq = nc.dram_tensor("Dq", [1024, ntok], BF16)
    Dk = nc.dram_tensor("Dk", [1024, ntok], BF16)
    DnegT = nc.dram_tensor("DnegT", [nch * 16 * 64, 64], BF16)
    Dog = nc.dram_tensor("Dog", [nch * 128, 16 * 128], BF16)

    S = Sched(nc)
    A = Alloc(nc)
    for s in ["ld", "st", "gld", "gst", "pe", "dve", "act", "pool",
              "cc_mod", "cc_wsum", "cc_wq", "cc_pq"]:
        S.sem(s)
    for e in S.ENGINES:
        S.sem(f"bar_{e}")

    cst = {nm: A.sb("const", "c_" + nm, shape, dt) for (nm, shape, dt) in CONSTS_CORE}
    cst["haloflag"] = A.sb("const", "c_haloflag", [128, 1], F32)
    cst["selp"] = A.sb("const", "c_selp", [128, 24], F32)
    cvals = A.sb("const", "cvals", [128, 8], F32)
    smallv = A.sb("const", "smallv", [16, 16], F32)
    fsc = A.sb("const", "fsc", [1, 16], F32)
    fscb = A.sb("const", "fscb", [128, 8], F32)
    thb = A.sb("const", "thb", [128, 8], F32)
    s1f = A.sb("fsc2", "s1f", [128, nch], F32)
    s2f = A.sb("fsc2", "s2f", [128, nch], F32)
    s3f = A.sb("fsc2", "s3f", [128, nch], F32)
    mx3 = A.sb("fsc2", "mx3", [128, nch], F32)
    ss3 = A.sb("fsc2", "ss3", [128, nch], F32)
    sc3 = A.sb("fsc2", "sc3", [128, 4], F32)
    st_cum = A.sb("stats", "st_cum", [128, nch * 16], F32)
    st_beta = A.sb("stats", "st_beta", [128, nch * 16], F32)
    st_bg = A.sb("stats", "st_bg", [128, nch * 16], F32)
    st_kbs = A.sb("stats", "st_kbs", [128, nch * 16], F32)
    st_cumb = A.sb("stats", "st_cumb", [128, nch * 16], F32)
    gl_row = A.sb("stats", "gl_row", [1, nch * 16], F32)
    glb = A.sb("stats", "glb", [128, nch * 16], F32)
    Ycs = A.sb("stats", "Ycs", [64, 2 * 128], BF16)
    for _nm, _shape, _dt in CONSTS_P1:
        cst[_nm] = A.sb("constp1", "c_" + _nm, _shape, _dt)

    pb = [nc.psum_tensor(f"pb{i}", [128, 512], F32).__enter__() for i in range(6)]
    pbf = nc.psum_tensor("pbf", [128, 2048], BF16).__enter__()
    bank_guard = {}

    def dma(engine, out, in_, waits=(), sem=None, stream=None):
        if stream is not None:
            sem = "q_" + stream
        if sem is None:
            sem = "gld" if engine == "gpsimd" else "ld"
        S.sem(sem)
        S.emit(engine, lambda eng, o=out, i=in_: eng.dma_start(out=o, in_=i), waits=waits, inc=(sem, 16))
        return (sem, S.val(sem))

    def cinit(eng):
        eng.memset(cvals[:, 0:1], 1e-5)
        eng.memset(cvals[:, 1:2], 1e-6)
        eng.memset(cvals[:, 2:3], EPS_ORMS)
        eng.memset(cvals[:, 3:4], 1e-8)
        return eng.memset(cvals[:, 4:5], 1.0)
    S.emit("vector", cinit, inc=("dve", 1))
    cinit_v = S.val("dve")
    for nm, t in cst.items():
        if nm in ("haloflag", "selp"):
            dma("sync", t[:, :], I[nm].partition_broadcast(128), stream="c")
        else:
            dma("sync", t[:, :], I[nm], stream="c")
    ld_consts = ("q_c", S.val("q_c"))

    S.phase("mod")
    g0 = "ph0"
    mwork = A.sb(g0, "mwork", [128, 8192], BF16)
    mfp = A.sb(g0, "mfp", [128, 8192], F32)
    cbf = A.sb(g0, "cbf", [128, 32], BF16)
    mrow = A.sb(g0, "mrow", [1, 12288], F32)
    mr2 = A.sb(g0, "mr2", [1, 2048], F32)
    mr3 = A.sb(g0, "mr3", [1, 2048], F32)
    g2t = A.sb(g0, "g2t", [2, 1536], F32)
    g2a = A.sb(g0, "g2a", [1, 1536], F32)
    g2b = A.sb(g0, "g2b", [1, 1536], F32)

    S.emit("scalar", lambda eng: eng.activation(cst["c_fm"][:, :], cst["c_fm"][:, :], AF.Silu),
           waits=[ld_consts], inc=("act", 1))
    S.emit("scalar", lambda eng: eng.copy(cbf[:, :], cst["c_fm"][:, :]), inc=("act", 1))
    silu_c = S.val("act")

    prev_pe = 0
    prev_cast = 0
    modbank_dve = [0, 0]
    for j in range(3):
        src = I["wmod_sl"][:, j * 512:(j + 1) * 512].rearrange("(ic p) o -> p ic o", p=128)
        ldv = dma("sync", mfp[:, :].rearrange("p (ic o) -> p ic o", ic=16), src,
                  waits=[("pe", prev_pe), ("act", prev_cast)], stream="mfp")
        S.emit("scalar", lambda eng: eng.copy(mwork[:, :], mfp[:, :]), waits=[ldv], inc=("act", 1))
        cv = S.val("act")
        prev_cast = cv

        def mm(eng, j=j):
            last = None
            for ic in range(16):
                last = eng.matmul(pb[j % 2][0:2, :], cbf[:, 2 * ic:2 * ic + 2],
                                  mwork[:, ic * 512:(ic + 1) * 512],
                                  start=(ic == 0), stop=(ic == 15))
            return last
        S.emit("tensor", mm, waits=[("act", cv), ("act", silu_c), ("dve", modbank_dve[j % 2])],
               inc=("pe", 1))
        prev_pe = S.val("pe")
        S.emit("vector", lambda eng, j=j: eng.tensor_copy(g2t[0:2, j * 512:(j + 1) * 512], pb[j % 2][0:2, :]),
               waits=[("pe", prev_pe)], inc=("dve", 1))
        modbank_dve[j % 2] = S.val("dve")
    mod_st = dma("sync", Dmodpart.ap(), g2t[0:2, :], waits=[("dve", S.val("dve"))], stream="stmod")
    S.emit("gpsimd", lambda eng: eng.collective_compute(
        "AllGather", ALU.bypass, replica_groups=[list(range(NCORES))],
        ins=[Dmodpart.ap().opt()], outs=[Dmodgath.ap().opt()]),
        waits=[mod_st], inc=("cc_mod", 1))
    for r in range(NCORES):
        dma("sync", g2a[0:1, :], Dmodgath.ap()[2 * r:2 * r + 1, :],
            waits=[("cc_mod", 1), ("dve", S.val("dve"))], stream="gath")
        ldv = dma("sync", g2b[0:1, :], Dmodgath.ap()[2 * r + 1:2 * r + 2, :], stream="gath")

        def blend(eng, r=r):
            eng.tensor_scalar(mr2[0:1, 0:1536], g2a[0:1, 0:1536], cst["selb"][0:1, 0:1], None, ALU.mult)
            return eng.scalar_tensor_tensor(mrow[0:1, r * 1536:(r + 1) * 1536], g2b[0:1, 0:1536],
                                            cst["selb"][0:1, 1:2], mr2[0:1, 0:1536], ALU.mult, ALU.add)
        S.emit("vector", blend, waits=[ldv], inc=("dve", 1))

    def vs(i):
        return mrow[0:1, i * 2048:(i + 1) * 2048]
    S.emit("vector", lambda eng: eng.tensor_scalar(vs(1), vs(1), 1.0, None, ALU.add), inc=("dve", 1))
    S.emit("vector", lambda eng: eng.tensor_scalar(vs(4), vs(4), 1.0, None, ALU.add), inc=("dve", 1))
    for iv in (2, 5):
        def clamp(eng, iv=iv):
            eng.tensor_scalar(mr2[0:1, :], vs(iv), 0.0, None, ALU.is_ge)
            eng.tensor_scalar(mr2[0:1, :], mr2[0:1, :], 2.0, -1.0, ALU.mult, ALU.add)
            eng.tensor_scalar(mr3[0:1, :], vs(iv), -1.0, None, ALU.mult)
            eng.tensor_tensor(mr3[0:1, :], mr3[0:1, :], vs(iv), ALU.max)
            eng.tensor_scalar(mr3[0:1, :], mr3[0:1, :], 0.25, None, ALU.max)
            return eng.tensor_mul(vs(iv), mr2[0:1, :], mr3[0:1, :])
        S.emit("vector", clamp, inc=("dve", 1))
    modproc = S.val("dve")
    for i, v in enumerate([1, 0, 2, 4, 3, 5]):
        modvec_st = dma("sync", Dmodvec.ap()[i:i + 1, :], vs(v), waits=[("dve", modproc)], stream="stmv")
    if "modvec" in taps:
        for i in range(6):
            dma("sync", taps["modvec"][i:i + 1, :], Dmodvec.ap()[i:i + 1, :], waits=[modvec_st], stream="sttap")
    A.free(g0)
    bar0 = S.barrier()

    S.phase("wq")
    Dfsc = nc.dram_tensor("Dfsc", [1, 16], F32)
    gw = "phW"
    wfq = A.sb(gw, "wfq", [128, HID], F32)
    wqo = A.sb(gw, "wqo", [128, HID], BF16)
    wsg = A.sb(gw, "wsg", [128, HID], F32)
    wcs = A.sb(gw, "wcs", [128, 4], F32)
    wrow = A.sb(gw, "wrow", [1, 32], F32)

    S.emit("vector", lambda eng: eng.memset(wcs[:, :], 0.0), inc=("dve", 1))
    wslices = [("w1T_sl", I["w1T_sl"], 0), ("w2T_sl", I["w2T_sl"], 1), ("w3T_sl", I["w3T_sl"], 2)]
    prev_red = S.val("dve")
    for nm, ap, col in wslices:
        flat = ap.rearrange("a b -> (a b)") if nm == "w3T_sl" else None
        for half in range(2):
            if nm == "w3T_sl":
                srcv = ap.rearrange("(x a) b -> x (a b)", x=2)[half:half + 1, :].rearrange(
                    "x (p c) -> (x p) c", p=128)
            else:
                srcv = ap[half * 128:(half + 1) * 128, :]
            ldv = dma("sync", wfq[:, :], srcv, waits=[("dve", prev_red)], stream="wq")
            S.emit("vector", lambda eng: eng.tensor_reduce(wcs[:, 3:4], wfq[:, :], AX.X, ALU.add,
                                                           apply_absolute_value=True),
                   waits=[ldv], inc=("dve", 1))
            S.emit("vector", lambda eng, col=col: eng.tensor_add(wcs[:, col:col + 1], wcs[:, col:col + 1],
                                                                 wcs[:, 3:4]),
                   inc=("dve", 1))
            prev_red = S.val("dve")
    S.emit("tensor", lambda eng: eng.matmul(pb[0][0:1, 0:4], cst["ones128x1"][:, 0:1], wcs[:, 0:4],
                                            start=True, stop=True),
           waits=[("dve", prev_red), ("dve", S.val("dve"))], inc=("pe", 1))
    S.emit("vector", lambda eng: eng.tensor_scalar(wrow[0:1, 0:4], pb[0][0:1, 0:4], 1.0 / (HID * C), None,
                                                   ALU.mult),
           waits=[("pe", S.val("pe"))], inc=("dve", 1))
    wsum_st = dma("sync", Dwsum_in.ap(), wrow[0:1, 0:4], waits=[("dve", S.val("dve"))], stream="stws")
    S.emit("gpsimd", lambda eng: eng.collective_compute(
        "AllReduce", ALU.add, replica_groups=[list(range(NCORES))],
        ins=[Dwsum_in.ap().opt()], outs=[Dwsum_out.ap().opt()]),
        waits=[wsum_st], inc=("cc_wsum", 1))
    ldw = dma("sync", wrow[0:1, 0:4], Dwsum_out.ap(), waits=[("cc_wsum", 1)], stream="wq")
    def wstats(eng):
        eng.tensor_scalar(wrow[0:1, 4:8], wrow[0:1, 0:4], 1e-8, ATANH_HALF, ALU.add, ALU.mult)
        eng.tensor_scalar(wrow[0:1, 8:12], wrow[0:1, 4:8], -1.0, None, ALU.mult)
        return eng.tensor_scalar(wrow[0:1, 12:16], wrow[0:1, 0:4], -1.0, None, ALU.mult)
    S.emit("vector", wstats, waits=[ldw], inc=("dve", 1))
    S.emit("scalar", lambda eng: eng.activation(wrow[0:1, 16:20], wrow[0:1, 0:4], AF.Ln, bias=1.0),
           waits=[("dve", S.val("dve"))], inc=("act", 1))
    S.emit("scalar", lambda eng: eng.activation(wrow[0:1, 20:24], wrow[0:1, 12:16], AF.Ln, bias=1.0),
           inc=("act", 1))

    def wfin(eng):
        eng.tensor_sub(wrow[0:1, 24:28], wrow[0:1, 16:20], wrow[0:1, 20:24])
        return eng.tensor_scalar(wrow[0:1, 24:28], wrow[0:1, 24:28], 0.5 / 127.0 * 0.5, None, ALU.mult)
    S.emit("vector", wfin, waits=[("act", S.val("act"))], inc=("dve", 1))
    wv = S.val("dve")
    dma("sync", Dfsc.ap()[0:1, 0:4], wrow[0:1, 24:28], waits=[("dve", wv)], stream="stfsc")
    dma("sync", Dfsc.ap()[0:1, 4:8], wrow[0:1, 8:12], waits=[("dve", wv)], stream="stfsc")
    fsc_st = dma("sync", Dfsc.ap()[0:1, 8:12], wrow[0:1, 4:8], waits=[("dve", wv)], stream="stfsc")
    dma("sync", fscb[:, 0:4], Dfsc.ap()[0:1, 0:4].partition_broadcast(128), waits=[fsc_st], stream="thld")
    ldth = dma("sync", thb[:, 0:8], Dfsc.ap()[0:1, 4:12].partition_broadcast(128), waits=[fsc_st], stream="thld")
    fscb_ld = ldth
    state_wq = {}
    prev_q = 0
    for nm, ap, col in wslices:
        for half in range(2):
            if nm == "w3T_sl":
                srcv = ap.rearrange("(x a) b -> x (a b)", x=2)[half:half + 1, :].rearrange(
                    "x (p c) -> (x p) c", p=128)
            else:
                srcv = ap[half * 128:(half + 1) * 128, :]
            ldv = dma("sync", wfq[:, :], srcv, waits=[("act", prev_q), ("dve", S.val("dve"))], stream="wq")
            S.emit("scalar", lambda eng, col=col: eng.activation(wsg[:, :], wfq[:, :], AF.Sign,
                                                                 bias=thb[:, col:col + 1]),
                   waits=[ldv, ldth], inc=("act", 1))
            S.emit("scalar", lambda eng, col=col: eng.activation(wfq[:, :], wfq[:, :], AF.Sign,
                                                                 bias=thb[:, 4 + col:4 + col + 1]),
                   inc=("act", 1))
            prev_q = S.val("act")
            wtq = [("act", prev_q)]
            if state_wq.get("st"):
                wtq.append(state_wq["st"])
            S.emit("vector", lambda eng: eng.tensor_add(wqo[:, :], wsg[:, :], wfq[:, :]),
                   waits=wtq, inc=("dve", 1))
            state_wq["st"] = dma("sync", Dwq_in.ap()[(col * 2 + half) * 128:(col * 2 + half + 1) * 128, :],
                                 wqo[:, :], waits=[("dve", S.val("dve"))], stream="stwq")
    S.emit("gpsimd", lambda eng: eng.collective_compute(
        "AllGather", ALU.bypass, replica_groups=[list(range(NCORES))],
        ins=[Dwq_in.ap().opt()], outs=[Dwq_gath.ap().opt()]),
        waits=[("q_stwq", S.val("q_stwq"))], inc=("cc_wq", 1))
    barW = S.barrier()
    A.free(gw)
    hm_cum = A.sb("hm", "hm_cum", [16, ntok], F32)
    hm_cumb = A.sb("hm", "hm_cumb", [16, ntok], F32)
    hm_expc = A.sb("hm", "hm_expc", [16, ntok], F32)

    if maxphase < 1:
        _finish(nc, S, O, ntok)
        return nc, taps

    S.phase("A1")
    h_fm = A.sb("hfm", "h_fm", [128, 16 * NTP], BF16)
    S.emit("vector", lambda eng: eng.memset(h_fm[:, :], 0.0), inc=("dve", 1))
    hfm_init = S.val("dve")
    gA = "phA1"
    xt = A.sb(gA, "xt", [128, C], F32)
    ht2 = A.sb(gA, "ht2", [128, C], F32)
    htb = A.sb(gA, "htb", [128, C], BF16)
    stt1 = A.sb(gA, "stt1", [128, 8], F32)
    mbc0 = A.sb(gA, "mbc0", [128, C], F32)
    mbc1 = A.sb(gA, "mbc1", [128, C], F32)

    dma("sync", mbc0[:, :], Dmodvec.ap()[0:1, :].partition_broadcast(128), waits=[modvec_st], stream="mbc")
    dma("sync", mbc1[:, :], Dmodvec.ap()[1:2, :].partition_broadcast(128), waits=[modvec_st], stream="mbc")
    ld_mod = ("q_mbc", S.val("q_mbc"))

    prev_evac = 0
    prev_ld = 0

    def ln_tile(rows, roff, coloff, is_halo):
        nonlocal prev_evac, prev_ld
        ldv = dma("sync", xt[0:rows, :], I["x_sl"][roff:roff + rows, :],
                  waits=[ld_mod, ("dve", prev_ld)], stream="x")
        prev_ld_stats = S.val("dve")
        S.emit("vector", lambda eng: eng.reduce_sum(stt1[0:rows, 0:1], xt[0:rows, :], AX.X),
               waits=[ldv], inc=("dve", 1))
        S.emit("scalar", lambda eng: eng.activation(ht2[0:rows, :], xt[0:rows, :], AF.Square,
                                                    accum_out=stt1[0:rows, 1:2]),
               waits=[ldv], inc=("act", 1))
        vact = S.val("act")

        def stats(eng):
            st = stt1
            eng.tensor_scalar(st[0:rows, 0:1], st[0:rows, 0:1], 1.0 / C, None, ALU.mult)
            eng.tensor_scalar(st[0:rows, 2:3], st[0:rows, 1:2], 1.0 / C, None, ALU.mult)
            eng.tensor_mul(st[0:rows, 3:4], st[0:rows, 0:1], st[0:rows, 0:1])
            return eng.tensor_sub(st[0:rows, 2:3], st[0:rows, 2:3], st[0:rows, 3:4])
        S.emit("vector", stats, waits=[("act", vact)], inc=("dve", 1))
        S.emit("scalar", lambda eng: eng.activation(stt1[0:rows, 4:5], stt1[0:rows, 2:3], AF.Sqrt,
                                                    bias=cvals[0:rows, 0:1]),
               waits=[("dve", S.val("dve")), ("dve", cinit_v)], inc=("act", 1))
        S.emit("vector", lambda eng: eng.reciprocal(stt1[0:rows, 5:6], stt1[0:rows, 4:5]),
               waits=[("act", S.val("act"))], inc=("dve", 1))
        S.emit("vector", lambda eng: eng.tensor_scalar(xt[0:rows, :], xt[0:rows, :],
                                                       stt1[0:rows, 0:1], stt1[0:rows, 5:6],
                                                       ALU.subtract, ALU.mult),
               inc=("dve", 1))
        S.emit("vector", lambda eng: eng.tensor_mul(ht2[0:rows, :], xt[0:rows, :], mbc0[0:rows, :]),
               inc=("dve", 1))
        if is_halo:
            S.emit("vector", lambda eng: eng.tensor_add(ht2[0:rows, :], ht2[0:rows, :], mbc1[0:rows, :]),
                   inc=("dve", 1))
            S.emit("vector", lambda eng: eng.tensor_scalar(htb[0:rows, :], ht2[0:rows, :],
                                                           cst["haloflag"][0:rows, 0:1], None, ALU.mult),
                   inc=("dve", 1))
        else:
            S.emit("vector", lambda eng: eng.tensor_tensor(htb[0:rows, :], ht2[0:rows, :], mbc1[0:rows, :], ALU.add),
                   inc=("dve", 1))
        vdve = S.val("dve")

        def transp(eng):
            last = None
            for ic in range(16):
                last = eng.transpose(pbf[:, ic * 128: ic * 128 + rows],
                                     htb[0:rows, ic * 128:(ic + 1) * 128], cst["identbf"][0:rows, 0:rows])
            return last
        S.emit("tensor", transp, waits=[("dve", vdve), ("dve", max(prev_evac, hfm_init))], inc=("pe", 1))
        vpe = S.val("pe")

        def evac(eng):
            last = None
            for ic in range(16):
                src = pbf[:, ic * 128: ic * 128 + rows]
                last = eng.tensor_copy(h_fm[:, ic * NTP + coloff: ic * NTP + coloff + rows], src)
            return last
        S.emit("vector", evac, waits=[("pe", vpe)], inc=("dve", 1))
        prev_evac = S.val("dve")
        prev_ld = prev_ld_stats
        prev_ld = prev_evac

    ln_tile(3, 0, 0, True)
    for t in range(ntok // 128):
        ln_tile(128, 3 + t * 128, 3 + t * 128, False)
    A.free(gA)
    barA1 = S.barrier()
    if "h_fm" in taps:
        dma("sync", taps["h_fm"], h_fm[:, :], stream="sttap")

    if maxphase < 2:
        _finish(nc, S, O, ntok)
        return nc, taps

    S.phase("A2")
    gB0 = "phA2w"
    wbf = A.sb(gB0, "wbf", [128, 2 * 2048], BF16)
    gB1 = "phA2s"
    raw = A.sb(gB1, "raw", [128, ntok + 3], F32)
    cvb = A.sb(gB1, "cvb", [128, ntok], F32)
    sqb = A.sb(gB1, "sqb", [128, 2 * 512], F32)
    nrm = A.sb(gB1, "nrm", [2, ntok], F32)
    bfnt4 = A.sb(gB1, "bfnt4", [128, 512], BF16)
    bf4 = A.sb(gB1, "bf4", [128, 512], BF16)
    qst = A.sb(gB1, "qst", [128, 2 * 512], BF16)

    ncols = ntok + 3
    grs = []
    c0 = 0
    while c0 < ncols:
        grs.append((c0, min(512, ncols - c0)))
        c0 += 512

    state = {"oc_done_dve": 0, "oc_done_pool": 0, "oc_done_st": 0, "oc_done_act": 0,
             "b01_dve": [0, 0], "vst": [], "vslot": 0, "vbank_dve": [0, 0],
             "sq_pe": [0, 0], "vtp_pe": [0, 0, 0, 0], "g_slot_st": [None, None],
             "qk_st": [None, None], "wslot": 0, "wslot_pe": [0, 0]}

    def proj_oc(kind, oc):
        wsrc, wcol = (I["qkT"], oc * 128) if kind in "qk" else (I["vgT"], oc * 128)
        if kind == "k":
            wcol += KEY_DIM
        if kind == "g":
            wcol += VAL_DIM
        src = wsrc[:, wcol:wcol + 128].rearrange("(ic p) o -> p ic o", p=128)
        wsl = state["wslot"] % 2
        state["wslot"] += 1
        ldv = dma("gpsimd", wbf[:, wsl * 2048:(wsl + 1) * 2048].rearrange("p (ic o) -> p ic o", ic=16),
                  src, waits=[("pe", state["wslot_pe"][wsl])], sem=f"wfg{wsl}")
        for gi, (g0c, w) in enumerate(grs):
            bk = gi % 2

            def mm(eng, g0c=g0c, w=w, bk=bk, wsl=wsl):
                last = None
                for ic in range(16):
                    last = eng.matmul(pb[bk][:, 0:w], wbf[:, wsl * 2048 + ic * 128: wsl * 2048 + (ic + 1) * 128],
                                      h_fm[:, ic * NTP + g0c: ic * NTP + g0c + w],
                                      start=(ic == 0), stop=(ic == 15))
                return last
            S.emit("tensor", mm, waits=[ldv, ("dve", state["b01_dve"][bk])], inc=("pe", 1))
            S.emit("vector", lambda eng, g0c=g0c, w=w, bk=bk: eng.tensor_copy(raw[:, g0c:g0c + w], pb[bk][:, 0:w]),
                   waits=[("pe", S.val("pe")), ("pool", state["oc_done_pool"]),
                          ("dve", state["oc_done_dve"])], inc=("dve", 1))
            state["b01_dve"][bk] = S.val("dve")
        state["wslot_pe"][wsl] = S.val("pe")
        rawv = S.val("dve")

        if kind == "g":
            wt_g = [("dve", rawv)]
            if state.get("last_g_st"):
                wt_g.append(state["last_g_st"])
            S.emit("scalar", lambda eng: eng.activation(cvb[:, 0:ntok], raw[:, 3:3 + ntok], AF.Silu),
                   waits=wt_g, inc=("act", 1))
            sil_g = S.val("act")
            state["oc_done_act"] = sil_g
            gslots = (bf4, bfnt4)
            for gi2, g0c in enumerate(range(0, ntok, 512)):
                w = min(512, ntok - g0c)
                gsl = gi2 % 2
                wt2 = [("act", sil_g)]
                if state["g_slot_st"][gsl] is not None:
                    wt2.append(state["g_slot_st"][gsl])
                else:
                    wt2 += state["vst"][-4:] + [("pe", S.val("pe"))]
                S.emit("vector", lambda eng, g0c=g0c, w=w, gsl=gsl: eng.tensor_copy(
                    gslots[gsl][:, 0:w], cvb[:, g0c:g0c + w]), waits=wt2, inc=("dve", 1))
                state["g_slot_st"][gsl] = dma("sync", Dgsp.ap()[oc * 128:(oc + 1) * 128, g0c:g0c + w],
                                              gslots[gsl][:, 0:w], waits=[("dve", S.val("dve"))],
                                              stream=f"stg{gsl}")
            state["oc_done_dve"] = S.val("dve")
            state["last_g_st"] = state["g_slot_st"][1] or state["g_slot_st"][0]
            return
        cg = {"q": 0, "k": 8, "v": 16}[kind] + oc

        def conv(eng, cg=cg):
            cw = cst["convw"]
            eng.tensor_scalar(cvb[:, 0:ntok], raw[:, 3:3 + ntok], cw[:, cg * 4 + 3:cg * 4 + 4], None, ALU.mult)
            for j in (2, 1, 0):
                last = eng.scalar_tensor_tensor(cvb[:, 0:ntok], raw[:, j:j + ntok],
                                                cw[:, cg * 4 + j:cg * 4 + j + 1], cvb[:, 0:ntok],
                                                ALU.mult, ALU.add)
            return last
        wt_conv = [("dve", rawv), ("act", state["oc_done_act"]), ld_consts]
        if state.get("last_g_st"):
            wt_conv.append(state["last_g_st"])
        S.emit("vector", conv, waits=wt_conv, inc=("dve", 1))
        state["oc_done_pool"] = S.val("pool")
        S.emit("scalar", lambda eng: eng.activation(cvb[:, 0:ntok], cvb[:, 0:ntok], AF.Silu),
               waits=[("dve", S.val("dve"))], inc=("act", 1))
        silv = S.val("act")
        state["oc_done_act"] = silv

        if kind == "v":
            for cch in range(nch):
                slot = state["vslot"] % 4
                state["vslot"] += 1
                bank = slot % 2
                pcol = bank * 1024 + (slot // 2) * 128
                S.emit("vector", lambda eng, cch=cch, slot=slot: eng.tensor_copy(
                    bfnt4[:, slot * 128:(slot + 1) * 128], cvb[:, cch * 128:(cch + 1) * 128]),
                    waits=[("act", silv), ("pe", state["vtp_pe"][slot])], inc=("dve", 1))
                vv = S.val("dve")
                wt = [("dve", vv), ("dve", state["b01_dve"][0]), ("dve", state["b01_dve"][1]),
                      ("dve", state["vbank_dve"][bank])]
                S.emit("tensor", lambda eng, slot=slot, pcol=pcol: eng.transpose(
                    pbf[:, pcol:pcol + 128],
                    bfnt4[:, slot * 128:(slot + 1) * 128], cst["identbf"][:, :]),
                    waits=wt, inc=("pe", 1))
                state["vtp_pe"][slot] = S.val("pe")
                wt2 = [("pe", S.val("pe"))]
                if len(state["vst"]) >= 4:
                    wt2.append(state["vst"][-4])
                S.emit("vector", lambda eng, slot=slot, pcol=pcol: eng.tensor_copy(
                    bf4[:, slot * 128:(slot + 1) * 128], pbf[:, pcol:pcol + 128]),
                    waits=wt2, inc=("dve", 1))
                state["vbank_dve"][bank] = S.val("dve")
                stv = dma("sync", Dvsp.ap()[(cch * 16 + oc) * 128:(cch * 16 + oc + 1) * 128, :],
                          bf4[:, slot * 128:(slot + 1) * 128], waits=[("dve", S.val("dve"))],
                          stream=f"stv{slot}")
                state["vst"].append(stv)
            state["oc_done_dve"] = S.val("dve")
            return
        lastn = 0
        for gi2, g0c in enumerate(range(0, ntok, 512)):
            w = min(512, ntok - g0c)
            sl2 = gi2 % 2
            S.emit("scalar", lambda eng, g0c=g0c, w=w, sl2=sl2: eng.activation(
                sqb[:, sl2 * 512:sl2 * 512 + w], cvb[:, g0c:g0c + w], AF.Square),
                waits=[("act", silv), ("pe", state["sq_pe"][sl2])], inc=("act", 1))
            vsq = S.val("act")
            S.emit("tensor", lambda eng, w=w, sl2=sl2: eng.matmul(pb[2][0:2, 0:w], cst["headmask"][:, :],
                                                                  sqb[:, sl2 * 512:sl2 * 512 + w],
                                                                  start=True, stop=True),
                   waits=[("act", vsq), ("dve", lastn)], inc=("pe", 1))
            state["sq_pe"][sl2] = S.val("pe")
            S.emit("vector", lambda eng, g0c=g0c, w=w: eng.tensor_copy(nrm[0:2, g0c:g0c + w], pb[2][0:2, 0:w]),
                   waits=[("pe", S.val("pe"))], inc=("dve", 1))
            lastn = S.val("dve")
        state["oc_done_act"] = S.val("act")
        S.emit("scalar", lambda eng: eng.activation(nrm[0:2, 0:ntok], nrm[0:2, 0:ntok], AF.Sqrt,
                                                    bias=cvals[0:2, 1:2]),
               waits=[("dve", lastn), ("dve", cinit_v)], inc=("act", 1))
        S.emit("vector", lambda eng: eng.reciprocal(nrm[0:2, 0:ntok], nrm[0:2, 0:ntok]),
               waits=[("act", S.val("act"))], inc=("dve", 1))
        vrs = S.val("dve")
        state["oc_done_act"] = S.val("act")
        maskT = cst["maskT_q"] if kind == "q" else cst["maskT_k"]
        dst_d = Dq if kind == "q" else Dk
        lastq = 0
        for gi2, g0c in enumerate(range(0, ntok, 512)):
            w = min(512, ntok - g0c)
            sl2 = gi2 % 2
            S.emit("tensor", lambda eng, g0c=g0c, w=w, maskT=maskT: eng.matmul(
                pb[3][:, 0:w], maskT[0:2, :], nrm[0:2, g0c:g0c + w], start=True, stop=True),
                waits=[("dve", vrs), ("dve", lastq)], inc=("pe", 1))
            wq2 = [("pe", S.val("pe"))]
            if state["qk_st"][sl2] is not None:
                wq2.append(state["qk_st"][sl2])
            S.emit("vector", lambda eng, g0c=g0c, w=w, sl2=sl2: eng.tensor_tensor(
                qst[:, sl2 * 512:sl2 * 512 + w], cvb[:, g0c:g0c + w], pb[3][:, 0:w], ALU.mult),
                waits=wq2, inc=("dve", 1))
            lastq = S.val("dve")
            state["qk_st"][sl2] = dma("sync", dst_d.ap()[oc * 128:(oc + 1) * 128, g0c:g0c + w],
                                      qst[:, sl2 * 512:sl2 * 512 + w], waits=[("dve", lastq)],
                                      stream=f"stqk{sl2}")
        state["oc_done_dve"] = lastq

    for oc in range(8):
        proj_oc("q", oc)
    if maxphase >= 3:
        for oc in range(8):
            proj_oc("k", oc)
    if maxphase >= 4:
        for oc in range(16):
            proj_oc("v", oc)
    if maxphase >= 5:
        for oc in range(16):
            proj_oc("g", oc)
    def qk_taps():
        for nm, dd in (("q_fm", Dq), ("k_fm", Dk)):
            if nm in taps:
                dma("sync", taps[nm][:, 0:8 * ntok].rearrange("p (g t) -> p g t", g=8),
                    dd.ap().rearrange("(g p) t -> p g t", p=128), stream="sttap")

    if maxphase < 6:
        A.free(gB1)
        A.free(gB0)
        S.barrier()
        qk_taps()
        if "vsp" in taps and maxphase >= 4:
            dma("sync", taps["vsp"], Dvsp.ap(), stream="sttap")
        if "gsp" in taps and maxphase >= 5:
            dma("sync", taps["gsp"], Dgsp.ap(), stream="sttap")
        _finish(nc, S, O, ntok)
        return nc, taps

    A.free(gB1)
    gBab = "phA2ab"
    abh = A.sb(gBab, "abh", [16, ntok], F32)
    abh2 = A.sb(gBab, "abh2", [16, ntok], F32)
    lnb = A.sb(gBab, "lnb", [16, ntok], F32)
    sc16 = A.sb(gBab, "sc16", [16, ntok], F32)
    gb1_pend = [e for e in state["qk_st"] + state["g_slot_st"] if e] + state["vst"][-4:]

    src = I["abT"].rearrange("(ic p) o -> p ic o", p=128)
    wsl_ab = state["wslot"] % 2
    ldab = dma("gpsimd", wbf[:, wsl_ab * 2048: wsl_ab * 2048 + 512].rearrange("p (ic o) -> p ic o", ic=16),
               src, waits=[("pe", state["wslot_pe"][wsl_ab])], sem=f"wfg{wsl_ab}")
    lastab = state["oc_done_dve"]
    for (wcol0, dst) in ((0, abh), (16, abh2)):
        for g0c in range(0, ntok, 512):
            w = min(512, ntok - g0c)
            def mm(eng, g0c=g0c, w=w, wcol0=wcol0, wsl_ab=wsl_ab):
                last = None
                for ic in range(16):
                    last = eng.matmul(pb[4][0:16, 0:w],
                                      wbf[:, wsl_ab * 2048 + ic * 32 + wcol0: wsl_ab * 2048 + ic * 32 + wcol0 + 16],
                                      h_fm[:, ic * NTP + 3 + g0c: ic * NTP + 3 + g0c + w],
                                      start=(ic == 0), stop=(ic == 15))
                return last
            S.emit("tensor", mm, waits=[ldab, ("dve", lastab)], inc=("pe", 1))
            S.emit("vector", lambda eng, g0c=g0c, w=w, dst=dst: eng.tensor_copy(dst[0:16, g0c:g0c + w],
                                                                                pb[4][0:16, 0:w]),
                   waits=[("pe", S.val("pe"))] + gb1_pend, inc=("dve", 1))
            gb1_pend = []
            lastab = S.val("dve")

    S.emit("vector", lambda eng: eng.tensor_add(smallv[0:16, 0:1], cst["b_bias"][0:16, 0:1],
                                                cst["beta_bias"][0:16, 0:1]),
           waits=[ld_consts, ("dve", lastab)], inc=("dve", 1))
    S.emit("scalar", lambda eng: eng.activation(abh2[0:16, 0:ntok], abh2[0:16, 0:ntok], AF.Sigmoid,
                                                bias=smallv[0:16, 0:1]),
           waits=[("dve", S.val("dve"))], inc=("act", 1))
    S.emit("scalar", lambda eng: eng.activation(lnb[0:16, 0:ntok], abh2[0:16, 0:ntok], AF.Ln, scale=2.0),
           inc=("act", 1))
    sig_act = S.val("act")
    S.emit("scalar", lambda eng: eng.activation(smallv[0:16, 1:2], cst["A_log"][0:16, 0:1], AF.Exp),
           waits=[ld_consts], inc=("act", 1))
    S.emit("vector", lambda eng: eng.tensor_scalar(smallv[0:16, 1:2], smallv[0:16, 1:2], -1.0, None, ALU.mult),
           waits=[("act", S.val("act"))], inc=("dve", 1))
    negA_v = S.val("dve")

    def gpre(eng):
        eng.tensor_scalar(abh[0:16, 0:ntok], abh[0:16, 0:ntok], cst["dt_bias"][0:16, 0:1], None, ALU.add)
        eng.tensor_scalar(sc16[0:16, 0:ntok], abh[0:16, 0:ntok], -1.0, None, ALU.mult)
        return eng.tensor_tensor(sc16[0:16, 0:ntok], sc16[0:16, 0:ntok], abh[0:16, 0:ntok], ALU.max)
    S.emit("vector", gpre, waits=[("dve", negA_v), ("act", sig_act)], inc=("dve", 1))
    S.emit("scalar", lambda eng: eng.activation(sc16[0:16, 0:ntok], sc16[0:16, 0:ntok], AF.Exp, scale=-1.0),
           waits=[("dve", S.val("dve"))], inc=("act", 1))
    S.emit("scalar", lambda eng: eng.activation(sc16[0:16, 0:ntok], sc16[0:16, 0:ntok], AF.Ln, bias=1.0),
           inc=("act", 1))
    lnact = S.val("act")

    def gfin(eng):
        eng.tensor_scalar(abh[0:16, 0:ntok], abh[0:16, 0:ntok], 0.0, None, ALU.max)
        eng.tensor_add(abh[0:16, 0:ntok], abh[0:16, 0:ntok], sc16[0:16, 0:ntok])
        return eng.tensor_scalar(abh[0:16, 0:ntok], abh[0:16, 0:ntok], smallv[0:16, 1:2], None, ALU.mult)
    S.emit("vector", gfin, waits=[("act", lnact)], inc=("dve", 1))

    def scans(eng):
        last = None
        for cch in range(nch):
            sl = slice(cch * L, (cch + 1) * L)
            last = eng.tensor_tensor_scan(hm_cum[0:16, sl], cst["onesrow"][0:16, 0:L],
                                          abh[0:16, sl], 0.0, ALU.mult, ALU.add)
        return last
    S.emit("vector", scans, inc=("dve", 1))
    S.emit("vector", lambda eng: eng.tensor_add(hm_cumb[0:16, 0:ntok], hm_cum[0:16, 0:ntok], lnb[0:16, 0:ntok]),
           inc=("dve", 1))
    S.emit("scalar", lambda eng: eng.activation(hm_expc[0:16, 0:ntok], hm_cum[0:16, 0:ntok], AF.Exp),
           waits=[("dve", S.val("dve"))], inc=("act", 1))
    S.emit("vector", lambda eng: eng.tensor_scalar(abh2[0:16, 0:ntok], abh2[0:16, 0:ntok], 2.0, None, ALU.mult),
           waits=[("act", S.val("act"))], inc=("dve", 1))
    S.emit("vector", lambda eng: eng.tensor_mul(lnb[0:16, 0:ntok], abh2[0:16, 0:ntok], hm_expc[0:16, 0:ntok]),
           inc=("dve", 1))
    bg_v = S.val("dve")

    prev_tp = 0
    for cch in range(nch):
        sl = slice(cch * L, (cch + 1) * L)
        S.emit("vector", lambda eng, cch=cch, sl=sl: eng.tensor_scalar(
            sc16[0:16, 0:L], hm_cum[0:16, sl], -1.0,
            hm_cum[0:16, cch * L + L - 1:cch * L + L], ALU.mult, ALU.add),
            waits=[("dve", bg_v), ("pe", prev_tp)], inc=("dve", 1))
        S.emit("scalar", lambda eng: eng.activation(sc16[0:16, 0:L], sc16[0:16, 0:L], AF.Exp),
               waits=[("dve", S.val("dve"))], inc=("act", 1))
        kbs_act = S.val("act")

        def tpose(eng, cch=cch, sl=sl):
            eng.transpose(pb[5][0:128, 0:16], hm_cum[0:16, sl], cst["ident"][0:16, 0:16])
            eng.transpose(pb[5][0:128, 16:32], abh2[0:16, sl], cst["ident"][0:16, 0:16])
            eng.transpose(pb[5][0:128, 32:48], lnb[0:16, sl], cst["ident"][0:16, 0:16])
            eng.transpose(pb[5][0:128, 48:64], sc16[0:16, 0:L], cst["ident"][0:16, 0:16])
            eng.transpose(pb[5][0:128, 80:96], hm_cumb[0:16, sl], cst["ident"][0:16, 0:16])
            return eng.transpose(pb[5][0:1, 64:80], hm_expc[0:16, cch * L + L - 1:cch * L + L],
                                 cst["ident"][0:16, 0:16])
        S.emit("tensor", tpose, waits=[("act", kbs_act)], inc=("pe", 1))
        prev_tp = S.val("pe")

        def tevac(eng, cch=cch):
            eng.tensor_copy(st_cum[:, cch * 16:(cch + 1) * 16], pb[5][0:128, 0:16])
            eng.tensor_copy(st_beta[:, cch * 16:(cch + 1) * 16], pb[5][0:128, 16:32])
            eng.tensor_copy(st_bg[:, cch * 16:(cch + 1) * 16], pb[5][0:128, 32:48])
            eng.tensor_copy(st_kbs[:, cch * 16:(cch + 1) * 16], pb[5][0:128, 48:64])
            eng.tensor_copy(st_cumb[:, cch * 16:(cch + 1) * 16], pb[5][0:128, 80:96])
            return eng.tensor_copy(gl_row[0:1, cch * 16:(cch + 1) * 16], pb[5][0:1, 64:80])
        S.emit("vector", tevac, waits=[("pe", prev_tp)], inc=("dve", 1))

    A.free(gBab)
    A.free(gB0)
    A.free("hfm")
    barA2 = S.barrier()
    qk_taps()
    for nm, src_, cols in (("cum", st_cum, nch * 16), ("beta", st_beta, nch * 16),
                           ("bg", st_bg, nch * 16), ("kbs", st_kbs, nch * 16)):
        if nm in taps:
            dma("sync", taps[nm][:, 0:cols], src_[:, 0:cols], stream="sttap")
    if "glrow" in taps:
        dma("sync", taps["glrow"], gl_row[0:1, :], stream="sttap")
    if "vsp" in taps:
        dma("sync", taps["vsp"], Dvsp.ap(), stream="sttap")
    if "gsp" in taps:
        dma("sync", taps["gsp"], Dgsp.ap(), stream="sttap")

    if maxphase < 7:
        _finish(nc, S, O, ntok)
        return nc, taps

    S.phase("p1")
    Dgl = nc.dram_tensor("Dgl", [1, nch * 16], F32)
    glst = dma("sync", Dgl.ap(), gl_row[0:1, :], stream="stgl")
    dma("sync", glb[:, :], Dgl.ap().partition_broadcast(128), waits=[glst], stream="mbc")
    glb_ld = ("q_mbc", S.val("q_mbc"))

    Dyc = nc.dram_tensor("Dyc", [nch * 16 * 64, 128], BF16)
    g1 = "p1"
    vtile = A.sb(g1, "vtile", [128, 2 * 16 * 128], BF16)
    em4 = A.sb(g1, "em4", [128, 2 * 512], F32)
    e4 = A.sb(g1, "e4", [128, 2 * 512], F32)
    abar4d = A.sb(g1, "abar4", [128, 2 * 512], BF16)
    a4d = A.sb(g1, "a4", [128, 2 * 512], BF16)
    bbar4d = A.sb(g1, "bbar4", [128, 2 * 512], BF16)
    qg4d = A.sb(g1, "qg4", [128, 2 * 512], F32)
    abuf = A.sb(g1, "abuf", [128, 2 * 6 * 512], BF16)
    apow = A.sb(g1, "apow", [128, 2 * 2 * 512], BF16)
    xab = A.sb(g1, "xab", [128, 2 * 2 * 768], BF16)
    kbt4 = A.sb(g1, "kbt4", [128, 2 * 256], BF16)
    pc4 = A.sb(g1, "pc4", [64, 2 * 256], BF16)
    Ycs4 = A.sb(g1, "Ycs4", [64, 2 * 512], BF16)
    negc4 = A.sb(g1, "negc4", [64, 2 * 256], BF16)
    oust4 = A.sb(g1, "oust4", [128, 2 * 512], BF16)
    ztst4 = A.sb(g1, "ztst4", [64, 2 * 512], BF16)
    pqacc = A.sb(g1, "pqacc", [64, 16 * 192], BF16)
    pqst = A.sb(g1, "pqst", [64, 2 * 192], BF16)
    qtl = A.sb(g1, "qtl", [128, 2 * 1024], BF16)
    ktl = A.sb(g1, "ktl", [128, 2 * 1024], BF16)

    pq_ev = {}
    def pqinit(eng):
        eng.memset(pqacc[0:64, :], 0.0)
        last = None
        for hh in range(16):
            last = eng.tensor_copy(pqacc[0:64, hh * 192: hh * 192 + 64], cst["identbf"][0:64, 0:64])
        return last
    S.emit("vector", pqinit, waits=[ld_consts], inc=("dve", 1))
    for hh in range(16):
        pq_ev[hh] = ("dve", S.val("dve"))

    bg_guard = {i: {} for i in range(6)}
    bg_guard["bfA"] = {}
    bg_guard["bfB"] = {}
    evac_flip = [0]

    def evac_engine():
        evac_flip[0] ^= 1
        return "vector" if evac_flip[0] else "scalar"

    def eng_sem(e):
        return "dve" if e == "vector" else "act"

    def pe_mm(fn, extra_waits, bank_keys):
        wl = list(extra_waits)
        for b in bank_keys:
            for snm, v in bg_guard[b].items():
                if v > 0:
                    wl.append((snm, v))
        S.emit("tensor", fn, waits=wl, inc=("pe", 1))
        return S.val("pe")

    def evac(fn, pev, bank_keys, extra_waits=(), scalar_ok=True):
        e = evac_engine() if scalar_ok else "vector"
        S.emit(e, fn, waits=[("pe", pev)] + list(extra_waits), inc=(eng_sem(e), 1))
        v = (eng_sem(e), S.val(eng_sem(e)))
        for b in bank_keys:
            bg_guard[b][v[0]] = v[1]
        return v

    vld_prev = [None, None]
    qk_cons = [None, None]
    styc4 = [None, None]
    stou4 = [None, None]
    stzt4 = [None, None]
    stnt4 = [None, None]
    nt4_dve = [0, 0]
    yc4_pe = [0, 0]

    def kf_ap(hh, cc):
        oc, half = hh // 2, (hh % 2) * 64
        base = (cc % 2) * 1024 + oc * L
        return ktl[half:half + 64, base:base + L]

    def qf_ap(hh, cc):
        oc, half = hh // 2, (hh % 2) * 64
        base = (cc % 2) * 1024 + oc * L
        return qtl[half:half + 64, base:base + L]

    def grp_stages(cc, grp, st, ldk, ldv):
        import os as _os
        HLP = HL_PARTS if (HL_PARTS2 is None or grp % 2 == 0) else HL_PARTS2
        par = cc % 2
        heads = list(range(grp * 4, grp * 4 + 4))
        ch0 = cc * 16 + heads[0]
        b0, b1, b2 = (0, 1, 2) if st == 0 else (3, 4, 5)
        bfk = "bfA" if st == 0 else "bfB"
        pfb = st * 1024
        sb = st * 512
        em = em4[:, sb:sb + 512]
        ex = e4[:, sb:sb + 512]
        abar4 = abar4d[:, sb:sb + 512]
        a4 = a4d[:, sb:sb + 512]
        bbar4 = bbar4d[:, sb:sb + 512]
        qg4 = qg4d[:, sb:sb + 512]
        abufs = abuf[:, st * 3072: (st + 1) * 3072]
        apows = apow[:, st * 1024: (st + 1) * 1024]
        xs = [xab[:, (st * 2 + j) * 768: (st * 2 + j + 1) * 768] for j in range(2)]
        kb4 = kbt4[:, st * 256:(st + 1) * 256]
        pcs = pc4[0:64, st * 256:(st + 1) * 256]

        fence = [("dve", S.val("dve")), ("act", S.val("act"))] if _os.environ.get("P1_FENCE") else []
        for i, hh in enumerate(heads):
            pv_g = pe_mm(lambda eng, i=i, hh=hh: eng.matmul(pb[b0][:, i * 128:(i + 1) * 128],
                                                            kf_ap(hh, cc), kf_ap(hh, cc),
                                                            start=True, stop=True),
                         [ldk] + fence if i == 0 else [ldk], [b0])
        for i, hh in enumerate(heads):
            pv_gq = pe_mm(lambda eng, i=i, hh=hh: eng.matmul(pb[b1][:, i * 128:(i + 1) * 128],
                                                             kf_ap(hh, cc), qf_ap(hh, cc),
                                                             start=True, stop=True), [ldk], [b1])
        for i, hh in enumerate(heads):
            pv_b = pe_mm(lambda eng, i=i, hh=hh: eng.matmul(pb[b2][:, i * 128:(i + 1) * 128],
                                                            cst["ohsel"][0:16, hh * 128:(hh + 1) * 128],
                                                            hm_cumb[0:16, cc * L:(cc + 1) * L],
                                                            start=True, stop=True), [], [b2])
        yield
        if "bld1" not in HLP:
            return

        def bld1(eng):
            last = None
            for i, hh in enumerate(heads):
                last = eng.scalar_tensor_tensor(em[:, i * 128:(i + 1) * 128],
                                                pb[b2][:, i * 128:(i + 1) * 128],
                                                st_cum[:, cc * 16 + hh: cc * 16 + hh + 1],
                                                cst["mneg_s4"][:, i * 128:(i + 1) * 128],
                                                ALU.subtract, ALU.add)
            return last
        em_v = evac(bld1, pv_b, [b2], scalar_ok=False)
        S.emit("scalar", lambda eng: eng.activation(ex[:, :], em[:, :], AF.Exp),
               waits=[em_v], inc=("act", 1))
        S.emit("vector", lambda eng: eng.scalar_tensor_tensor(abar4[:, :], pb[b0][:, :], 1.0,
                                                              ex[:, :], ALU.mult, ALU.mult),
               waits=[("act", S.val("act")), ("pe", pv_g)], inc=("dve", 1))
        abar_bld = ("dve", S.val("dve"))
        yield
        if "bld2" not in HLP:
            return

        for i, hh in enumerate(heads):
            pv_b2 = pe_mm(lambda eng, i=i, hh=hh: eng.matmul(pb[b2][:, i * 128:(i + 1) * 128],
                                                             cst["ohsel"][0:16, hh * 128:(hh + 1) * 128],
                                                             hm_cum[0:16, cc * L:(cc + 1) * L],
                                                             start=True, stop=True),
                          [em_v] if i == 0 else [], [b2] if i == 0 else [])

        def bld2(eng):
            for i, hh in enumerate(heads):
                eng.tensor_scalar(em[:, i * 128:(i + 1) * 128], pb[b2][:, i * 128:(i + 1) * 128],
                                  -1.0, st_cumb[:, cc * 16 + hh: cc * 16 + hh + 1],
                                  ALU.mult, ALU.add)
            return eng.tensor_add(em[:, :], em[:, :], cst["mneg_sT4"][:, :])
        em_v2 = evac(bld2, pv_b2, [], extra_waits=[abar_bld], scalar_ok=False)
        S.emit("scalar", lambda eng: eng.activation(ex[:, :], em[:, :], AF.Exp),
               waits=[em_v2], inc=("act", 1))
        S.emit("vector", lambda eng: eng.scalar_tensor_tensor(a4[:, :], pb[b0][:, :], 1.0, ex[:, :],
                                                              ALU.mult, ALU.mult),
               waits=[("act", S.val("act"))], inc=("dve", 1))
        bg_guard[b0]["dve"] = S.val("dve")

        def bld3(eng):
            last = None
            for i, hh in enumerate(heads):
                last = eng.scalar_tensor_tensor(em[:, i * 128:(i + 1) * 128],
                                                pb[b2][:, i * 128:(i + 1) * 128],
                                                st_cum[:, cc * 16 + hh: cc * 16 + hh + 1],
                                                cst["mneg_i4"][:, i * 128:(i + 1) * 128],
                                                ALU.subtract, ALU.add)
            return last
        S.emit("vector", bld3, waits=[], inc=("dve", 1))
        em_v3 = ("dve", S.val("dve"))
        bg_guard[b2]["dve"] = S.val("dve")
        S.emit("scalar", lambda eng: eng.activation(ex[:, :], em[:, :], AF.Exp),
               waits=[em_v3], inc=("act", 1))
        S.emit("vector", lambda eng: eng.scalar_tensor_tensor(bbar4[:, :], pb[b1][:, :], 1.0, ex[:, :],
                                                              ALU.mult, ALU.mult),
               waits=[("act", S.val("act")), ("pe", pv_gq)], inc=("dve", 1))
        bg_guard[b1]["dve"] = S.val("dve")
        yield
        if "bld4" not in HLP:
            return

        for i, hh in enumerate(heads):
            pv_b3 = pe_mm(lambda eng, i=i, hh=hh: eng.matmul(pb[b2][:, i * 128:(i + 1) * 128],
                                                             cst["ohsel"][0:16, hh * 128:(hh + 1) * 128],
                                                             hm_expc[0:16, cc * L:(cc + 1) * L],
                                                             start=True, stop=True),
                          [], [b2] if i == 0 else [])

        def bld4(eng):
            last = None
            for i, hh in enumerate(heads):
                half = (hh % 2) * 64
                last = eng.tensor_tensor(qg4[half:half + 64, i * 128:(i + 1) * 128],
                                         qf_ap(hh, cc), pb[b2][half:half + 64, i * 128:(i + 1) * 128],
                                         ALU.mult)
            return last
        qg_v = evac(bld4, pv_b3, [b2], scalar_ok=False)
        if "ktr" not in HLP:
            return

        for j in range(2):
            hh = heads[j * 2]
            oc = hh // 2
            base = (cc % 2) * 1024 + oc * L
            pv_kt = pe_mm(lambda eng, j=j, base=base: eng.transpose(
                pbf[:, pfb + j * 128: pfb + (j + 1) * 128], ktl[:, base:base + L],
                cst["identbf"][:, :]),
                [], [bfk] if j == 0 else [])
        yield
        if "x0" not in HLP:
            return

        def x0w(eng):
            last = None
            for i, hh in enumerate(heads):
                ch = cc * 16 + hh
                pcol = pfb + (i // 2) * 128 + (hh % 2) * 64
                eng.tensor_scalar(kb4[:, i * 64:(i + 1) * 64], pbf[:, pcol:pcol + 64],
                                  st_kbs[:, ch:ch + 1], None, ALU.mult)
                eng.tensor_scalar(xs[0][:, i * 192:i * 192 + 128],
                                  vtile[:, par * 2048 + hh * 128: par * 2048 + (hh + 1) * 128],
                                  st_beta[:, ch:ch + 1], None, ALU.mult)
                last = eng.tensor_scalar(xs[0][:, i * 192 + 128:(i + 1) * 192],
                                         pbf[:, pcol:pcol + 64],
                                         st_bg[:, ch:ch + 1], None, ALU.mult)
            return last
        x_ev = evac(x0w, pv_kt, [bfk], extra_waits=[ldv], scalar_ok=False)
        yield
        if "powers" not in HLP:
            return

        acur_sl = [a4[:, i * 128:(i + 1) * 128] for i in range(4)]
        abcur_sl = [abar4[:, i * 128:(i + 1) * 128] for i in range(4)]
        acur_ev = ("dve", 0)
        abar_ev = ("dve", 0)
        for lev in range(6):
            for i in range(4):
                pvA = pe_mm(lambda eng, i=i, a=acur_sl[i], ab=abcur_sl[i]: eng.matmul(
                    pb[b0][:, i * 128:(i + 1) * 128], a, ab, start=True, stop=True),
                    [acur_ev, abar_ev] if i == 0 else [], [b0] if i == 0 else [])
            if lev < 5:
                for i in range(4):
                    pvB = pe_mm(lambda eng, i=i, a=acur_sl[i], ab=abcur_sl[i]: eng.matmul(
                        pb[b1][:, i * 128:(i + 1) * 128], ab, a, start=True, stop=True),
                        [acur_ev, abar_ev] if i == 0 else [], [b1] if i == 0 else [])
            dstA = abufs[:, lev * 512:(lev + 1) * 512]
            abar_ev = evac(lambda eng, dstA=dstA: eng.xcopy(dstA, pb[b0][:, :]), pvA, [b0])
            abcur_sl = [dstA[:, i * 128:(i + 1) * 128] for i in range(4)]
            if lev < 5:
                dstB = apows[:, (lev % 2) * 512:(lev % 2 + 1) * 512]
                acur_ev = evac(lambda eng, dstB=dstB: eng.xcopy(dstB, pb[b1][:, :]), pvB, [b1])
                acur_sl = [dstB[:, i * 128:(i + 1) * 128] for i in range(4)]
            yield

        if "applies" not in HLP:
            return
        xcur, xnext = xs[0], xs[1]
        xev = x_ev
        for j, lev in enumerate([6, 5, 4, 3, 2, 1, 0]):
            if lev == 0:
                lhs = [abar4[:, i * 128:(i + 1) * 128] for i in range(4)]
            else:
                lhs = [abufs[:, (lev - 1) * 512 + i * 128:(lev - 1) * 512 + (i + 1) * 128]
                       for i in range(4)]
            sgn = -1.0 if lev == 0 else 1.0

            for k in range(4):
                bk = b0 if k < 2 else b1
                col = (k % 2) * 192
                pv_ = pe_mm(lambda eng, lhs=lhs[k], xc=xcur, k=k, bk=bk, col=col: eng.matmul(
                    pb[bk][:, col:col + 192], lhs, xc[:, k * 192:(k + 1) * 192],
                    start=True, stop=True),
                    [xev, abar_ev] if k in (0, 2) else [], [bk] if k in (0, 2) else [])
                if k == 1:
                    pvA = pv_
            pvB = pv_

            def appev(eng, xn=xnext, xc=xcur, sgn=sgn):
                eng.scalar_tensor_tensor(xn[:, 0:384], pb[b0][:, 0:384], sgn, xc[:, 0:384],
                                         ALU.mult, ALU.add)
                return eng.scalar_tensor_tensor(xn[:, 384:768], pb[b1][:, 0:384], sgn, xc[:, 384:768],
                                                ALU.mult, ALU.add)
            S.emit("vector", appev, waits=[("pe", pvB)], inc=("dve", 1))
            xev = ("dve", S.val("dve"))
            bg_guard[b0]["dve"] = S.val("dve")
            bg_guard[b1]["dve"] = S.val("dve")
            xcur, xnext = xnext, xcur
            yield

        if "post" not in HLP:
            return
        for i in range(4):
            pvY = pe_mm(lambda eng, i=i, xc=xcur: eng.matmul(
                pb[b0][0:64, i * 128:(i + 1) * 128], kb4[:, i * 64:(i + 1) * 64],
                xc[:, i * 192:i * 192 + 128], start=True, stop=True),
                [xev] if i == 0 else [], [b0] if i == 0 else [])
        for i in range(4):
            pvN = pe_mm(lambda eng, i=i, xc=xcur: eng.matmul(
                pb[b1][0:64, i * 64:(i + 1) * 64], xc[:, i * 192 + 128:(i + 1) * 192],
                kb4[:, i * 64:(i + 1) * 64], start=True, stop=True),
                [xev] if i == 0 else [], [b1] if i == 0 else [])
        wyc = [] if styc4[st] is None else [styc4[st]]
        if yc4_pe[st]:
            wyc.append(("pe", yc4_pe[st]))
        yc_ev = evac(lambda eng: eng.xcopy(Ycs4[0:64, st * 512:(st + 1) * 512], pb[b0][0:64, 0:512]),
                     pvY, [b0], extra_waits=wyc)
        wnt = [] if stnt4[st] is None else [stnt4[st]]
        if nt4_dve[st]:
            wnt.append(("dve", nt4_dve[st]))
        nt_ev = evac(lambda eng: eng.xmul(negc4[0:64, st * 256:(st + 1) * 256], pb[b1][0:64, 0:256],
                                          -1.0), pvN, [b1], extra_waits=wnt)
        styc4[st] = dma("sync", Dyc.ap()[ch0 * 64:ch0 * 64 + 256, :].rearrange("(h p) t -> p h t", p=64),
                        Ycs4[0:64, st * 512:(st + 1) * 512].rearrange("p (h t) -> p h t", h=4),
                        waits=[yc_ev], stream=f"styc{st}")
        stnt4[st] = dma("sync", DnegT.ap()[ch0 * 64:ch0 * 64 + 256, :].rearrange("(h p) t -> p h t", p=64),
                        negc4[0:64, st * 256:(st + 1) * 256].rearrange("p (h t) -> p h t", h=4),
                        waits=[nt_ev], stream=f"stnt{st}")
        yield

        for i in range(4):
            pvO = pe_mm(lambda eng, i=i, xc=xcur: eng.matmul(
                pb[b0][:, i * 128:(i + 1) * 128], xc[:, i * 192:i * 192 + 128],
                bbar4[:, i * 128:(i + 1) * 128], start=True, stop=True),
                [xev] if i == 0 else [], [b0] if i == 0 else [])
        for i in range(4):
            pvZ = pe_mm(lambda eng, i=i, xc=xcur: eng.matmul(
                pb[b1][0:64, i * 128:(i + 1) * 128], xc[:, i * 192 + 128:(i + 1) * 192],
                bbar4[:, i * 128:(i + 1) * 128], start=True, stop=True),
                [xev] if i == 0 else [], [b1] if i == 0 else [])
        wou = [] if stou4[st] is None else [stou4[st]]
        ou_ev = evac(lambda eng: eng.xcopy(oust4[:, st * 512:(st + 1) * 512], pb[b0][:, 0:512]),
                     pvO, [b0], extra_waits=wou)

        def ztev(eng):
            last = None
            for i, hh in enumerate(heads):
                half = (hh % 2) * 64
                last = eng.scalar_tensor_tensor(ztst4[0:64, st * 512 + i * 128: st * 512 + (i + 1) * 128],
                                                pb[b1][0:64, i * 128:(i + 1) * 128], -1.0,
                                                qg4[half:half + 64, i * 128:(i + 1) * 128],
                                                ALU.mult, ALU.add)
            return last
        wzt = [] if stzt4[st] is None else [stzt4[st]]
        zt_ev = evac(ztev, pvZ, [b1], extra_waits=wzt, scalar_ok=False)
        stou4[st] = dma("sync", Dou.ap()[ch0 * 128:ch0 * 128 + 512, :].rearrange("(h p) t -> p h t", p=128),
                        oust4[:, st * 512:(st + 1) * 512].rearrange("p (h t) -> p h t", h=4),
                        waits=[ou_ev], stream=f"stou{st}")
        stzt4[st] = dma("sync", Dzt.ap()[ch0 * 64:ch0 * 64 + 256, :].rearrange("(h p) t -> p h t", p=64),
                        ztst4[0:64, st * 512:(st + 1) * 512].rearrange("p (h t) -> p h t", h=4),
                        waits=[zt_ev], stream=f"stzt{st}")
        yield

        if "compose" not in HLP:
            return
        def pcb(eng):
            last = None
            for i, hh in enumerate(heads):
                ch = cc * 16 + hh
                eng.tensor_scalar(pcs[:, i * 64:(i + 1) * 64], cst["identbf"][0:64, 0:64],
                                  glb[0:64, ch:ch + 1], None, ALU.mult)
                last = eng.tensor_tensor(pcs[:, i * 64:(i + 1) * 64], pcs[:, i * 64:(i + 1) * 64],
                                         negc4[0:64, st * 256 + i * 64: st * 256 + (i + 1) * 64], ALU.add)
            return last
        wpc = [glb_ld, nt_ev, ("pe", S.val("pe"))]
        S.emit("vector", pcb, waits=wpc, inc=("dve", 1))
        pcv = ("dve", S.val("dve"))
        nt4_dve[st] = S.val("dve")
        for half2 in range(2):
            i0_, i1_ = half2 * 2, half2 * 2 + 1

            for i in (i0_, i1_):
                def cmps1(eng, i=i, base=(i - i0_) * 192, hh=heads[i]):
                    eng.matmul(pb[b2][0:64, base:base + 192], pcs[:, i * 64:(i + 1) * 64],
                               pqacc[0:64, hh * 192:(hh + 1) * 192], start=True, stop=False)
                    return eng.matmul(pb[b2][0:64, base + 64:base + 192], cst["identbf"][0:64, 0:64],
                                      Ycs4[0:64, st * 512 + i * 128: st * 512 + (i + 1) * 128],
                                      start=False, stop=True)
                pvC = pe_mm(cmps1,
                            [pcv, yc_ev, pq_ev[heads[i0_]], pq_ev[heads[i1_]]] if i == i0_ else [],
                            [b2] if i == i0_ else [])

            def cev(eng, i0_=i0_):
                return eng.xcopy(pqacc[0:64, heads[i0_] * 192: heads[i0_] * 192 + 384],
                                 pb[b2][0:64, 0:384])
            ev = evac(cev, pvC, [b2])
            pq_ev[heads[i0_]] = ev
            pq_ev[heads[i0_] + 1] = ev
        yc4_pe[st] = S.val("pe")
        yield

    p1_nch, p1_ngrp = P1_LIMIT if P1_LIMIT else (nch, 4)
    for cc in range(p1_nch):
        par = cc % 2
        vsrc = Dvsp.ap()[cc * 2048:(cc + 1) * 2048, :].rearrange("(o t) v -> t o v", t=128)
        vwaits = [] if vld_prev[par] is None else list(vld_prev[par])
        ldv = dma("sync", vtile[:, par * 2048:(par + 1) * 2048].rearrange("t (o v) -> t o v", o=16),
                  vsrc, waits=vwaits, stream=f"vld{par}")
        qkw = [] if qk_cons[par] is None else list(qk_cons[par])
        dma("sync", qtl[:, par * 1024:(par + 1) * 1024].rearrange("p (g t) -> p g t", g=8),
            Dq.ap()[:, cc * L:(cc + 1) * L].rearrange("(g p) t -> p g t", p=128),
            waits=qkw, stream=f"qld{par}")
        ldk = dma("sync", ktl[:, par * 1024:(par + 1) * 1024].rearrange("p (g t) -> p g t", g=8),
            Dk.ap()[:, cc * L:(cc + 1) * L].rearrange("(g p) t -> p g t", p=128),
            stream=f"qld{par}")
        import os as _os
        _ilv = _os.environ.get("P1_ILV")
        if _ilv:
            for gp in range(0, p1_ngrp, 2):
                gens = [grp_stages(cc, gp, 0, ldk, ldv)]
                if gp + 1 < p1_ngrp:
                    gens.append(grp_stages(cc, gp + 1, 1, ldk, ldv))
                live = list(gens)
                while live:
                    nxt = []
                    for g in live:
                        try:
                            next(g)
                            nxt.append(g)
                        except StopIteration:
                            pass
                    live = nxt
        else:
            for gp in range(0, p1_ngrp):
                gens = [grp_stages(cc, gp, gp % 2, ldk, ldv)]
                for g in gens:
                    for _ in g:
                        pass
                    S.barrier()
        vld_prev[par] = [("pe", S.val("pe")), ("dve", S.val("dve")), ldv]
        qk_cons[par] = [("pe", S.val("pe")), ("dve", S.val("dve"))]

    pqslot_prev = [None, None]
    for hh in range(16):
        slot = hh % 2
        bfb = "bfA" if slot == 0 else "bfB"
        bfcol = 0 if slot == 0 else 1024
        pv = pe_mm(lambda eng, hh=hh, bfcol=bfcol: eng.transpose(
            pbf[:, bfcol:bfcol + 64][0:64, :], pqacc[0:64, hh * 192:hh * 192 + 64],
            cst["identbf"][0:64, 0:64]), [], [bfb])
        wst = [] if pqslot_prev[slot] is None else [pqslot_prev[slot]]
        def pqev(eng, hh=hh, slot=slot, bfcol=bfcol):
            eng.xcopy(pqst[0:64, slot * 192:slot * 192 + 64], pbf[0:64, bfcol:bfcol + 64])
            return eng.xcopy(pqst[0:64, slot * 192 + 64:(slot + 1) * 192],
                             pqacc[0:64, hh * 192 + 64:(hh + 1) * 192])
        ev = evac(pqev, pv, [bfb], extra_waits=wst)
        pqslot_prev[slot] = dma("sync", Dpq_in.ap()[hh * 64:(hh + 1) * 64, :],
                                pqst[0:64, slot * 192:(slot + 1) * 192], waits=[ev], stream=f"stpq{slot}")
    S.flush_dma()
    S.emit("gpsimd", lambda eng: eng.collective_compute(
        "AllGather", ALU.bypass, replica_groups=[list(range(NCORES))],
        ins=[Dpq_in.ap().opt()], outs=[Dpq_gath.ap().opt()]),
        waits=[pqslot_prev[0], pqslot_prev[1]], inc=("cc_pq", 1))

    barP1 = S.barrier()
    if "negT" in taps:
        dma("sync", taps["negT"].rearrange("p (a t) -> p a t", t=64),
            DnegT.ap().rearrange("(a p) t -> p a t", p=64), stream="sttap")
    if "Yc" in taps:
        dma("sync", taps["Yc"].rearrange("p (c t) -> p c t", t=128),
            Dyc.ap().rearrange("(c p) t -> p c t", p=64), stream="sttap")
    for cc_ in range(p1_nch):
        for h_ in range(p1_ngrp * 4):
            ch_ = cc_ * 16 + h_
            if "ouT" in taps:
                dma("sync", taps["ouT"][ch_ * 128:(ch_ + 1) * 128, :],
                    Dou.ap()[ch_ * 128:(ch_ + 1) * 128, :], stream="sttap")
            if "zT" in taps:
                dma("sync", taps["zT"][ch_ * 64:(ch_ + 1) * 64, :],
                    Dzt.ap()[ch_ * 64:(ch_ + 1) * 64, :], stream="sttap")
    if "pq" in taps:
        dma("sync", taps["pq"], Dpq_in.ap(), stream="sttap")

    A.free(g1)
    A.free("hm")
    A.free("constp1")
    if maxphase < 8:
        _finish(nc, S, O, ntok)
        return nc, taps

    S.phase("blend")
    S_f = A.sb("sfin", "S_f", [64, 16 * 128], F32)
    sbf = A.sb("sfin", "sbf", [64, 128], BF16)
    gb_ = "blend"
    pqs = A.sb(gb_, "pqs", [64, NCORES * 16 * 192], BF16)
    acc3 = A.sb(gb_, "acc3", [64, 3 * 16 * 192], BF16)
    tcomp = A.sb(gb_, "tcomp", [64, 128], BF16)
    ldpq = dma("sync", pqs[:, :].rearrange("p (r h c) -> p r h c", r=NCORES, h=16),
               Dpq_gath.ap().rearrange("(r h p) c -> p r h c", p=64, h=16),
               waits=[("cc_pq", 1)], stream="pq")
    S.emit("vector", lambda eng: eng.memset(acc3[0:64, :], 0.0), inc=("dve", 1))
    for j in range(3):
        for r in range(NCORES):
            S.emit("vector", lambda eng, j=j, r=r: eng.scalar_tensor_tensor(
                acc3[0:64, j * 3072:(j + 1) * 3072], pqs[0:64, r * 3072:(r + 1) * 3072],
                cst["selp"][0:64, j * 8 + r:j * 8 + r + 1], acc3[0:64, j * 3072:(j + 1) * 3072],
                ALU.mult, ALU.add),
                waits=[ldpq, ld_consts] if (j == 0 and r == 0) else [], inc=("dve", 1))
    blend_v = S.val("dve")

    def PQj(j, hh):
        base = j * 3072 + hh * 192
        return acc3[0:64, base:base + 64], acc3[0:64, base + 64:base + 192]

    for hh in range(16):
        P1T, Q1 = PQj(1, hh)
        P0T, Q0 = PQj(0, hh)
        _, Q2 = PQj(2, hh)

        def c1(eng, P1T=P1T, Q1=Q1, Q2=Q2):
            eng.matmul(pb[0][0:64, 0:128], P1T, Q2, start=True, stop=False)
            return eng.matmul(pb[0][0:64, 0:128], cst["identbf"][0:64, 0:64], Q1, start=False, stop=True)
        S.emit("tensor", c1, waits=[("dve", blend_v), ("dve", S.val("dve")), ("act", S.val("act"))],
               inc=("pe", 1))
        S.emit("vector", lambda eng: eng.tensor_copy(tcomp[0:64, 0:128], pb[0][0:64, 0:128]),
               waits=[("pe", S.val("pe"))], inc=("dve", 1))
        tv = S.val("dve")

        def c2(eng, P0T=P0T, Q0=Q0):
            eng.matmul(pb[0][0:64, 0:128], P0T, tcomp[0:64, 0:128], start=True, stop=False)
            return eng.matmul(pb[0][0:64, 0:128], cst["identbf"][0:64, 0:64], Q0, start=False, stop=True)
        S.emit("tensor", c2, waits=[("dve", tv)], inc=("pe", 1))
        S.emit("vector", lambda eng, hh=hh: eng.tensor_copy(S_f[0:64, hh * 128:(hh + 1) * 128],
                                                            pb[0][0:64, 0:128]),
               waits=[("pe", S.val("pe"))], inc=("dve", 1))
    A.free(gb_)
    if "sinit" in taps:
        dma("sync", taps["sinit"], S_f[0:64, :], waits=[("dve", S.val("dve"))], stream="sttap")
    barBL = S.barrier()

    S.phase("p2a")
    g2a_ = "p2a"
    ztt = A.sb(g2a_, "ztt", [64, 2 * 16 * 128], BF16)
    outt = A.sb(g2a_, "outt", [128, 2 * 16 * 128], BF16)
    ycc = A.sb(g2a_, "ycc", [64, 2 * 16 * 128], BF16)
    gout_t = A.sb(g2a_, "gout_t", [128, 2 * 16 * 128], BF16)
    ngc = A.sb(g2a_, "ngc", [64, 2 * 16 * 64], BF16)
    og = A.sb(g2a_, "og", [128, 2 * 16 * 128], BF16)
    o2s = A.sb(g2a_, "o2s", [128, 128], F32)
    rstdb = A.sb(g2a_, "rstdb", [128, 128], F32)
    rrow = A.sb(g2a_, "rrow", [1, 128], F32)
    identg = A.sb(g2a_, "identg", [64, 64], BF16)

    prev_ch_ld = [None, None]
    stog = [None, None]
    for cc in range(nch):
        par = cc % 2
        wts = [] if prev_ch_ld[par] is None else list(prev_ch_ld[par])
        dma("sync", ztt[0:64, par * 2048:(par + 1) * 2048].rearrange("p (h t) -> p h t", h=16),
            Dzt.ap()[cc * 1024:(cc + 1) * 1024, :].rearrange("(h p) t -> p h t", p=64),
            waits=wts, stream=f"p2z{par}")
        dma("sync", outt[:, par * 2048:(par + 1) * 2048].rearrange("p (h t) -> p h t", h=16),
            Dou.ap()[cc * 2048:(cc + 1) * 2048, :].rearrange("(h p) t -> p h t", p=128),
            stream=f"p2z{par}")
        dma("sync", ycc[0:64, par * 2048:(par + 1) * 2048].rearrange("p (h t) -> p h t", h=16),
            Dyc.ap()[cc * 1024:(cc + 1) * 1024, :].rearrange("(h p) t -> p h t", p=64),
            stream=f"p2z{par}")
        dma("sync", ngc[0:64, par * 1024:(par + 1) * 1024].rearrange("p (h t) -> p h t", h=16),
            DnegT.ap()[cc * 1024:(cc + 1) * 1024, :].rearrange("(h p) t -> p h t", p=64),
            stream=f"p2z{par}")
        chld = dma("sync", gout_t[:, par * 2048:(par + 1) * 2048].rearrange("p (h t) -> p h t", h=16),
                   Dgsp.ap()[:, cc * 128:(cc + 1) * 128].rearrange("(h p) t -> p h t", p=128),
                   stream=f"p2z{par}")
        for hh in range(16):
            ch = cc * 16 + hh
            zts = ztt[0:64, par * 2048 + hh * 128: par * 2048 + (hh + 1) * 128]
            ous = outt[:, par * 2048 + hh * 128: par * 2048 + (hh + 1) * 128]
            ycs2 = ycc[0:64, par * 2048 + hh * 128: par * 2048 + (hh + 1) * 128]
            gos = gout_t[:, par * 2048 + hh * 128: par * 2048 + (hh + 1) * 128]
            ngs = ngc[0:64, par * 1024 + hh * 64: par * 1024 + (hh + 1) * 64]

            def precast(eng, hh=hh, ch=ch):
                eng.tensor_copy(sbf[0:64, 0:128], S_f[0:64, hh * 128:(hh + 1) * 128])
                return eng.tensor_scalar(identg[0:64, 0:64], cst["identbf"][0:64, 0:64],
                                         glb[0:64, ch:ch + 1], None, ALU.mult)
            S.emit("vector", precast, waits=[("pe", S.val("pe")), glb_ld], inc=("dve", 1))
            pcv = S.val("dve")

            def outmm(eng, zts=zts, ous=ous):
                eng.matmul(pb[2][:, 0:128], sbf[0:64, 0:128], zts, start=True, stop=False)
                return eng.matmul(pb[2][:, 0:128], cst["identbf"][:, :], ous, start=False, stop=True)
            S.emit("tensor", outmm, waits=[("dve", pcv), chld,
                                           ("dve", bg_guard[2].get("dve", 0)),
                                           ("act", bg_guard[2].get("act", 0))], inc=("pe", 1))
            outpe = S.val("pe")

            def chainmm(eng, ngs=ngs, ycs2=ycs2):
                eng.matmul(pb[1][0:64, 0:128], ngs, sbf[0:64, 0:128], start=True, stop=False)
                eng.matmul(pb[1][0:64, 0:128], identg[0:64, 0:64], sbf[0:64, 0:128], start=False, stop=False)
                return eng.matmul(pb[1][0:64, 0:128], cst["identbf"][0:64, 0:64], ycs2, start=False, stop=True)
            S.emit("tensor", chainmm, waits=[chld, ("dve", bg_guard[1].get("dve", 0)),
                                             ("act", bg_guard[1].get("act", 0))], inc=("pe", 1))
            chpe = S.val("pe")
            S.emit("vector", lambda eng, hh=hh: eng.tensor_copy(S_f[0:64, hh * 128:(hh + 1) * 128],
                                                                pb[1][0:64, 0:128]),
                   waits=[("pe", chpe)], inc=("dve", 1))
            bg_guard[1]["dve"] = S.val("dve")

            S.emit("scalar", lambda eng: eng.activation(o2s[:, 0:128], pb[2][:, 0:128], AF.Square),
                   waits=[("pe", outpe)], inc=("act", 1))
            S.emit("tensor", lambda eng: eng.matmul(pb[4][0:1, 0:128], cst["ones128x1"][:, 0:1],
                                                    o2s[:, 0:128], start=True, stop=True),
                   waits=[("act", S.val("act")), ("dve", bg_guard[4].get("dve", 0)),
                          ("act", bg_guard[4].get("act", 0))], inc=("pe", 1))
            S.emit("scalar", lambda eng: eng.activation(rrow[0:1, 0:128], pb[4][0:1, 0:128], AF.Sqrt,
                                                        scale=1.0 / DV, bias=cvals[0:1, 2:3]),
                   waits=[("pe", S.val("pe"))], inc=("act", 1))
            bg_guard[4]["act"] = S.val("act")
            S.emit("vector", lambda eng: eng.reciprocal(rrow[0:1, 0:128], rrow[0:1, 0:128]),
                   waits=[("act", S.val("act"))], inc=("dve", 1))
            S.emit("tensor", lambda eng: eng.matmul(pb[5][:, 0:128], cst["onesrow"][0:1, 0:128],
                                                    rrow[0:1, 0:128], start=True, stop=True),
                   waits=[("dve", S.val("dve")), ("dve", bg_guard[5].get("dve", 0)),
                          ("act", bg_guard[5].get("act", 0))], inc=("pe", 1))
            S.emit("vector", lambda eng: eng.tensor_copy(rstdb[:, 0:128], pb[5][:, 0:128]),
                   waits=[("pe", S.val("pe"))], inc=("dve", 1))
            bg_guard[5]["dve"] = S.val("dve")
            S.emit("vector", lambda eng: eng.tensor_tensor(o2s[:, 0:128], pb[2][:, 0:128], rstdb[:, 0:128],
                                                           ALU.mult),
                   inc=("dve", 1))
            bg_guard[2]["dve"] = S.val("dve")
            wog = [chld]
            if hh == 0 and stog[par] is not None:
                wog.append(stog[par])
            S.emit("vector", lambda eng, hh=hh, gos=gos, par=par: eng.tensor_tensor(
                og[:, par * 2048 + hh * 128: par * 2048 + (hh + 1) * 128], o2s[:, 0:128], gos, ALU.mult),
                waits=wog, inc=("dve", 1))
        ogv = S.val("dve")
        prev_ch_ld[par] = [("dve", ogv), ("pe", S.val("pe"))]
        stog[par] = dma("sync", Dog.ap()[cc * 128:(cc + 1) * 128, :], og[:, par * 2048:(par + 1) * 2048],
                        waits=[("dve", ogv)], stream=f"stog{par}")
    A.free(g2a_)
    A.free("sfin")
    barP2a = S.barrier()

    S.phase("p2b")
    g2b_ = "p2b"
    owTb = A.sb(g2b_, "owTb", [128, 16 * C], BF16)
    mbc_ga = A.sb(g2b_, "mbc_ga", [128, C], BF16)
    mbc_cb = A.sb(g2b_, "mbc_cb", [128, C], BF16)
    mbc_sb = A.sb(g2b_, "mbc_sb", [128, C], BF16)
    g1bb = A.sb(g2b_, "g1bb", [128, C], F32)
    g2bb = A.sb(g2b_, "g2bb", [128, C], F32)
    og2 = A.sb(g2b_, "og2", [128, 2 * 2048], BF16)
    xtile = A.sb(g2b_, "xtile", [128, C], F32)
    x1t = A.sb(g2b_, "x1t", [128, C], F32)
    h2t = A.sb(g2b_, "h2t", [128, C], F32)
    qsc = A.sb(g2b_, "qsc", [128, C], F32)
    itok = A.sb(g2b_, "itok", [128, C], BF16)
    ibf = A.sb(g2b_, "ibf", [128, 4 * 128], BF16)
    st2 = A.sb(g2b_, "st2", [128, 8], F32)

    dma("gpsimd", mbc_ga[:, :], Dmodvec.ap()[2:3, :].partition_broadcast(128), waits=[modvec_st], sem="gmbc")
    dma("gpsimd", mbc_cb[:, :], Dmodvec.ap()[3:4, :].partition_broadcast(128), sem="gmbc")
    dma("gpsimd", mbc_sb[:, :], Dmodvec.ap()[4:5, :].partition_broadcast(128), sem="gmbc")
    mbc_gld = ("gmbc", S.val("gmbc"))
    dma("sync", g1bb[:, :], I["g1v"].partition_broadcast(128), stream="mbc")
    dma("sync", g2bb[:, :], I["g2v"].partition_broadcast(128), stream="mbc")
    g12_ld = ("q_mbc", S.val("q_mbc"))
    ldow = dma("gpsimd", owTb[:, :].rearrange("p (v o) -> p v o", v=16),
               I["owT"].rearrange("(v p) o -> p v o", p=128), sem="gow")

    ip_slot_ev = {}
    og2_pe = [0, 0]
    x1_st_prev = None
    prev_x1v = 0
    prev_act_tail = 0
    for cc in range(nch):
        par = cc % 2
        ldog = dma("sync", og2[:, par * 2048:(par + 1) * 2048], Dog.ap()[cc * 128:(cc + 1) * 128, :],
                   waits=[("pe", og2_pe[par])], stream=f"ogl{par}")
        ldx = dma("sync", xtile[:, :], I["x_sl"][3 + cc * 128:3 + (cc + 1) * 128, :],
                  waits=[("dve", prev_x1v)], stream="x2")
        for o2i in range(2):
            for half2 in range(2):
                oc = o2i * 2 + half2
                bank = 0 if half2 == 0 else 3

                def owmm(eng, oc=oc, bank=bank, par=par):
                    last = None
                    for hh in range(16):
                        last = eng.matmul(pb[bank][:, 0:512],
                                          og2[:, par * 2048 + hh * 128: par * 2048 + (hh + 1) * 128],
                                          owTb[:, hh * C + oc * 512: hh * C + oc * 512 + 512],
                                          start=(hh == 0), stop=(hh == 15))
                    return last
                S.emit("tensor", owmm, waits=[ldog, ldow,
                                              ("dve", bg_guard[bank].get("dve", 0)),
                                              ("act", bg_guard[bank].get("act", 0))], inc=("pe", 1))
                opev = S.val("pe")

                def x1ev(eng, oc=oc, bank=bank):
                    eng.tensor_tensor(qsc[:, oc * 512:(oc + 1) * 512], pb[bank][:, 0:512],
                                      mbc_ga[:, oc * 512:(oc + 1) * 512], ALU.mult)
                    return eng.tensor_add(x1t[:, oc * 512:(oc + 1) * 512], qsc[:, oc * 512:(oc + 1) * 512],
                                          xtile[:, oc * 512:(oc + 1) * 512])
                wx1e = [("pe", opev), ldx, mbc_gld, ("act", prev_act_tail)]
                if x1_st_prev is not None:
                    wx1e.append(x1_st_prev)
                S.emit("vector", x1ev, waits=wx1e, inc=("dve", 1))
                bg_guard[bank]["dve"] = S.val("dve")
        og2_pe[par] = S.val("pe")
        x1v = S.val("dve")
        wx1 = [("dve", x1v)] if x1_st_prev is None else [("dve", x1v), x1_st_prev]
        x1_st_prev = dma("sync", Dx1.ap()[cc * 128:(cc + 1) * 128, :], x1t[:, :], waits=wx1, stream="stx1")

        S.emit("vector", lambda eng: eng.reduce_sum(st2[:, 0:1], x1t[:, :], AX.X),
               waits=[("dve", x1v)], inc=("dve", 1))
        S.emit("scalar", lambda eng: eng.activation(qsc[:, :], x1t[:, :], AF.Square, accum_out=st2[:, 1:2]),
               waits=[("dve", x1v), ("act", prev_act_tail)], inc=("act", 1))

        def st2f(eng):
            eng.tensor_scalar(st2[:, 0:1], st2[:, 0:1], 1.0 / C, None, ALU.mult)
            eng.tensor_scalar(st2[:, 2:3], st2[:, 1:2], 1.0 / C, None, ALU.mult)
            eng.tensor_mul(st2[:, 3:4], st2[:, 0:1], st2[:, 0:1])
            return eng.tensor_sub(st2[:, 2:3], st2[:, 2:3], st2[:, 3:4])
        S.emit("vector", st2f, waits=[("act", S.val("act"))], inc=("dve", 1))
        S.emit("scalar", lambda eng: eng.activation(st2[:, 4:5], st2[:, 2:3], AF.Sqrt, bias=cvals[:, 0:1]),
               waits=[("dve", S.val("dve"))], inc=("act", 1))
        S.emit("vector", lambda eng: eng.reciprocal(st2[:, 5:6], st2[:, 4:5]),
               waits=[("act", S.val("act"))], inc=("dve", 1))
        S.emit("vector", lambda eng: eng.tensor_scalar(h2t[:, :], x1t[:, :], st2[:, 0:1], st2[:, 5:6],
                                                       ALU.subtract, ALU.mult),
               inc=("dve", 1))
        S.emit("vector", lambda eng: eng.tensor_mul(h2t[:, :], h2t[:, :], mbc_cb[:, :]),
               waits=[mbc_gld], inc=("dve", 1))
        S.emit("vector", lambda eng: eng.tensor_tensor(h2t[:, :], h2t[:, :], mbc_sb[:, :], ALU.add),
               inc=("dve", 1))
        h2v = S.val("dve")
        S.emit("scalar", lambda eng: eng.activation(qsc[:, :], h2t[:, :], AF.Square, accum_out=st2[:, 6:7]),
               waits=[("dve", h2v)], inc=("act", 1))
        S.emit("vector", lambda eng: eng.tensor_scalar(st2[:, 6:7], st2[:, 6:7], 1.0 / C, None, ALU.mult),
               waits=[("act", S.val("act"))], inc=("dve", 1))
        S.emit("scalar", lambda eng: eng.activation(st2[:, 7:8], st2[:, 6:7], AF.Sqrt, bias=cvals[:, 3:4]),
               waits=[("dve", S.val("dve"))], inc=("act", 1))
        S.emit("vector", lambda eng: eng.reciprocal(st2[:, 7:8], st2[:, 7:8]),
               waits=[("act", S.val("act"))], inc=("dve", 1))
        rstd2v = S.val("dve")
        prev_act_tail = S.val("act")

        for br, (gbt, sft, di, fcol) in enumerate(((g1bb, s1f, Di1, 0), (g2bb, s2f, Di2, 1))):
            S.emit("vector", lambda eng, gbt=gbt: eng.scalar_tensor_tensor(
                qsc[:, :], h2t[:, :], st2[:, 7:8], gbt[:, :], ALU.mult, ALU.mult),
                waits=[("dve", rstd2v), g12_ld, ("act", prev_act_tail)], inc=("dve", 1))
            S.emit("vector", lambda eng: eng.tensor_reduce(st2[:, 6:7], qsc[:, :], AX.X, ALU.max,
                                                           apply_absolute_value=True),
                   inc=("dve", 1))

            def qsc1(eng, sft=sft, fcol=fcol, cc=cc):
                eng.tensor_scalar(st2[:, 6:7], st2[:, 6:7], 1e-4, None, ALU.max)
                eng.tensor_scalar(sft[:, cc:cc + 1], st2[:, 6:7], fscb[:, fcol:fcol + 1], None, ALU.mult)
                return eng.reciprocal(st2[:, 6:7], st2[:, 6:7])
            S.emit("vector", qsc1, waits=[fscb_ld], inc=("dve", 1))

            def qround(eng):
                eng.tensor_scalar(qsc[:, :], qsc[:, :], st2[:, 6:7], None, ALU.mult)
                eng.tensor_scalar(qsc[:, :], qsc[:, :], 127.0, CMAGIC, ALU.mult, ALU.add)
                eng.tensor_scalar(qsc[:, :], qsc[:, :], CMAGIC - 128.0, CMAGIC + 127.0, ALU.max, ALU.min)
                return eng.tensor_scalar(itok[:, :], qsc[:, :], -CMAGIC, None, ALU.add)
            S.emit("vector", qround, waits=[("pe", S.val("pe"))], inc=("dve", 1))
            qv = S.val("dve")
            for ic in range(16):
                slot = ic % 4
                bank = slot % 2
                pcol = bank * 1024 + (slot // 2) * 128
                key = slot
                wt = [("dve", qv)]
                gkb = "bfA" if bank == 0 else "bfB"
                for snm, v in bg_guard[gkb].items():
                    if v > 0:
                        wt.append((snm, v))
                S.emit("tensor", lambda eng, ic=ic, pcol=pcol: eng.transpose(
                    pbf[:, pcol:pcol + 128], itok[:, ic * 128:(ic + 1) * 128], cst["identbf"][:, :]),
                    waits=wt, inc=("pe", 1))
                wtc = [("pe", S.val("pe"))]
                if key in ip_slot_ev:
                    wtc.append(ip_slot_ev[key])
                S.emit("vector", lambda eng, slot=slot, pcol=pcol: eng.tensor_copy(
                    ibf[:, slot * 128:(slot + 1) * 128], pbf[:, pcol:pcol + 128]),
                    waits=wtc, inc=("dve", 1))
                bg_guard[gkb]["dve"] = S.val("dve")
                ip_slot_ev[key] = dma("sync", di.ap()[(ic * nch + cc) * 128:(ic * nch + cc + 1) * 128, :],
                                      ibf[:, slot * 128:(slot + 1) * 128],
                                      waits=[("dve", S.val("dve"))], stream=f"sti{br}{slot}")
        prev_x1v = S.val("dve")
    A.free(g2b_)
    A.free("stats")
    barP2 = S.barrier()
    if "x1" in taps:
        dma("sync", taps["x1"], Dx1.ap(), stream="sttap")
    if "i1" in taps:
        dma("sync", taps["i1"], Di1.ap(), stream="sttap")

    if maxphase < 9:
        _finish(nc, S, O, ntok)
        return nc, taps

    S.phase("f1")
    gf1 = "f1"
    i1r = A.sb(gf1, "i1r", [128, 16 * nch * 128], BF16)
    i2r = A.sb(gf1, "i2r", [128, 16 * nch * 128], BF16)
    w12 = A.sb(gf1, "w12", [128, 32 * 512], BF16)
    hswt = A.sb(gf1, "hswt", [128, 512], F32)
    hbf = A.sb(gf1, "hbf", [128, 512], BF16)
    g3p = A.sb(gf1, "g3p", [128, 512], F32)
    y2t = A.sb(gf1, "y2t", [128, 512], F32)
    g3sl = A.sb(gf1, "g3sl", [128, 2 * 512], F32)

    ld1 = dma("sync", i1r[:, :].rearrange("p (a t) -> p a t", a=16 * nch),
              Di1.ap().rearrange("(a p) t -> p a t", p=128), stream="f1i1")
    ld2 = dma("sync", i2r[:, :].rearrange("p (a t) -> p a t", a=16 * nch),
              Di2.ap().rearrange("(a p) t -> p a t", p=128), stream="f1i2")
    S.emit("vector", lambda eng: (eng.memset(mx3[:, :], 0.0), eng.memset(ss3[:, :], 0.0))[-1],
           inc=("dve", 1))
    f1init = S.val("dve")

    hst_prev = None
    wcons_pe = 0
    g3_cons = [0, 0]
    prev_hbf_v = 0
    prev_ss3_v = 0
    for n in range(HID // 512):
        g3par = n % 2
        wg3 = [("dve", g3_cons[g3par])] if g3_cons[g3par] else []
        g3ld = dma("gpsimd", g3sl[:, g3par * 512:(g3par + 1) * 512],
                   I["g3v"][:, n * 512:(n + 1) * 512].partition_broadcast(128), waits=wg3,
                   sem=f"g3l{g3par}")
        ldws_by = [[], []]
        for wsel in range(2):
            base_row = 0 if wsel == 0 else 256
            for r in range(8):
                ldws_by[wsel].append(dma("sync",
                                         w12[:, (wsel * 16 + r * 2) * 512:(wsel * 16 + r * 2 + 2) * 512]
                                         .rearrange("p (a t) -> p a t", a=2),
                                         Dwq_gath.ap()[r * 768 + base_row: r * 768 + base_row + 256,
                                                       n * 512:(n + 1) * 512]
                                         .rearrange("(a p) t -> p a t", p=128),
                                         waits=[("cc_wq", 1), ("pe", wcons_pe)], stream=f"f1w{wsel}"))
        for tc in range(nch):
            pe12 = []
            for wsel, bank, ir, ldi in ((0, 0, i1r, ld1), (1, 1, i2r, ld2)):
                def f1mm(eng, tc=tc, wsel=wsel, bank=bank, ir=ir):
                    last = None
                    for ic in range(16):
                        wslot = wsel * 16 + ic
                        last = eng.matmul(pb[bank][:, 0:512],
                                          ir[:, (ic * nch + tc) * 128:(ic * nch + tc + 1) * 128],
                                          w12[:, wslot * 512:(wslot + 1) * 512],
                                          start=(ic == 0), stop=(ic == 15))
                    return last
                wt = [ldi, ldws_by[wsel][-1], ("dve", bg_guard[bank].get("dve", 0)),
                      ("act", bg_guard[bank].get("act", 0))]
                S.emit("tensor", f1mm, waits=wt, inc=("pe", 1))
                pe12.append(S.val("pe"))
            wcons_pe = S.val("pe")
            S.emit("scalar", lambda eng, tc=tc: eng.activation(hswt[:, 0:512], pb[0][:, 0:512], AF.Silu,
                                                               scale=s1f[:, tc:tc + 1]),
                   waits=[("pe", pe12[0]), ("dve", prev_hbf_v), ("dve", prev_ss3_v)], inc=("act", 1))
            bg_guard[0]["act"] = S.val("act")
            S.emit("vector", lambda eng, tc=tc: eng.tensor_scalar(y2t[:, 0:512], pb[1][:, 0:512],
                                                                  s2f[:, tc:tc + 1], None, ALU.mult),
                   waits=[("pe", pe12[1])], inc=("dve", 1))
            bg_guard[1]["dve"] = S.val("dve")
            hw = [("act", S.val("act"))]
            if hst_prev is not None:
                hw.append(hst_prev)
            S.emit("vector", lambda eng: eng.tensor_tensor(hbf[:, 0:512], hswt[:, 0:512], y2t[:, 0:512],
                                                           ALU.mult),
                   waits=hw, inc=("dve", 1))
            hv = S.val("dve")
            prev_hbf_v = hv
            S.emit("scalar", lambda eng: eng.activation(hswt[:, 0:512], hbf[:, 0:512], AF.Square,
                                                        accum_out=sc3[:, 0:1]),
                   waits=[("dve", hv), ("dve", prev_ss3_v)], inc=("act", 1))
            S.emit("vector", lambda eng, tc=tc: eng.tensor_add(ss3[:, tc:tc + 1], ss3[:, tc:tc + 1],
                                                               sc3[:, 0:1]),
                   waits=[("act", S.val("act")), ("dve", f1init)], inc=("dve", 1))
            prev_ss3_v = S.val("dve")
            S.emit("vector", lambda eng, g3par=g3par: eng.tensor_tensor(
                g3p[:, 0:512], hbf[:, 0:512], g3sl[:, g3par * 512:(g3par + 1) * 512], ALU.mult),
                waits=[g3ld], inc=("dve", 1))
            S.emit("vector", lambda eng: eng.tensor_reduce(sc3[:, 1:2], g3p[:, 0:512], AX.X, ALU.max,
                                                           apply_absolute_value=True),
                   inc=("dve", 1))
            S.emit("vector", lambda eng, tc=tc: eng.tensor_max(mx3[:, tc:tc + 1], mx3[:, tc:tc + 1],
                                                               sc3[:, 1:2]),
                   inc=("dve", 1))
            hst_prev = dma("sync", Dh.ap()[tc * 128:(tc + 1) * 128, n * 512:(n + 1) * 512], hbf[:, 0:512],
                           waits=[("dve", hv)], stream="sth")
        g3_cons[g3par] = S.val("dve")

    S.emit("vector", lambda eng: eng.tensor_scalar(ss3[:, :], ss3[:, :], 1.0 / HID, None, ALU.mult),
           waits=[("dve", S.val("dve")), hst_prev], inc=("dve", 1))
    S.emit("scalar", lambda eng: eng.activation(ss3[:, :], ss3[:, :], AF.Sqrt, bias=cvals[:, 3:4]),
           waits=[("dve", S.val("dve"))], inc=("act", 1))

    def rs3b(eng):
        eng.reciprocal(ss3[:, :], ss3[:, :])
        eng.tensor_mul(mx3[:, :], mx3[:, :], ss3[:, :])
        eng.tensor_scalar(mx3[:, :], mx3[:, :], 1e-4, None, ALU.max)
        eng.tensor_scalar(s3f[:, :], mx3[:, :], fscb[:, 2:3], None, ALU.mult)
        return eng.reciprocal(mx3[:, :], mx3[:, :])
    S.emit("vector", rs3b, waits=[("act", S.val("act")), fscb_ld], inc=("dve", 1))
    if "f2sc" in taps:
        def scdump(eng):
            for i, t in enumerate((s1f, s2f, s3f, mx3, ss3)):
                eng.tensor_copy(hswt[:, i * 16:i * 16 + t.shape[1]], t[:, :])
            return eng.tensor_copy(hswt[:, 80:82], sc3[:, 0:2])
        S.emit("vector", scdump, waits=[("dve", S.val("dve"))], inc=("dve", 1))
        dma("sync", taps["f2sc"][:, 0:96], hswt[:, 0:96], waits=[("dve", S.val("dve"))], stream="sttap")
    A.free(gf1)
    barF1 = S.barrier()
    if "hmid" in taps:
        dma("sync", taps["hmid"], Dh.ap(), stream="sttap")

    if maxphase < 10:
        _finish(nc, S, O, ntok)
        return nc, taps

    S.phase("f2")
    gf2 = "f2"
    QT = min(512, ntok)
    nblk = ntok // QT
    nch_q = QT // 128
    i3h = A.sb(gf2, "i3h", [128, 2 * 64 * QT], BF16)
    hqb = A.sb(gf2, "hqb", [128, 2 * 512], BF16)
    q32 = A.sb(gf2, "q32", [128, 512], F32)
    i3t = A.sb(gf2, "i3t", [128, 512], BF16)
    w3t = A.sb(gf2, "w3t", [128, 2 * 2048], BF16)
    x1r = A.sb(gf2, "x1r", [128, 2 * 512], F32)
    yout = A.sb(gf2, "yout", [128, 2 * 512], F32)
    g3s2 = A.sb(gf2, "g3s2", [128, 2 * 512], F32)
    mbc_gb = A.sb(gf2, "mbc_gb", [128, C], BF16)
    dma("gpsimd", mbc_gb[:, :], Dmodvec.ap()[5:6, :].partition_broadcast(128), sem="gmb2")
    gb_ld = ("gmb2", S.val("gmb2"))

    yst_prev = [None, None]
    hld_prev = [None, None]
    g3_cons2 = [0, 0]
    i3pe = [0, 0]
    i3v = [0, 0]
    banks4 = [0, 1, 3, 4]
    x1r_cons = [0, 0]
    w3slot_ev = [None, None]

    def f2_assemble(blk_i):
        par = blk_i % 2
        for n in range(HID // 512):
            g3par = n % 2
            wg3 = [("dve", g3_cons2[g3par])] if g3_cons2[g3par] else []
            g3ld2 = dma("gpsimd", g3s2[:, g3par * 512:(g3par + 1) * 512],
                        I["g3v"][:, n * 512:(n + 1) * 512].partition_broadcast(128), waits=wg3,
                        sem=f"g3m{g3par}")
            for tcl in range(nch_q):
                tc = blk_i * nch_q + tcl
                hslot = (n * nch_q + tcl) % 2
                wth = [("q_sth", S.cnt["q_sth"])]
                if hld_prev[hslot] is not None:
                    wth.append(hld_prev[hslot])
                ldh = dma("sync", hqb[:, hslot * 512:(hslot + 1) * 512],
                          Dh.ap()[tc * 128:(tc + 1) * 128, n * 512:(n + 1) * 512],
                          waits=wth, stream=f"f2h{hslot}")

                def i3q(eng, tc=tc, g3par=g3par, hslot=hslot):
                    eng.scalar_tensor_tensor(q32[:, 0:512], hqb[:, hslot * 512:(hslot + 1) * 512],
                                             ss3[:, tc:tc + 1], g3s2[:, g3par * 512:(g3par + 1) * 512],
                                             ALU.mult, ALU.mult)
                    eng.tensor_scalar(q32[:, 0:512], q32[:, 0:512], mx3[:, tc:tc + 1], None, ALU.mult)
                    eng.tensor_scalar(q32[:, 0:512], q32[:, 0:512], 127.0, CMAGIC, ALU.mult, ALU.add)
                    eng.tensor_scalar(q32[:, 0:512], q32[:, 0:512], CMAGIC - 128.0, CMAGIC + 127.0,
                                      ALU.max, ALU.min)
                    return eng.tensor_scalar(i3t[:, 0:512], q32[:, 0:512], -CMAGIC, None, ALU.add)
                S.emit("vector", i3q, waits=[ldh, g3ld2, ("pe", S.val("pe"))], inc=("dve", 1))
                hld_prev[hslot] = ("dve", S.val("dve"))
                qv = S.val("dve")
                for s4 in range(4):
                    icg = n * 4 + s4
                    bank = s4 % 2
                    pcol = bank * 1024 + (s4 // 2) * 128
                    gkb = "bfA" if bank == 0 else "bfB"
                    wt = [("dve", qv), ("pe", i3pe[par])]
                    for snm, v in bg_guard[gkb].items():
                        if v > 0:
                            wt.append((snm, v))
                    S.emit("tensor", lambda eng, s4=s4, pcol=pcol: eng.transpose(
                        pbf[:, pcol:pcol + 128], i3t[:, s4 * 128:(s4 + 1) * 128], cst["identbf"][:, :]),
                        waits=wt, inc=("pe", 1))
                    S.emit("vector", lambda eng, icg=icg, tcl=tcl, pcol=pcol, par=par: eng.tensor_copy(
                        i3h[:, par * 64 * QT + icg * QT + tcl * 128:
                            par * 64 * QT + icg * QT + (tcl + 1) * 128],
                        pbf[:, pcol:pcol + 128]),
                        waits=[("pe", S.val("pe"))], inc=("dve", 1))
                    bg_guard[gkb]["dve"] = S.val("dve")
            g3_cons2[g3par] = S.val("dve")
        i3v[par] = S.val("dve")

    def f2_matmul(blk_i):
        par = blk_i % 2
        for oc in range(4):
            for icg in range(16):
                wslot = icg % 2
                g = icg * 512
                r, j = g // 1024, g % 1024
                blk = Dwq_gath.ap()[r * 768 + 512: r * 768 + 768, :].rearrange(
                    "a (b c) -> (a b) c", b=4)
                wt = [("cc_wq", 1)]
                if w3slot_ev[wslot] is not None:
                    wt.append(w3slot_ev[wslot])
                ldw3 = dma("sync", w3t[:, wslot * 2048:(wslot + 1) * 2048]
                           .rearrange("p (a c) -> p a c", a=4),
                           blk[j:j + 512, oc * 512:(oc + 1) * 512].rearrange("(a p) c -> p a c", p=128),
                           waits=wt, stream=f"f2w{wslot}")
                for tci in range(nch_q):
                    bank = banks4[tci]

                    def f2mm(eng, icg=icg, tci=tci, wslot=wslot, bank=bank, par=par):
                        last = None
                        for a in range(4):
                            ic = icg * 4 + a
                            last = eng.matmul(
                                pb[bank][:, 0:512],
                                i3h[:, par * 64 * QT + ic * QT + tci * 128:
                                    par * 64 * QT + ic * QT + (tci + 1) * 128],
                                w3t[:, wslot * 2048 + a * 512: wslot * 2048 + (a + 1) * 512],
                                start=(ic == 0), stop=(ic == 63))
                        return last
                    wtm = [ldw3, ("dve", i3v[par])]
                    if icg == 0:
                        wtm += [("dve", bg_guard[bank].get("dve", 0)),
                                ("act", bg_guard[bank].get("act", 0))]
                    S.emit("tensor", f2mm, waits=wtm, inc=("pe", 1))
                w3slot_ev[wslot] = ("pe", S.val("pe"))
            mmend = S.val("pe")
            for tci in range(nch_q):
                tc = blk_i * nch_q + tci
                bank = banks4[tci]
                xsl = tci % 2
                ldx1 = dma("sync", x1r[:, xsl * 512:(xsl + 1) * 512],
                           Dx1.ap()[tc * 128:(tc + 1) * 128, oc * 512:(oc + 1) * 512],
                           waits=[x1_st_prev, ("dve", x1r_cons[xsl])], stream=f"f2x{xsl}")

                def yev(eng, tc=tc, oc=oc, bank=bank, xsl=xsl):
                    eng.tensor_scalar(yout[:, xsl * 512:(xsl + 1) * 512], pb[bank][:, 0:512],
                                      s3f[:, tc:tc + 1], None, ALU.mult)
                    eng.tensor_tensor(yout[:, xsl * 512:(xsl + 1) * 512],
                                      yout[:, xsl * 512:(xsl + 1) * 512],
                                      mbc_gb[:, oc * 512:(oc + 1) * 512], ALU.mult)
                    return eng.tensor_add(yout[:, xsl * 512:(xsl + 1) * 512],
                                          yout[:, xsl * 512:(xsl + 1) * 512], x1r[:, xsl * 512:(xsl + 1) * 512])
                wyev = [("pe", mmend), ldx1, gb_ld]
                if yst_prev[xsl] is not None:
                    wyev.append(yst_prev[xsl])
                S.emit("vector", yev, waits=wyev, inc=("dve", 1))
                bg_guard[bank]["dve"] = S.val("dve")
                x1r_cons[xsl] = S.val("dve")
                yst_prev[xsl] = dma("sync", O["y"][tc * 128:(tc + 1) * 128, oc * 512:(oc + 1) * 512],
                                    yout[:, xsl * 512:(xsl + 1) * 512],
                                    waits=[("dve", S.val("dve"))], stream=f"sty{xsl}")
        i3pe[par] = S.val("pe")

    f2_assemble(0)
    for blk_i in range(nblk):
        if blk_i + 1 < nblk:
            f2_assemble(blk_i + 1)
        f2_matmul(blk_i)
    print(f"[alloc] peak {A.peak}B/part at {A.peak_at}")
    _finish(nc, S, O, ntok, y_done=True)
    return nc, taps


def _finish(nc, S, O, ntok, y_done=False):
    if not y_done:
        ztile = nc.sbuf_tensor("zfin", [128, C], F32).__enter__()
        S.emit("vector", lambda eng: eng.memset(ztile[:, :], 0.0), inc=("dve", 1))
        vz = S.val("dve")
        for r in range(0, ntok, 128):
            S.emit("sync", lambda eng, r=r: eng.dma_start(out=O["y"][r:r + 128, :], in_=ztile[:, :]),
                   waits=[("dve", vz)], inc=("st", 16))
    finals = [(nm, S.cnt[nm]) for nm in list(S.cnt)
              if (nm.startswith("q_st") or nm in ("st", "gst")) and S.cnt[nm] > 0]

    def fin(eng):
        for nm, v in finals:
            eng.wait_ge(S.sems[nm], v)
    S.events.append(("sync", fin, "fin"))
    with nc.Block() as block:
        S.replay(block)


_CACHE = {}


def kernel(**inputs):
    from concourse.bass_utils import run_bass_kernel_spmd
    ntok = 2048
    if "nc" not in _CACHE:
        nc, _ = build(ntok=ntok, debug_taps=(), maxphase=99)
        _CACHE["nc"] = nc
    nc = _CACHE["nc"]
    in_maps = host_inputs(inputs, ntok=ntok)
    res = run_bass_kernel_spmd(nc, in_maps, core_ids=list(range(NCORES)))
    y = np.zeros((2, 4 * ntok, C), np.float32)
    for c in range(NCORES):
        y[c // 4, (c % 4) * ntok:(c % 4 + 1) * ntok] = res.results[c]["y"]
    return y


# version 06
# speedup vs baseline: 1.0091x; 1.0091x over previous
import numpy as np
import ml_dtypes
import concourse.bass as bass
import concourse.mybir as mybir

F32 = mybir.dt.float32
BF16 = mybir.dt.bfloat16
AF = mybir.ActivationFunctionType
ALU = mybir.AluOpType
AX = mybir.AxisListType

NCORES = 8
DBG_CAPTURE = None
P1_LIMIT = None
SKIP_HEADLOOP = False
HL_PARTS = ("bld1", "bld2", "bld4", "ktr", "x0", "powers", "applies", "post", "compose")
HL_PARTS2 = None
H, DK, DV = 16, 64, 128
KEY_DIM, VAL_DIM = H * DK, H * DV
C = 2048
HID = 8192
L = 128
CMAGIC = 12582912.0
ATANH_HALF = 0.5493061443340549
EPS_ORMS = 1.1920929e-07


_DRAIN_OPS = {
    "tensor_scalar", "tensor_single_scalar", "tensor_scalar_add", "tensor_scalar_sub",
    "tensor_scalar_mul", "tensor_scalar_max", "tensor_scalar_min", "tensor_tensor",
    "tensor_add", "tensor_sub", "tensor_mul", "tensor_max", "tensor_copy", "tensor_relu",
    "scalar_tensor_tensor", "tensor_tensor_scan", "tensor_reduce", "reduce_sum",
    "reduce_max", "reciprocal", "memset", "memzero", "select", "copy_predicated",
    "activation", "copy", "add", "mul", "square", "sqrt", "sign", "transpose",
    "stream_shuffle",
}


class DrainingEngine:

    def __init__(self, eng):
        object.__setattr__(self, "_e", eng)

    def xcopy(self, out, in_):
        e = self._e
        e.drain()
        if hasattr(e, "tensor_copy"):
            return e.tensor_copy(out, in_)
        return e.copy(out, in_)

    def xmul(self, out, in_, s):
        e = self._e
        e.drain()
        if hasattr(e, "tensor_scalar_mul"):
            return e.tensor_scalar_mul(out, in_, s)
        return e.mul(out, in_, s)

    def __getattr__(self, name):
        attr = getattr(self._e, name)
        if name in _DRAIN_OPS and callable(attr):
            e = self._e

            def call(*a, **k):
                e.drain()
                return attr(*a, **k)
            return call
        return attr


class Sched:
    ENGINES = ("sync", "tensor", "vector", "scalar", "gpsimd")

    def __init__(self, nc):
        self.nc = nc
        self.events = []
        self.sems = {}
        self.cnt = {}
        self.cur_phase = "init"

    def phase(self, name):
        self.cur_phase = name

    def sem(self, name):
        if name not in self.sems:
            self.sems[name] = self.nc.semaphore(name).__enter__()
            self.cnt[name] = 0
        return self.sems[name]

    def bump(self, name, amt):
        self.sem(name)
        self.cnt[name] += amt
        return self.cnt[name]

    def val(self, name):
        self.sem(name)
        return self.cnt[name]

    def emit(self, engine, fn, waits=(), inc=None):
        assert engine in self.ENGINES, engine
        wl = [(self.sems[s], v) for (s, v) in waits if v > 0]
        ih = None
        if inc is not None:
            self.bump(inc[0], inc[1])
            ih = (self.sems[inc[0]], inc[1])

        def wrapped(eng, fn=fn, wl=wl, ih=ih):
            for (s, v) in wl:
                eng.wait_ge(s, v)
            ins = fn(eng)
            if ih is not None:
                assert ins is not None, "emit with inc: fn must return an instruction"
                ins.then_inc(ih[0], ih[1])
        self.events.append((engine, wrapped, self.cur_phase))

    def flush_dma(self):
        pend = [(nm, self.cnt[nm]) for nm in list(self.cnt)
                if nm.startswith("q_") and self.cnt[nm] > 0]

        def fl(eng, pend=pend):
            for nm, v in pend:
                eng.wait_ge(self.sems[nm], v)
        self.events.append(("sync", fl, self.cur_phase))

    def barrier(self):
        self.flush_dma()
        vals = {}
        for e in self.ENGINES:
            sn = f"bar_{e}"
            vals[sn] = self.bump(sn, 1)
            sem = self.sems[sn]

            def post(eng, sem=sem):
                eng.drain()
                return eng.nop().then_inc(sem, 1)
            self.events.append((e, post, self.cur_phase))
        allv = dict(vals)

        def waitall(eng, allv=allv):
            for sn, v in allv.items():
                eng.wait_ge(self.sems[sn], v)
        for e in self.ENGINES:
            self.events.append((e, waitall, self.cur_phase))
        return allv

    def replay(self, block):
        evs = self.events
        nc = self.nc

        def run(eng, sel, wrap=False):
            d = DrainingEngine(eng) if wrap else eng
            cur = None
            sid = None
            for (e, fn, ph) in evs:
                if e != sel:
                    continue
                if ph != cur:
                    if cur is not None:
                        nc.leave_named_scope(cur, sid, False)
                    cur = ph
                    sid, _ = nc.enter_named_scope(cur, False)
                fn(d)
            if cur is not None:
                nc.leave_named_scope(cur, sid, False)

        @block.sync
        def _(eng):
            run(eng, "sync")

        @block.tensor
        def _(eng):
            run(eng, "tensor")

        @block.vector
        def _(eng):
            run(eng, "vector", wrap=True)

        @block.scalar
        def _(eng):
            run(eng, "scalar", wrap=True)

        @block.gpsimd
        def _(eng):
            run(eng, "gpsimd", wrap=True)


class Alloc:

    def __init__(self, nc):
        self.nc = nc
        self.groups = {}

    def sb(self, group, name, shape, dt=F32):
        g = self.groups.setdefault(group, [])
        cm = self.nc.sbuf_tensor(name, list(shape), dt)
        h = cm.__enter__()
        nb = shape[1] * (2 if dt == BF16 else 4)
        if not hasattr(self, "cur"):
            self.cur, self.peak, self.peak_at = 0, 0, ""
        self.cur += nb
        if self.cur > self.peak:
            self.peak, self.peak_at = self.cur, name
        g.append((cm, nb))
        return h

    def free(self, group):
        for cm, nb in reversed(self.groups.get(group, [])):
            cm.__exit__(None, None, None)
            self.cur -= nb
        self.groups[group] = []


def host_inputs(inp, ntok=2048):
    nslice = 4
    x = inp["L0_attn_x"]
    B, T, _ = x.shape
    f32 = np.float32

    def cc(a):
        return np.ascontiguousarray(a.astype(f32))

    convall = np.concatenate([
        inp["L0_attn_q_conv_w"][:, 0, :],
        inp["L0_attn_k_conv_w"][:, 0, :],
        inp["L0_attn_v_conv_w"][:, 0, :],
    ], axis=0)
    convw = cc(convall.reshape(32, 128, 4).transpose(1, 0, 2).reshape(128, 128))
    cm = inp["L0_c"]
    c_fm = cc(cm.reshape(2, 16, 128).transpose(2, 1, 0).reshape(128, 32))
    ident = np.eye(128, dtype=f32)
    identbf = np.eye(128, dtype=ml_dtypes.bfloat16)
    ar = np.arange(L)
    mneg_s = np.where(ar[:, None] < ar[None, :], 0.0, -1e30).astype(f32)
    mneg_i = np.where(ar[:, None] <= ar[None, :], 0.0, -1e30).astype(f32)
    mneg_sT = np.ascontiguousarray(mneg_s.T)

    def mask4(m):
        return np.ascontiguousarray(np.tile(m, (1, 4)))
    headmask = np.zeros((128, 2), f32)
    headmask[:64, 0] = 1.0
    headmask[64:, 1] = 1.0
    maskT_q = np.zeros((2, 128), f32)
    maskT_q[0, :64] = DK ** -0.5
    maskT_q[1, 64:] = DK ** -0.5
    maskT_k = (maskT_q > 0).astype(f32)
    ohsel = np.zeros((16, 2048), f32)
    for hh in range(16):
        ohsel[hh, hh * 128:(hh + 1) * 128] = 1.0

    shared = {
        "qkT": cc(inp["L0_attn_qk_w"].T), "vgT": cc(inp["L0_attn_vg_w"].T),
        "owT": cc(inp["L0_attn_o_w"].T),
        "abT": cc(np.concatenate([inp["L0_attn_a_w"].T, inp["L0_attn_b_w"].T], axis=1)),
        "convw": convw, "c_fm": c_fm,
        "b_bias": cc(inp["L0_attn_b_bias"].reshape(16, 1)),
        "beta_bias": cc(inp["L0_attn_beta_bias"].reshape(16, 1)),
        "dt_bias": cc(inp["L0_attn_dt_bias"].reshape(16, 1)),
        "A_log": cc(inp["L0_attn_A_log"].reshape(16, 1)),
        "g1v": cc(inp["L0_ffn_g1"].reshape(1, C)),
        "g2v": cc(inp["L0_ffn_g2"].reshape(1, C)),
        "g3v": cc(inp["L0_ffn_g3"].reshape(1, HID)),
        "ident": ident, "identbf": identbf,
        "mneg_s4": mask4(mneg_s), "mneg_i4": mask4(mneg_i), "mneg_sT4": mask4(mneg_sT),
        "headmask": headmask, "maskT_q": maskT_q, "maskT_k": maskT_k,
        "ones128x1": np.ones((128, 1), f32), "onesrow": np.ones((16, L), f32),
        "ohsel": ohsel,
    }
    in_maps = []
    for core in range(NCORES):
        b, p = core // nslice, core % nslice
        t0 = p * ntok
        m = dict(shared)
        xs = np.zeros((ntok + 3, C), f32)
        if t0 >= 3:
            xs[:] = x[b, t0 - 3:t0 + ntok]
        else:
            xs[3 - t0:] = x[b, 0:t0 + ntok]
        m["x_sl"] = cc(xs)
        m["haloflag"] = np.array([[0.0 if t0 == 0 else 1.0]], f32)
        m["wmod_sl"] = cc(inp["L0_wmod"][:, core * 1536:(core + 1) * 1536])
        selb = np.zeros((1, 2), f32)
        selb[0, b] = 1.0
        m["selb"] = selb
        selp = np.zeros((1, 24), f32)
        for j in range(3):
            r = core - 1 - j
            if r >= b * nslice:
                selp[0, j * 8 + r] = 1.0
        m["selp"] = selp
        m["w1T_sl"] = cc(inp["L0_ffn_w1"].T[core * 256:(core + 1) * 256])
        m["w2T_sl"] = cc(inp["L0_ffn_w2"].T[core * 256:(core + 1) * 256])
        m["w3T_sl"] = cc(inp["L0_ffn_w3"].T[core * 1024:(core + 1) * 1024])
        in_maps.append(m)
    return in_maps


CONSTS_CORE = [
    ("identbf", [128, 128], BF16),
    ("ones128x1", [128, 1], F32), ("onesrow", [16, L], F32),
    ("selb", [1, 2], F32), ("headmask", [128, 2], F32),
    ("b_bias", [16, 1], F32), ("beta_bias", [16, 1], F32),
    ("dt_bias", [16, 1], F32), ("A_log", [16, 1], F32),
]
CONSTS_P1 = [
    ("convw", [128, 128], F32), ("c_fm", [128, 32], F32),
    ("ident", [128, 128], F32),
    ("mneg_s4", [128, 512], F32), ("mneg_i4", [128, 512], F32), ("mneg_sT4", [128, 512], F32),
    ("maskT_q", [2, 128], F32), ("maskT_k", [2, 128], F32),
    ("ohsel", [16, 2048], F32),
]
CONSTS = CONSTS_CORE + CONSTS_P1


def build(ntok=2048, debug_taps=(), maxphase=99):
    nch = ntok // L
    NTP = ntok + 4
    nc = bass.Bass("TRN2", target_bir_lowering=False, debug=False)

    def din(name, shape, dt=F32):
        return nc.dram_tensor(name, list(shape), dt, kind="ExternalInput").ap()

    def dout(name, shape, dt=F32):
        return nc.dram_tensor(name, list(shape), dt, kind="ExternalOutput").ap()

    I = {}
    I["x_sl"] = din("x_sl", [ntok + 3, C])
    I["qkT"] = din("qkT", [C, C])
    I["vgT"] = din("vgT", [C, 2 * VAL_DIM])
    I["owT"] = din("owT", [VAL_DIM, C])
    I["abT"] = din("abT", [C, 32])
    I["g1v"] = din("g1v", [1, C])
    I["g2v"] = din("g2v", [1, C])
    I["g3v"] = din("g3v", [1, HID])
    I["wmod_sl"] = din("wmod_sl", [C, 1536])
    I["w1T_sl"] = din("w1T_sl", [256, HID])
    I["w2T_sl"] = din("w2T_sl", [256, HID])
    I["w3T_sl"] = din("w3T_sl", [1024, C])
    for (nm, shape, dt) in CONSTS:
        I[nm] = din(nm, shape, dt)
    I["haloflag"] = din("haloflag", [1, 1])
    I["selp"] = din("selp", [1, 24])

    O = {"y": dout("y", [ntok, C])}
    taps = {}

    def mktap(name, shape, dt=F32):
        if name in debug_taps:
            taps[name] = dout("tap_" + name, list(shape), dt)

    mktap("h_fm", [128, 16 * NTP], BF16)
    mktap("modvec", [8, C])
    mktap("q_fm", [128, 8 * ntok], BF16)
    mktap("k_fm", [128, 8 * ntok], BF16)
    mktap("vsp", [nch * 16 * 128, 128], BF16)
    mktap("gsp", [VAL_DIM, ntok], BF16)
    mktap("cum", [128, nch * 16])
    mktap("beta", [128, nch * 16])
    mktap("bg", [128, nch * 16])
    mktap("kbs", [128, nch * 16])
    mktap("glrow", [1, nch * 16])
    mktap("negT", [64, nch * 16 * 64], BF16)
    mktap("Yc", [64, nch * 16 * 128], BF16)
    mktap("ouT", [nch * 16 * 128, 128], BF16)
    mktap("zT", [nch * 16 * 64, 128], BF16)
    mktap("pq", [H * 64, 192], BF16)
    mktap("sinit", [64, 16 * 128])
    mktap("x1", [ntok, C])
    mktap("i1", [16 * nch * 128, 128], BF16)
    mktap("hmid", [ntok, HID], BF16)
    mktap("f2sc", [128, 6 * 16])
    mktap("i3h", [128, 64 * 128], BF16)

    Dmodpart = nc.dram_tensor("Dmodpart", [2, 1536], F32)
    Dmodgath = nc.dram_tensor("Dmodgath", [16, 1536], F32)
    Dmodvec = nc.dram_tensor("Dmodvec", [8, C], F32)
    Dwsum_in = nc.dram_tensor("Dwsum_in", [1, 4], F32)
    Dwsum_out = nc.dram_tensor("Dwsum_out", [1, 4], F32)
    Dwq_in = nc.dram_tensor("Dwq_in", [768, HID], BF16)
    Dwq_gath = nc.dram_tensor("Dwq_gath", [NCORES * 768, HID], BF16)
    Dpq_in = nc.dram_tensor("Dpq_in", [H * 64, 192], BF16)
    Dpq_gath = nc.dram_tensor("Dpq_gath", [NCORES * H * 64, 192], BF16)
    Dvsp = nc.dram_tensor("Dvsp", [nch * 16 * 128, 128], BF16)
    Dgsp = nc.dram_tensor("Dgsp", [VAL_DIM, ntok], BF16)
    Dou = nc.dram_tensor("Dou", [nch * 16 * 128, 128], BF16)
    Dzt = nc.dram_tensor("Dzt", [nch * 16 * 64, 128], BF16)
    Dx1 = nc.dram_tensor("Dx1", [ntok, C], F32)
    Di1 = nc.dram_tensor("Di1", [16 * nch * 128, 128], BF16)
    Di2 = nc.dram_tensor("Di2", [16 * nch * 128, 128], BF16)
    Dh = nc.dram_tensor("Dh", [ntok, HID], BF16)
    Dq = nc.dram_tensor("Dq", [1024, ntok], BF16)
    Dk = nc.dram_tensor("Dk", [1024, ntok], BF16)
    DnegT = nc.dram_tensor("DnegT", [nch * 16 * 64, 64], BF16)
    Dog = nc.dram_tensor("Dog", [nch * 128, 16 * 128], BF16)

    S = Sched(nc)
    A = Alloc(nc)
    for s in ["ld", "st", "gld", "gst", "pe", "dve", "act", "pool",
              "cc_mod", "cc_wsum", "cc_wq", "cc_pq"]:
        S.sem(s)
    for e in S.ENGINES:
        S.sem(f"bar_{e}")

    cst = {nm: A.sb("const", "c_" + nm, shape, dt) for (nm, shape, dt) in CONSTS_CORE}
    cst["haloflag"] = A.sb("const", "c_haloflag", [128, 1], F32)
    cst["selp"] = A.sb("const", "c_selp", [128, 24], F32)
    cvals = A.sb("const", "cvals", [128, 8], F32)
    smallv = A.sb("const", "smallv", [16, 16], F32)
    fsc = A.sb("const", "fsc", [1, 16], F32)
    fscb = A.sb("const", "fscb", [128, 8], F32)
    thb = A.sb("const", "thb", [128, 8], F32)
    s1f = A.sb("fsc2", "s1f", [128, nch], F32)
    s2f = A.sb("fsc2", "s2f", [128, nch], F32)
    s3f = A.sb("fsc2", "s3f", [128, nch], F32)
    mx3 = A.sb("fsc2", "mx3", [128, nch], F32)
    ss3 = A.sb("fsc2", "ss3", [128, nch], F32)
    sc3 = A.sb("fsc2", "sc3", [128, 4], F32)
    st_cum = A.sb("stats", "st_cum", [128, nch * 16], F32)
    st_beta = A.sb("stats", "st_beta", [128, nch * 16], F32)
    st_bg = A.sb("stats", "st_bg", [128, nch * 16], F32)
    st_kbs = A.sb("stats", "st_kbs", [128, nch * 16], F32)
    st_cumb = A.sb("stats", "st_cumb", [128, nch * 16], F32)
    gl_row = A.sb("stats", "gl_row", [1, nch * 16], F32)
    glb = A.sb("stats", "glb", [128, nch * 16], F32)
    Ycs = A.sb("stats", "Ycs", [64, 2 * 128], BF16)
    for _nm, _shape, _dt in CONSTS_P1:
        cst[_nm] = A.sb("constp1", "c_" + _nm, _shape, _dt)

    pb = [nc.psum_tensor(f"pb{i}", [128, 512], F32).__enter__() for i in range(6)]
    pbf = nc.psum_tensor("pbf", [128, 2048], BF16).__enter__()
    bank_guard = {}

    def dma(engine, out, in_, waits=(), sem=None, stream=None):
        if stream is not None:
            sem = "q_" + stream
        if sem is None:
            sem = "gld" if engine == "gpsimd" else "ld"
        S.sem(sem)
        S.emit(engine, lambda eng, o=out, i=in_: eng.dma_start(out=o, in_=i), waits=waits, inc=(sem, 16))
        return (sem, S.val(sem))

    def cinit(eng):
        eng.memset(cvals[:, 0:1], 1e-5)
        eng.memset(cvals[:, 1:2], 1e-6)
        eng.memset(cvals[:, 2:3], EPS_ORMS)
        eng.memset(cvals[:, 3:4], 1e-8)
        return eng.memset(cvals[:, 4:5], 1.0)
    S.emit("vector", cinit, inc=("dve", 1))
    cinit_v = S.val("dve")
    for nm, t in cst.items():
        if nm in ("haloflag", "selp"):
            dma("sync", t[:, :], I[nm].partition_broadcast(128), stream="c")
        else:
            dma("sync", t[:, :], I[nm], stream="c")
    ld_consts = ("q_c", S.val("q_c"))

    S.phase("mod")
    g0 = "ph0"
    mwork = A.sb(g0, "mwork", [128, 8192], BF16)
    mfp = A.sb(g0, "mfp", [128, 8192], F32)
    cbf = A.sb(g0, "cbf", [128, 32], BF16)
    mrow = A.sb(g0, "mrow", [1, 12288], F32)
    mr2 = A.sb(g0, "mr2", [1, 2048], F32)
    mr3 = A.sb(g0, "mr3", [1, 2048], F32)
    g2t = A.sb(g0, "g2t", [2, 1536], F32)
    g2a = A.sb(g0, "g2a", [1, 1536], F32)
    g2b = A.sb(g0, "g2b", [1, 1536], F32)

    S.emit("scalar", lambda eng: eng.activation(cst["c_fm"][:, :], cst["c_fm"][:, :], AF.Silu),
           waits=[ld_consts], inc=("act", 1))
    S.emit("scalar", lambda eng: eng.copy(cbf[:, :], cst["c_fm"][:, :]), inc=("act", 1))
    silu_c = S.val("act")

    prev_pe = 0
    prev_cast = 0
    modbank_dve = [0, 0]
    for j in range(3):
        src = I["wmod_sl"][:, j * 512:(j + 1) * 512].rearrange("(ic p) o -> p ic o", p=128)
        ldv = dma("sync", mfp[:, :].rearrange("p (ic o) -> p ic o", ic=16), src,
                  waits=[("pe", prev_pe), ("act", prev_cast)], stream="mfp")
        S.emit("scalar", lambda eng: eng.copy(mwork[:, :], mfp[:, :]), waits=[ldv], inc=("act", 1))
        cv = S.val("act")
        prev_cast = cv

        def mm(eng, j=j):
            last = None
            for ic in range(16):
                last = eng.matmul(pb[j % 2][0:2, :], cbf[:, 2 * ic:2 * ic + 2],
                                  mwork[:, ic * 512:(ic + 1) * 512],
                                  start=(ic == 0), stop=(ic == 15))
            return last
        S.emit("tensor", mm, waits=[("act", cv), ("act", silu_c), ("dve", modbank_dve[j % 2])],
               inc=("pe", 1))
        prev_pe = S.val("pe")
        S.emit("vector", lambda eng, j=j: eng.tensor_copy(g2t[0:2, j * 512:(j + 1) * 512], pb[j % 2][0:2, :]),
               waits=[("pe", prev_pe)], inc=("dve", 1))
        modbank_dve[j % 2] = S.val("dve")
    mod_st = dma("sync", Dmodpart.ap(), g2t[0:2, :], waits=[("dve", S.val("dve"))], stream="stmod")
    S.emit("gpsimd", lambda eng: eng.collective_compute(
        "AllGather", ALU.bypass, replica_groups=[list(range(NCORES))],
        ins=[Dmodpart.ap().opt()], outs=[Dmodgath.ap().opt()]),
        waits=[mod_st], inc=("cc_mod", 1))
    for r in range(NCORES):
        dma("sync", g2a[0:1, :], Dmodgath.ap()[2 * r:2 * r + 1, :],
            waits=[("cc_mod", 1), ("dve", S.val("dve"))], stream="gath")
        ldv = dma("sync", g2b[0:1, :], Dmodgath.ap()[2 * r + 1:2 * r + 2, :], stream="gath")

        def blend(eng, r=r):
            eng.tensor_scalar(mr2[0:1, 0:1536], g2a[0:1, 0:1536], cst["selb"][0:1, 0:1], None, ALU.mult)
            return eng.scalar_tensor_tensor(mrow[0:1, r * 1536:(r + 1) * 1536], g2b[0:1, 0:1536],
                                            cst["selb"][0:1, 1:2], mr2[0:1, 0:1536], ALU.mult, ALU.add)
        S.emit("vector", blend, waits=[ldv], inc=("dve", 1))

    def vs(i):
        return mrow[0:1, i * 2048:(i + 1) * 2048]
    S.emit("vector", lambda eng: eng.tensor_scalar(vs(1), vs(1), 1.0, None, ALU.add), inc=("dve", 1))
    S.emit("vector", lambda eng: eng.tensor_scalar(vs(4), vs(4), 1.0, None, ALU.add), inc=("dve", 1))
    for iv in (2, 5):
        def clamp(eng, iv=iv):
            eng.tensor_scalar(mr2[0:1, :], vs(iv), 0.0, None, ALU.is_ge)
            eng.tensor_scalar(mr2[0:1, :], mr2[0:1, :], 2.0, -1.0, ALU.mult, ALU.add)
            eng.tensor_scalar(mr3[0:1, :], vs(iv), -1.0, None, ALU.mult)
            eng.tensor_tensor(mr3[0:1, :], mr3[0:1, :], vs(iv), ALU.max)
            eng.tensor_scalar(mr3[0:1, :], mr3[0:1, :], 0.25, None, ALU.max)
            return eng.tensor_mul(vs(iv), mr2[0:1, :], mr3[0:1, :])
        S.emit("vector", clamp, inc=("dve", 1))
    modproc = S.val("dve")
    for i, v in enumerate([1, 0, 2, 4, 3, 5]):
        modvec_st = dma("sync", Dmodvec.ap()[i:i + 1, :], vs(v), waits=[("dve", modproc)], stream="stmv")
    if "modvec" in taps:
        for i in range(6):
            dma("sync", taps["modvec"][i:i + 1, :], Dmodvec.ap()[i:i + 1, :], waits=[modvec_st], stream="sttap")
    A.free(g0)
    bar0 = S.barrier()

    S.phase("wq")
    Dfsc = nc.dram_tensor("Dfsc", [1, 16], F32)
    gw = "phW"
    wfq = A.sb(gw, "wfq", [128, HID], F32)
    wqo = A.sb(gw, "wqo", [128, HID], BF16)
    wsg = A.sb(gw, "wsg", [128, HID], F32)
    wcs = A.sb(gw, "wcs", [128, 4], F32)
    wrow = A.sb(gw, "wrow", [1, 32], F32)

    S.emit("vector", lambda eng: eng.memset(wcs[:, :], 0.0), inc=("dve", 1))
    wslices = [("w1T_sl", I["w1T_sl"], 0), ("w2T_sl", I["w2T_sl"], 1), ("w3T_sl", I["w3T_sl"], 2)]
    prev_red = S.val("dve")
    for nm, ap, col in wslices:
        flat = ap.rearrange("a b -> (a b)") if nm == "w3T_sl" else None
        for half in range(2):
            if nm == "w3T_sl":
                srcv = ap.rearrange("(x a) b -> x (a b)", x=2)[half:half + 1, :].rearrange(
                    "x (p c) -> (x p) c", p=128)
            else:
                srcv = ap[half * 128:(half + 1) * 128, :]
            ldv = dma("sync", wfq[:, :], srcv, waits=[("dve", prev_red)], stream="wq")
            S.emit("vector", lambda eng: eng.tensor_reduce(wcs[:, 3:4], wfq[:, :], AX.X, ALU.add,
                                                           apply_absolute_value=True),
                   waits=[ldv], inc=("dve", 1))
            S.emit("vector", lambda eng, col=col: eng.tensor_add(wcs[:, col:col + 1], wcs[:, col:col + 1],
                                                                 wcs[:, 3:4]),
                   inc=("dve", 1))
            prev_red = S.val("dve")
    S.emit("tensor", lambda eng: eng.matmul(pb[0][0:1, 0:4], cst["ones128x1"][:, 0:1], wcs[:, 0:4],
                                            start=True, stop=True),
           waits=[("dve", prev_red), ("dve", S.val("dve"))], inc=("pe", 1))
    S.emit("vector", lambda eng: eng.tensor_scalar(wrow[0:1, 0:4], pb[0][0:1, 0:4], 1.0 / (HID * C), None,
                                                   ALU.mult),
           waits=[("pe", S.val("pe"))], inc=("dve", 1))
    wsum_st = dma("sync", Dwsum_in.ap(), wrow[0:1, 0:4], waits=[("dve", S.val("dve"))], stream="stws")
    S.emit("gpsimd", lambda eng: eng.collective_compute(
        "AllReduce", ALU.add, replica_groups=[list(range(NCORES))],
        ins=[Dwsum_in.ap().opt()], outs=[Dwsum_out.ap().opt()]),
        waits=[wsum_st], inc=("cc_wsum", 1))
    ldw = dma("sync", wrow[0:1, 0:4], Dwsum_out.ap(), waits=[("cc_wsum", 1)], stream="wq")
    def wstats(eng):
        eng.tensor_scalar(wrow[0:1, 4:8], wrow[0:1, 0:4], 1e-8, ATANH_HALF, ALU.add, ALU.mult)
        eng.tensor_scalar(wrow[0:1, 8:12], wrow[0:1, 4:8], -1.0, None, ALU.mult)
        return eng.tensor_scalar(wrow[0:1, 12:16], wrow[0:1, 0:4], -1.0, None, ALU.mult)
    S.emit("vector", wstats, waits=[ldw], inc=("dve", 1))
    S.emit("scalar", lambda eng: eng.activation(wrow[0:1, 16:20], wrow[0:1, 0:4], AF.Ln, bias=1.0),
           waits=[("dve", S.val("dve"))], inc=("act", 1))
    S.emit("scalar", lambda eng: eng.activation(wrow[0:1, 20:24], wrow[0:1, 12:16], AF.Ln, bias=1.0),
           inc=("act", 1))

    def wfin(eng):
        eng.tensor_sub(wrow[0:1, 24:28], wrow[0:1, 16:20], wrow[0:1, 20:24])
        return eng.tensor_scalar(wrow[0:1, 24:28], wrow[0:1, 24:28], 0.5 / 127.0 * 0.5, None, ALU.mult)
    S.emit("vector", wfin, waits=[("act", S.val("act"))], inc=("dve", 1))
    wv = S.val("dve")
    dma("sync", Dfsc.ap()[0:1, 0:4], wrow[0:1, 24:28], waits=[("dve", wv)], stream="stfsc")
    dma("sync", Dfsc.ap()[0:1, 4:8], wrow[0:1, 8:12], waits=[("dve", wv)], stream="stfsc")
    fsc_st = dma("sync", Dfsc.ap()[0:1, 8:12], wrow[0:1, 4:8], waits=[("dve", wv)], stream="stfsc")
    dma("sync", fscb[:, 0:4], Dfsc.ap()[0:1, 0:4].partition_broadcast(128), waits=[fsc_st], stream="thld")
    ldth = dma("sync", thb[:, 0:8], Dfsc.ap()[0:1, 4:12].partition_broadcast(128), waits=[fsc_st], stream="thld")
    fscb_ld = ldth
    state_wq = {}
    prev_q = 0
    for nm, ap, col in wslices:
        for half in range(2):
            if nm == "w3T_sl":
                srcv = ap.rearrange("(x a) b -> x (a b)", x=2)[half:half + 1, :].rearrange(
                    "x (p c) -> (x p) c", p=128)
            else:
                srcv = ap[half * 128:(half + 1) * 128, :]
            ldv = dma("sync", wfq[:, :], srcv, waits=[("act", prev_q), ("dve", S.val("dve"))], stream="wq")
            S.emit("scalar", lambda eng, col=col: eng.activation(wsg[:, :], wfq[:, :], AF.Sign,
                                                                 bias=thb[:, col:col + 1]),
                   waits=[ldv, ldth], inc=("act", 1))
            S.emit("scalar", lambda eng, col=col: eng.activation(wfq[:, :], wfq[:, :], AF.Sign,
                                                                 bias=thb[:, 4 + col:4 + col + 1]),
                   inc=("act", 1))
            prev_q = S.val("act")
            wtq = [("act", prev_q)]
            if state_wq.get("st"):
                wtq.append(state_wq["st"])
            S.emit("vector", lambda eng: eng.tensor_add(wqo[:, :], wsg[:, :], wfq[:, :]),
                   waits=wtq, inc=("dve", 1))
            state_wq["st"] = dma("sync", Dwq_in.ap()[(col * 2 + half) * 128:(col * 2 + half + 1) * 128, :],
                                 wqo[:, :], waits=[("dve", S.val("dve"))], stream="stwq")
    S.emit("gpsimd", lambda eng: eng.collective_compute(
        "AllGather", ALU.bypass, replica_groups=[list(range(NCORES))],
        ins=[Dwq_in.ap().opt()], outs=[Dwq_gath.ap().opt()]),
        waits=[("q_stwq", S.val("q_stwq"))], inc=("cc_wq", 1))
    barW = S.barrier()
    A.free(gw)
    hm_cum = A.sb("hm", "hm_cum", [16, ntok], F32)
    hm_cumb = A.sb("hm", "hm_cumb", [16, ntok], F32)
    hm_expc = A.sb("hm", "hm_expc", [16, ntok], F32)

    if maxphase < 1:
        _finish(nc, S, O, ntok)
        return nc, taps

    S.phase("A1")
    h_fm = A.sb("hfm", "h_fm", [128, 16 * NTP], BF16)
    S.emit("vector", lambda eng: eng.memset(h_fm[:, :], 0.0), inc=("dve", 1))
    hfm_init = S.val("dve")
    gA = "phA1"
    xt = A.sb(gA, "xt", [128, C], F32)
    ht2 = A.sb(gA, "ht2", [128, C], F32)
    htb = A.sb(gA, "htb", [128, C], BF16)
    stt1 = A.sb(gA, "stt1", [128, 8], F32)
    mbc0 = A.sb(gA, "mbc0", [128, C], F32)
    mbc1 = A.sb(gA, "mbc1", [128, C], F32)

    dma("sync", mbc0[:, :], Dmodvec.ap()[0:1, :].partition_broadcast(128), waits=[modvec_st], stream="mbc")
    dma("sync", mbc1[:, :], Dmodvec.ap()[1:2, :].partition_broadcast(128), waits=[modvec_st], stream="mbc")
    ld_mod = ("q_mbc", S.val("q_mbc"))

    prev_evac = 0
    prev_ld = 0

    def ln_tile(rows, roff, coloff, is_halo):
        nonlocal prev_evac, prev_ld
        ldv = dma("sync", xt[0:rows, :], I["x_sl"][roff:roff + rows, :],
                  waits=[ld_mod, ("dve", prev_ld)], stream="x")
        prev_ld_stats = S.val("dve")
        S.emit("vector", lambda eng: eng.reduce_sum(stt1[0:rows, 0:1], xt[0:rows, :], AX.X),
               waits=[ldv], inc=("dve", 1))
        S.emit("scalar", lambda eng: eng.activation(ht2[0:rows, :], xt[0:rows, :], AF.Square,
                                                    accum_out=stt1[0:rows, 1:2]),
               waits=[ldv], inc=("act", 1))
        vact = S.val("act")

        def stats(eng):
            st = stt1
            eng.tensor_scalar(st[0:rows, 0:1], st[0:rows, 0:1], 1.0 / C, None, ALU.mult)
            eng.tensor_scalar(st[0:rows, 2:3], st[0:rows, 1:2], 1.0 / C, None, ALU.mult)
            eng.tensor_mul(st[0:rows, 3:4], st[0:rows, 0:1], st[0:rows, 0:1])
            return eng.tensor_sub(st[0:rows, 2:3], st[0:rows, 2:3], st[0:rows, 3:4])
        S.emit("vector", stats, waits=[("act", vact)], inc=("dve", 1))
        S.emit("scalar", lambda eng: eng.activation(stt1[0:rows, 4:5], stt1[0:rows, 2:3], AF.Sqrt,
                                                    bias=cvals[0:rows, 0:1]),
               waits=[("dve", S.val("dve")), ("dve", cinit_v)], inc=("act", 1))
        S.emit("vector", lambda eng: eng.reciprocal(stt1[0:rows, 5:6], stt1[0:rows, 4:5]),
               waits=[("act", S.val("act"))], inc=("dve", 1))
        S.emit("vector", lambda eng: eng.tensor_scalar(xt[0:rows, :], xt[0:rows, :],
                                                       stt1[0:rows, 0:1], stt1[0:rows, 5:6],
                                                       ALU.subtract, ALU.mult),
               inc=("dve", 1))
        S.emit("vector", lambda eng: eng.tensor_mul(ht2[0:rows, :], xt[0:rows, :], mbc0[0:rows, :]),
               inc=("dve", 1))
        if is_halo:
            S.emit("vector", lambda eng: eng.tensor_add(ht2[0:rows, :], ht2[0:rows, :], mbc1[0:rows, :]),
                   inc=("dve", 1))
            S.emit("vector", lambda eng: eng.tensor_scalar(htb[0:rows, :], ht2[0:rows, :],
                                                           cst["haloflag"][0:rows, 0:1], None, ALU.mult),
                   inc=("dve", 1))
        else:
            S.emit("vector", lambda eng: eng.tensor_tensor(htb[0:rows, :], ht2[0:rows, :], mbc1[0:rows, :], ALU.add),
                   inc=("dve", 1))
        vdve = S.val("dve")

        def transp(eng):
            last = None
            for ic in range(16):
                last = eng.transpose(pbf[:, ic * 128: ic * 128 + rows],
                                     htb[0:rows, ic * 128:(ic + 1) * 128], cst["identbf"][0:rows, 0:rows])
            return last
        S.emit("tensor", transp, waits=[("dve", vdve), ("dve", max(prev_evac, hfm_init))], inc=("pe", 1))
        vpe = S.val("pe")

        def evac(eng):
            last = None
            for ic in range(16):
                src = pbf[:, ic * 128: ic * 128 + rows]
                last = eng.tensor_copy(h_fm[:, ic * NTP + coloff: ic * NTP + coloff + rows], src)
            return last
        S.emit("vector", evac, waits=[("pe", vpe)], inc=("dve", 1))
        prev_evac = S.val("dve")
        prev_ld = prev_ld_stats
        prev_ld = prev_evac

    ln_tile(3, 0, 0, True)
    for t in range(ntok // 128):
        ln_tile(128, 3 + t * 128, 3 + t * 128, False)
    A.free(gA)
    barA1 = S.barrier()
    if "h_fm" in taps:
        dma("sync", taps["h_fm"], h_fm[:, :], stream="sttap")

    if maxphase < 2:
        _finish(nc, S, O, ntok)
        return nc, taps

    S.phase("A2")
    gB0 = "phA2w"
    wbf = A.sb(gB0, "wbf", [128, 2 * 2048], BF16)
    gB1 = "phA2s"
    raw = A.sb(gB1, "raw", [128, ntok + 3], F32)
    cvb = A.sb(gB1, "cvb", [128, ntok], F32)
    sqb = A.sb(gB1, "sqb", [128, 2 * 512], F32)
    nrm = A.sb(gB1, "nrm", [2, ntok], F32)
    bfnt4 = A.sb(gB1, "bfnt4", [128, 512], BF16)
    bf4 = A.sb(gB1, "bf4", [128, 512], BF16)
    qst = A.sb(gB1, "qst", [128, 2 * 512], BF16)

    ncols = ntok + 3
    grs = []
    c0 = 0
    while c0 < ncols:
        grs.append((c0, min(512, ncols - c0)))
        c0 += 512

    state = {"oc_done_dve": 0, "oc_done_pool": 0, "oc_done_st": 0, "oc_done_act": 0,
             "b01_dve": [0, 0], "vst": [], "vslot": 0, "vbank_dve": [0, 0],
             "sq_pe": [0, 0], "vtp_pe": [0, 0, 0, 0], "g_slot_st": [None, None],
             "qk_st": [None, None], "wslot": 0, "wslot_pe": [0, 0]}

    def proj_oc(kind, oc):
        wsrc, wcol = (I["qkT"], oc * 128) if kind in "qk" else (I["vgT"], oc * 128)
        if kind == "k":
            wcol += KEY_DIM
        if kind == "g":
            wcol += VAL_DIM
        src = wsrc[:, wcol:wcol + 128].rearrange("(ic p) o -> p ic o", p=128)
        wsl = state["wslot"] % 2
        state["wslot"] += 1
        ldv = dma("gpsimd", wbf[:, wsl * 2048:(wsl + 1) * 2048].rearrange("p (ic o) -> p ic o", ic=16),
                  src, waits=[("pe", state["wslot_pe"][wsl])], sem=f"wfg{wsl}")
        for gi, (g0c, w) in enumerate(grs):
            bk = gi % 2

            def mm(eng, g0c=g0c, w=w, bk=bk, wsl=wsl):
                last = None
                for ic in range(16):
                    last = eng.matmul(pb[bk][:, 0:w], wbf[:, wsl * 2048 + ic * 128: wsl * 2048 + (ic + 1) * 128],
                                      h_fm[:, ic * NTP + g0c: ic * NTP + g0c + w],
                                      start=(ic == 0), stop=(ic == 15))
                return last
            S.emit("tensor", mm, waits=[ldv, ("dve", state["b01_dve"][bk])], inc=("pe", 1))
            S.emit("vector", lambda eng, g0c=g0c, w=w, bk=bk: eng.tensor_copy(raw[:, g0c:g0c + w], pb[bk][:, 0:w]),
                   waits=[("pe", S.val("pe")), ("pool", state["oc_done_pool"]),
                          ("dve", state["oc_done_dve"])], inc=("dve", 1))
            state["b01_dve"][bk] = S.val("dve")
        state["wslot_pe"][wsl] = S.val("pe")
        rawv = S.val("dve")

        if kind == "g":
            wt_g = [("dve", rawv)]
            if state.get("last_g_st"):
                wt_g.append(state["last_g_st"])
            S.emit("scalar", lambda eng: eng.activation(cvb[:, 0:ntok], raw[:, 3:3 + ntok], AF.Silu),
                   waits=wt_g, inc=("act", 1))
            sil_g = S.val("act")
            state["oc_done_act"] = sil_g
            gslots = (bf4, bfnt4)
            for gi2, g0c in enumerate(range(0, ntok, 512)):
                w = min(512, ntok - g0c)
                gsl = gi2 % 2
                wt2 = [("act", sil_g)]
                if state["g_slot_st"][gsl] is not None:
                    wt2.append(state["g_slot_st"][gsl])
                else:
                    wt2 += state["vst"][-4:] + [("pe", S.val("pe"))]
                S.emit("vector", lambda eng, g0c=g0c, w=w, gsl=gsl: eng.tensor_copy(
                    gslots[gsl][:, 0:w], cvb[:, g0c:g0c + w]), waits=wt2, inc=("dve", 1))
                state["g_slot_st"][gsl] = dma("sync", Dgsp.ap()[oc * 128:(oc + 1) * 128, g0c:g0c + w],
                                              gslots[gsl][:, 0:w], waits=[("dve", S.val("dve"))],
                                              stream=f"stg{gsl}")
            state["oc_done_dve"] = S.val("dve")
            state["last_g_st"] = state["g_slot_st"][1] or state["g_slot_st"][0]
            return
        cg = {"q": 0, "k": 8, "v": 16}[kind] + oc

        def conv(eng, cg=cg):
            cw = cst["convw"]
            eng.tensor_scalar(cvb[:, 0:ntok], raw[:, 3:3 + ntok], cw[:, cg * 4 + 3:cg * 4 + 4], None, ALU.mult)
            for j in (2, 1, 0):
                last = eng.scalar_tensor_tensor(cvb[:, 0:ntok], raw[:, j:j + ntok],
                                                cw[:, cg * 4 + j:cg * 4 + j + 1], cvb[:, 0:ntok],
                                                ALU.mult, ALU.add)
            return last
        wt_conv = [("dve", rawv), ("act", state["oc_done_act"]), ld_consts]
        if state.get("last_g_st"):
            wt_conv.append(state["last_g_st"])
        S.emit("vector", conv, waits=wt_conv, inc=("dve", 1))
        state["oc_done_pool"] = S.val("pool")
        S.emit("scalar", lambda eng: eng.activation(cvb[:, 0:ntok], cvb[:, 0:ntok], AF.Silu),
               waits=[("dve", S.val("dve"))], inc=("act", 1))
        silv = S.val("act")
        state["oc_done_act"] = silv

        if kind == "v":
            for cch in range(nch):
                slot = state["vslot"] % 4
                state["vslot"] += 1
                bank = slot % 2
                pcol = bank * 1024 + (slot // 2) * 128
                S.emit("vector", lambda eng, cch=cch, slot=slot: eng.tensor_copy(
                    bfnt4[:, slot * 128:(slot + 1) * 128], cvb[:, cch * 128:(cch + 1) * 128]),
                    waits=[("act", silv), ("pe", state["vtp_pe"][slot])], inc=("dve", 1))
                vv = S.val("dve")
                wt = [("dve", vv), ("dve", state["b01_dve"][0]), ("dve", state["b01_dve"][1]),
                      ("dve", state["vbank_dve"][bank])]
                S.emit("tensor", lambda eng, slot=slot, pcol=pcol: eng.transpose(
                    pbf[:, pcol:pcol + 128],
                    bfnt4[:, slot * 128:(slot + 1) * 128], cst["identbf"][:, :]),
                    waits=wt, inc=("pe", 1))
                state["vtp_pe"][slot] = S.val("pe")
                wt2 = [("pe", S.val("pe"))]
                if len(state["vst"]) >= 4:
                    wt2.append(state["vst"][-4])
                S.emit("vector", lambda eng, slot=slot, pcol=pcol: eng.tensor_copy(
                    bf4[:, slot * 128:(slot + 1) * 128], pbf[:, pcol:pcol + 128]),
                    waits=wt2, inc=("dve", 1))
                state["vbank_dve"][bank] = S.val("dve")
                stv = dma("sync", Dvsp.ap()[(cch * 16 + oc) * 128:(cch * 16 + oc + 1) * 128, :],
                          bf4[:, slot * 128:(slot + 1) * 128], waits=[("dve", S.val("dve"))],
                          stream=f"stv{slot}")
                state["vst"].append(stv)
            state["oc_done_dve"] = S.val("dve")
            return
        lastn = 0
        for gi2, g0c in enumerate(range(0, ntok, 512)):
            w = min(512, ntok - g0c)
            sl2 = gi2 % 2
            S.emit("scalar", lambda eng, g0c=g0c, w=w, sl2=sl2: eng.activation(
                sqb[:, sl2 * 512:sl2 * 512 + w], cvb[:, g0c:g0c + w], AF.Square),
                waits=[("act", silv), ("pe", state["sq_pe"][sl2])], inc=("act", 1))
            vsq = S.val("act")
            S.emit("tensor", lambda eng, w=w, sl2=sl2: eng.matmul(pb[2][0:2, 0:w], cst["headmask"][:, :],
                                                                  sqb[:, sl2 * 512:sl2 * 512 + w],
                                                                  start=True, stop=True),
                   waits=[("act", vsq), ("dve", lastn)], inc=("pe", 1))
            state["sq_pe"][sl2] = S.val("pe")
            S.emit("vector", lambda eng, g0c=g0c, w=w: eng.tensor_copy(nrm[0:2, g0c:g0c + w], pb[2][0:2, 0:w]),
                   waits=[("pe", S.val("pe"))], inc=("dve", 1))
            lastn = S.val("dve")
        state["oc_done_act"] = S.val("act")
        S.emit("scalar", lambda eng: eng.activation(nrm[0:2, 0:ntok], nrm[0:2, 0:ntok], AF.Sqrt,
                                                    bias=cvals[0:2, 1:2]),
               waits=[("dve", lastn), ("dve", cinit_v)], inc=("act", 1))
        S.emit("vector", lambda eng: eng.reciprocal(nrm[0:2, 0:ntok], nrm[0:2, 0:ntok]),
               waits=[("act", S.val("act"))], inc=("dve", 1))
        vrs = S.val("dve")
        state["oc_done_act"] = S.val("act")
        maskT = cst["maskT_q"] if kind == "q" else cst["maskT_k"]
        dst_d = Dq if kind == "q" else Dk
        lastq = 0
        for gi2, g0c in enumerate(range(0, ntok, 512)):
            w = min(512, ntok - g0c)
            sl2 = gi2 % 2
            S.emit("tensor", lambda eng, g0c=g0c, w=w, maskT=maskT: eng.matmul(
                pb[3][:, 0:w], maskT[0:2, :], nrm[0:2, g0c:g0c + w], start=True, stop=True),
                waits=[("dve", vrs), ("dve", lastq)], inc=("pe", 1))
            wq2 = [("pe", S.val("pe"))]
            if state["qk_st"][sl2] is not None:
                wq2.append(state["qk_st"][sl2])
            S.emit("vector", lambda eng, g0c=g0c, w=w, sl2=sl2: eng.tensor_tensor(
                qst[:, sl2 * 512:sl2 * 512 + w], cvb[:, g0c:g0c + w], pb[3][:, 0:w], ALU.mult),
                waits=wq2, inc=("dve", 1))
            lastq = S.val("dve")
            state["qk_st"][sl2] = dma("sync", dst_d.ap()[oc * 128:(oc + 1) * 128, g0c:g0c + w],
                                      qst[:, sl2 * 512:sl2 * 512 + w], waits=[("dve", lastq)],
                                      stream=f"stqk{sl2}")
        state["oc_done_dve"] = lastq

    for oc in range(8):
        proj_oc("q", oc)
    if maxphase >= 3:
        for oc in range(8):
            proj_oc("k", oc)
    if maxphase >= 4:
        for oc in range(16):
            proj_oc("v", oc)
    if maxphase >= 5:
        for oc in range(16):
            proj_oc("g", oc)
    def qk_taps():
        for nm, dd in (("q_fm", Dq), ("k_fm", Dk)):
            if nm in taps:
                dma("sync", taps[nm][:, 0:8 * ntok].rearrange("p (g t) -> p g t", g=8),
                    dd.ap().rearrange("(g p) t -> p g t", p=128), stream="sttap")

    if maxphase < 6:
        A.free(gB1)
        A.free(gB0)
        S.barrier()
        qk_taps()
        if "vsp" in taps and maxphase >= 4:
            dma("sync", taps["vsp"], Dvsp.ap(), stream="sttap")
        if "gsp" in taps and maxphase >= 5:
            dma("sync", taps["gsp"], Dgsp.ap(), stream="sttap")
        _finish(nc, S, O, ntok)
        return nc, taps

    A.free(gB1)
    gBab = "phA2ab"
    abh = A.sb(gBab, "abh", [16, ntok], F32)
    abh2 = A.sb(gBab, "abh2", [16, ntok], F32)
    lnb = A.sb(gBab, "lnb", [16, ntok], F32)
    sc16 = A.sb(gBab, "sc16", [16, ntok], F32)
    gb1_pend = [e for e in state["qk_st"] + state["g_slot_st"] if e] + state["vst"][-4:]

    src = I["abT"].rearrange("(ic p) o -> p ic o", p=128)
    wsl_ab = state["wslot"] % 2
    ldab = dma("gpsimd", wbf[:, wsl_ab * 2048: wsl_ab * 2048 + 512].rearrange("p (ic o) -> p ic o", ic=16),
               src, waits=[("pe", state["wslot_pe"][wsl_ab])], sem=f"wfg{wsl_ab}")
    lastab = state["oc_done_dve"]
    for (wcol0, dst) in ((0, abh), (16, abh2)):
        for g0c in range(0, ntok, 512):
            w = min(512, ntok - g0c)
            def mm(eng, g0c=g0c, w=w, wcol0=wcol0, wsl_ab=wsl_ab):
                last = None
                for ic in range(16):
                    last = eng.matmul(pb[4][0:16, 0:w],
                                      wbf[:, wsl_ab * 2048 + ic * 32 + wcol0: wsl_ab * 2048 + ic * 32 + wcol0 + 16],
                                      h_fm[:, ic * NTP + 3 + g0c: ic * NTP + 3 + g0c + w],
                                      start=(ic == 0), stop=(ic == 15))
                return last
            S.emit("tensor", mm, waits=[ldab, ("dve", lastab)], inc=("pe", 1))
            S.emit("vector", lambda eng, g0c=g0c, w=w, dst=dst: eng.tensor_copy(dst[0:16, g0c:g0c + w],
                                                                                pb[4][0:16, 0:w]),
                   waits=[("pe", S.val("pe"))] + gb1_pend, inc=("dve", 1))
            gb1_pend = []
            lastab = S.val("dve")

    S.emit("vector", lambda eng: eng.tensor_add(smallv[0:16, 0:1], cst["b_bias"][0:16, 0:1],
                                                cst["beta_bias"][0:16, 0:1]),
           waits=[ld_consts, ("dve", lastab)], inc=("dve", 1))
    S.emit("scalar", lambda eng: eng.activation(abh2[0:16, 0:ntok], abh2[0:16, 0:ntok], AF.Sigmoid,
                                                bias=smallv[0:16, 0:1]),
           waits=[("dve", S.val("dve"))], inc=("act", 1))
    S.emit("scalar", lambda eng: eng.activation(lnb[0:16, 0:ntok], abh2[0:16, 0:ntok], AF.Ln, scale=2.0),
           inc=("act", 1))
    sig_act = S.val("act")
    S.emit("scalar", lambda eng: eng.activation(smallv[0:16, 1:2], cst["A_log"][0:16, 0:1], AF.Exp),
           waits=[ld_consts], inc=("act", 1))
    S.emit("vector", lambda eng: eng.tensor_scalar(smallv[0:16, 1:2], smallv[0:16, 1:2], -1.0, None, ALU.mult),
           waits=[("act", S.val("act"))], inc=("dve", 1))
    negA_v = S.val("dve")

    def gpre(eng):
        eng.tensor_scalar(abh[0:16, 0:ntok], abh[0:16, 0:ntok], cst["dt_bias"][0:16, 0:1], None, ALU.add)
        eng.tensor_scalar(sc16[0:16, 0:ntok], abh[0:16, 0:ntok], -1.0, None, ALU.mult)
        return eng.tensor_tensor(sc16[0:16, 0:ntok], sc16[0:16, 0:ntok], abh[0:16, 0:ntok], ALU.max)
    S.emit("vector", gpre, waits=[("dve", negA_v), ("act", sig_act)], inc=("dve", 1))
    S.emit("scalar", lambda eng: eng.activation(sc16[0:16, 0:ntok], sc16[0:16, 0:ntok], AF.Exp, scale=-1.0),
           waits=[("dve", S.val("dve"))], inc=("act", 1))
    S.emit("scalar", lambda eng: eng.activation(sc16[0:16, 0:ntok], sc16[0:16, 0:ntok], AF.Ln, bias=1.0),
           inc=("act", 1))
    lnact = S.val("act")

    def gfin(eng):
        eng.tensor_scalar(abh[0:16, 0:ntok], abh[0:16, 0:ntok], 0.0, None, ALU.max)
        eng.tensor_add(abh[0:16, 0:ntok], abh[0:16, 0:ntok], sc16[0:16, 0:ntok])
        return eng.tensor_scalar(abh[0:16, 0:ntok], abh[0:16, 0:ntok], smallv[0:16, 1:2], None, ALU.mult)
    S.emit("vector", gfin, waits=[("act", lnact)], inc=("dve", 1))

    def scans(eng):
        last = None
        for cch in range(nch):
            sl = slice(cch * L, (cch + 1) * L)
            last = eng.tensor_tensor_scan(hm_cum[0:16, sl], cst["onesrow"][0:16, 0:L],
                                          abh[0:16, sl], 0.0, ALU.mult, ALU.add)
        return last
    S.emit("vector", scans, inc=("dve", 1))
    S.emit("vector", lambda eng: eng.tensor_add(hm_cumb[0:16, 0:ntok], hm_cum[0:16, 0:ntok], lnb[0:16, 0:ntok]),
           inc=("dve", 1))
    S.emit("scalar", lambda eng: eng.activation(hm_expc[0:16, 0:ntok], hm_cum[0:16, 0:ntok], AF.Exp),
           waits=[("dve", S.val("dve"))], inc=("act", 1))
    S.emit("vector", lambda eng: eng.tensor_scalar(abh2[0:16, 0:ntok], abh2[0:16, 0:ntok], 2.0, None, ALU.mult),
           waits=[("act", S.val("act"))], inc=("dve", 1))
    S.emit("vector", lambda eng: eng.tensor_mul(lnb[0:16, 0:ntok], abh2[0:16, 0:ntok], hm_expc[0:16, 0:ntok]),
           inc=("dve", 1))
    bg_v = S.val("dve")

    prev_tp = 0
    for cch in range(nch):
        sl = slice(cch * L, (cch + 1) * L)
        S.emit("vector", lambda eng, cch=cch, sl=sl: eng.tensor_scalar(
            sc16[0:16, 0:L], hm_cum[0:16, sl], -1.0,
            hm_cum[0:16, cch * L + L - 1:cch * L + L], ALU.mult, ALU.add),
            waits=[("dve", bg_v), ("pe", prev_tp)], inc=("dve", 1))
        S.emit("scalar", lambda eng: eng.activation(sc16[0:16, 0:L], sc16[0:16, 0:L], AF.Exp),
               waits=[("dve", S.val("dve"))], inc=("act", 1))
        kbs_act = S.val("act")

        def tpose(eng, cch=cch, sl=sl):
            eng.transpose(pb[5][0:128, 0:16], hm_cum[0:16, sl], cst["ident"][0:16, 0:16])
            eng.transpose(pb[5][0:128, 16:32], abh2[0:16, sl], cst["ident"][0:16, 0:16])
            eng.transpose(pb[5][0:128, 32:48], lnb[0:16, sl], cst["ident"][0:16, 0:16])
            eng.transpose(pb[5][0:128, 48:64], sc16[0:16, 0:L], cst["ident"][0:16, 0:16])
            eng.transpose(pb[5][0:128, 80:96], hm_cumb[0:16, sl], cst["ident"][0:16, 0:16])
            return eng.transpose(pb[5][0:1, 64:80], hm_expc[0:16, cch * L + L - 1:cch * L + L],
                                 cst["ident"][0:16, 0:16])
        S.emit("tensor", tpose, waits=[("act", kbs_act)], inc=("pe", 1))
        prev_tp = S.val("pe")

        def tevac(eng, cch=cch):
            eng.tensor_copy(st_cum[:, cch * 16:(cch + 1) * 16], pb[5][0:128, 0:16])
            eng.tensor_copy(st_beta[:, cch * 16:(cch + 1) * 16], pb[5][0:128, 16:32])
            eng.tensor_copy(st_bg[:, cch * 16:(cch + 1) * 16], pb[5][0:128, 32:48])
            eng.tensor_copy(st_kbs[:, cch * 16:(cch + 1) * 16], pb[5][0:128, 48:64])
            eng.tensor_copy(st_cumb[:, cch * 16:(cch + 1) * 16], pb[5][0:128, 80:96])
            return eng.tensor_copy(gl_row[0:1, cch * 16:(cch + 1) * 16], pb[5][0:1, 64:80])
        S.emit("vector", tevac, waits=[("pe", prev_tp)], inc=("dve", 1))

    A.free(gBab)
    A.free(gB0)
    A.free("hfm")
    barA2 = S.barrier()
    qk_taps()
    for nm, src_, cols in (("cum", st_cum, nch * 16), ("beta", st_beta, nch * 16),
                           ("bg", st_bg, nch * 16), ("kbs", st_kbs, nch * 16)):
        if nm in taps:
            dma("sync", taps[nm][:, 0:cols], src_[:, 0:cols], stream="sttap")
    if "glrow" in taps:
        dma("sync", taps["glrow"], gl_row[0:1, :], stream="sttap")
    if "vsp" in taps:
        dma("sync", taps["vsp"], Dvsp.ap(), stream="sttap")
    if "gsp" in taps:
        dma("sync", taps["gsp"], Dgsp.ap(), stream="sttap")

    if maxphase < 7:
        _finish(nc, S, O, ntok)
        return nc, taps

    S.phase("p1")
    Dgl = nc.dram_tensor("Dgl", [1, nch * 16], F32)
    glst = dma("sync", Dgl.ap(), gl_row[0:1, :], stream="stgl")
    dma("sync", glb[:, :], Dgl.ap().partition_broadcast(128), waits=[glst], stream="mbc")
    glb_ld = ("q_mbc", S.val("q_mbc"))

    Dyc = nc.dram_tensor("Dyc", [nch * 16 * 64, 128], BF16)
    g1 = "p1"
    vtile = A.sb(g1, "vtile", [128, 2 * 16 * 128], BF16)
    em4 = A.sb(g1, "em4", [128, 2 * 512], F32)
    e4 = A.sb(g1, "e4", [128, 2 * 512], F32)
    abar4d = A.sb(g1, "abar4", [128, 2 * 512], BF16)
    a4d = A.sb(g1, "a4", [128, 2 * 512], BF16)
    bbar4d = A.sb(g1, "bbar4", [128, 2 * 512], BF16)
    qg4d = A.sb(g1, "qg4", [128, 2 * 512], F32)
    abuf = A.sb(g1, "abuf", [128, 2 * 6 * 512], BF16)
    apow = A.sb(g1, "apow", [128, 2 * 2 * 512], BF16)
    xab = A.sb(g1, "xab", [128, 2 * 2 * 768], BF16)
    kbt4 = A.sb(g1, "kbt4", [128, 2 * 256], BF16)
    pc4 = A.sb(g1, "pc4", [64, 2 * 256], BF16)
    Ycs4 = A.sb(g1, "Ycs4", [64, 2 * 512], BF16)
    negc4 = A.sb(g1, "negc4", [64, 2 * 256], BF16)
    oust4 = A.sb(g1, "oust4", [128, 2 * 512], BF16)
    ztst4 = A.sb(g1, "ztst4", [64, 2 * 512], BF16)
    pqacc = A.sb(g1, "pqacc", [64, 16 * 192], BF16)
    pqst = A.sb(g1, "pqst", [64, 2 * 192], BF16)
    qtl = A.sb(g1, "qtl", [128, 2 * 1024], BF16)
    ktl = A.sb(g1, "ktl", [128, 2 * 1024], BF16)

    pq_ev = {}
    def pqinit(eng):
        eng.memset(pqacc[0:64, :], 0.0)
        last = None
        for hh in range(16):
            last = eng.tensor_copy(pqacc[0:64, hh * 192: hh * 192 + 64], cst["identbf"][0:64, 0:64])
        return last
    S.emit("vector", pqinit, waits=[ld_consts], inc=("dve", 1))
    for hh in range(16):
        pq_ev[hh] = ("dve", S.val("dve"))

    bg_guard = {i: {} for i in range(6)}
    bg_guard["bfA"] = {}
    bg_guard["bfB"] = {}
    evac_flip = [0]

    def evac_engine():
        evac_flip[0] ^= 1
        return "vector" if evac_flip[0] else "scalar"

    def eng_sem(e):
        return "dve" if e == "vector" else "act"

    def pe_mm(fn, extra_waits, bank_keys):
        wl = list(extra_waits)
        for b in bank_keys:
            for snm, v in bg_guard[b].items():
                if v > 0:
                    wl.append((snm, v))
        S.emit("tensor", fn, waits=wl, inc=("pe", 1))
        return S.val("pe")

    def evac(fn, pev, bank_keys, extra_waits=(), scalar_ok=True):
        e = evac_engine() if scalar_ok else "vector"
        S.emit(e, fn, waits=[("pe", pev)] + list(extra_waits), inc=(eng_sem(e), 1))
        v = (eng_sem(e), S.val(eng_sem(e)))
        for b in bank_keys:
            bg_guard[b][v[0]] = v[1]
        return v

    vld_prev = [None, None]
    qk_cons = [None, None]
    styc4 = [None, None]
    stou4 = [None, None]
    stzt4 = [None, None]
    stnt4 = [None, None]
    nt4_dve = [0, 0]
    yc4_pe = [0, 0]

    def kf_ap(hh, cc):
        oc, half = hh // 2, (hh % 2) * 64
        base = (cc % 2) * 1024 + oc * L
        return ktl[half:half + 64, base:base + L]

    def qf_ap(hh, cc):
        oc, half = hh // 2, (hh % 2) * 64
        base = (cc % 2) * 1024 + oc * L
        return qtl[half:half + 64, base:base + L]

    def grp_stages(cc, grp, st, ldk, ldv):
        import os as _os
        HLP = HL_PARTS if (HL_PARTS2 is None or grp % 2 == 0) else HL_PARTS2
        par = cc % 2
        heads = list(range(grp * 4, grp * 4 + 4))
        ch0 = cc * 16 + heads[0]
        b0, b1, b2 = (0, 1, 2) if st == 0 else (3, 4, 5)
        bfk = "bfA" if st == 0 else "bfB"
        pfb = st * 1024
        sb = st * 512
        em = em4[:, sb:sb + 512]
        ex = e4[:, sb:sb + 512]
        abar4 = abar4d[:, sb:sb + 512]
        a4 = a4d[:, sb:sb + 512]
        bbar4 = bbar4d[:, sb:sb + 512]
        qg4 = qg4d[:, sb:sb + 512]
        abufs = abuf[:, st * 3072: (st + 1) * 3072]
        apows = apow[:, st * 1024: (st + 1) * 1024]
        xs = [xab[:, (st * 2 + j) * 768: (st * 2 + j + 1) * 768] for j in range(2)]
        kb4 = kbt4[:, st * 256:(st + 1) * 256]
        pcs = pc4[0:64, st * 256:(st + 1) * 256]

        fence = [("dve", S.val("dve")), ("act", S.val("act"))] if _os.environ.get("P1_FENCE") else []
        for i, hh in enumerate(heads):
            pv_g = pe_mm(lambda eng, i=i, hh=hh: eng.matmul(pb[b0][:, i * 128:(i + 1) * 128],
                                                            kf_ap(hh, cc), kf_ap(hh, cc),
                                                            start=True, stop=True),
                         [ldk] + fence if i == 0 else [ldk], [b0])
        for i, hh in enumerate(heads):
            pv_gq = pe_mm(lambda eng, i=i, hh=hh: eng.matmul(pb[b1][:, i * 128:(i + 1) * 128],
                                                             kf_ap(hh, cc), qf_ap(hh, cc),
                                                             start=True, stop=True), [ldk], [b1])
        for i, hh in enumerate(heads):
            pv_b = pe_mm(lambda eng, i=i, hh=hh: eng.matmul(pb[b2][:, i * 128:(i + 1) * 128],
                                                            cst["ohsel"][0:16, hh * 128:(hh + 1) * 128],
                                                            hm_cumb[0:16, cc * L:(cc + 1) * L],
                                                            start=True, stop=True), [], [b2])
        yield
        if "bld1" not in HLP:
            return

        def bld1(eng):
            last = None
            for i, hh in enumerate(heads):
                last = eng.scalar_tensor_tensor(em[:, i * 128:(i + 1) * 128],
                                                pb[b2][:, i * 128:(i + 1) * 128],
                                                st_cum[:, cc * 16 + hh: cc * 16 + hh + 1],
                                                cst["mneg_s4"][:, i * 128:(i + 1) * 128],
                                                ALU.subtract, ALU.add)
            return last
        em_v = evac(bld1, pv_b, [b2], scalar_ok=False)
        S.emit("scalar", lambda eng: eng.activation(ex[:, :], em[:, :], AF.Exp),
               waits=[em_v], inc=("act", 1))
        S.emit("vector", lambda eng: eng.scalar_tensor_tensor(abar4[:, :], pb[b0][:, :], 1.0,
                                                              ex[:, :], ALU.mult, ALU.mult),
               waits=[("act", S.val("act")), ("pe", pv_g)], inc=("dve", 1))
        abar_bld = ("dve", S.val("dve"))
        yield
        if "bld2" not in HLP:
            return

        for i, hh in enumerate(heads):
            pv_b2 = pe_mm(lambda eng, i=i, hh=hh: eng.matmul(pb[b2][:, i * 128:(i + 1) * 128],
                                                             cst["ohsel"][0:16, hh * 128:(hh + 1) * 128],
                                                             hm_cum[0:16, cc * L:(cc + 1) * L],
                                                             start=True, stop=True),
                          [em_v] if i == 0 else [], [b2] if i == 0 else [])

        def bld2(eng):
            for i, hh in enumerate(heads):
                eng.tensor_scalar(em[:, i * 128:(i + 1) * 128], pb[b2][:, i * 128:(i + 1) * 128],
                                  -1.0, st_cumb[:, cc * 16 + hh: cc * 16 + hh + 1],
                                  ALU.mult, ALU.add)
            return eng.tensor_add(em[:, :], em[:, :], cst["mneg_sT4"][:, :])
        em_v2 = evac(bld2, pv_b2, [], extra_waits=[abar_bld], scalar_ok=False)
        S.emit("scalar", lambda eng: eng.activation(ex[:, :], em[:, :], AF.Exp),
               waits=[em_v2], inc=("act", 1))
        S.emit("vector", lambda eng: eng.scalar_tensor_tensor(a4[:, :], pb[b0][:, :], 1.0, ex[:, :],
                                                              ALU.mult, ALU.mult),
               waits=[("act", S.val("act"))], inc=("dve", 1))
        bg_guard[b0]["dve"] = S.val("dve")

        def bld3(eng):
            last = None
            for i, hh in enumerate(heads):
                last = eng.scalar_tensor_tensor(em[:, i * 128:(i + 1) * 128],
                                                pb[b2][:, i * 128:(i + 1) * 128],
                                                st_cum[:, cc * 16 + hh: cc * 16 + hh + 1],
                                                cst["mneg_i4"][:, i * 128:(i + 1) * 128],
                                                ALU.subtract, ALU.add)
            return last
        S.emit("vector", bld3, waits=[], inc=("dve", 1))
        em_v3 = ("dve", S.val("dve"))
        bg_guard[b2]["dve"] = S.val("dve")
        S.emit("scalar", lambda eng: eng.activation(ex[:, :], em[:, :], AF.Exp),
               waits=[em_v3], inc=("act", 1))
        S.emit("vector", lambda eng: eng.scalar_tensor_tensor(bbar4[:, :], pb[b1][:, :], 1.0, ex[:, :],
                                                              ALU.mult, ALU.mult),
               waits=[("act", S.val("act")), ("pe", pv_gq)], inc=("dve", 1))
        bg_guard[b1]["dve"] = S.val("dve")
        yield
        if "bld4" not in HLP:
            return

        for i, hh in enumerate(heads):
            pv_b3 = pe_mm(lambda eng, i=i, hh=hh: eng.matmul(pb[b2][:, i * 128:(i + 1) * 128],
                                                             cst["ohsel"][0:16, hh * 128:(hh + 1) * 128],
                                                             hm_expc[0:16, cc * L:(cc + 1) * L],
                                                             start=True, stop=True),
                          [], [b2] if i == 0 else [])

        def bld4(eng):
            last = None
            for i, hh in enumerate(heads):
                half = (hh % 2) * 64
                last = eng.tensor_tensor(qg4[half:half + 64, i * 128:(i + 1) * 128],
                                         qf_ap(hh, cc), pb[b2][half:half + 64, i * 128:(i + 1) * 128],
                                         ALU.mult)
            return last
        qg_v = evac(bld4, pv_b3, [b2], scalar_ok=False)
        if "ktr" not in HLP:
            return

        for j in range(2):
            hh = heads[j * 2]
            oc = hh // 2
            base = (cc % 2) * 1024 + oc * L
            pv_kt = pe_mm(lambda eng, j=j, base=base: eng.transpose(
                pbf[:, pfb + j * 128: pfb + (j + 1) * 128], ktl[:, base:base + L],
                cst["identbf"][:, :]),
                [], [bfk] if j == 0 else [])
        yield
        if "x0" not in HLP:
            return

        def x0w(eng):
            last = None
            for i, hh in enumerate(heads):
                ch = cc * 16 + hh
                pcol = pfb + (i // 2) * 128 + (hh % 2) * 64
                eng.tensor_scalar(kb4[:, i * 64:(i + 1) * 64], pbf[:, pcol:pcol + 64],
                                  st_kbs[:, ch:ch + 1], None, ALU.mult)
                eng.tensor_scalar(xs[0][:, i * 192:i * 192 + 128],
                                  vtile[:, par * 2048 + hh * 128: par * 2048 + (hh + 1) * 128],
                                  st_beta[:, ch:ch + 1], None, ALU.mult)
                last = eng.tensor_scalar(xs[0][:, i * 192 + 128:(i + 1) * 192],
                                         pbf[:, pcol:pcol + 64],
                                         st_bg[:, ch:ch + 1], None, ALU.mult)
            return last
        x_ev = evac(x0w, pv_kt, [bfk], extra_waits=[ldv], scalar_ok=False)
        yield
        if "powers" not in HLP:
            return

        acur_sl = [a4[:, i * 128:(i + 1) * 128] for i in range(4)]
        abcur_sl = [abar4[:, i * 128:(i + 1) * 128] for i in range(4)]
        acur_ev = ("dve", 0)
        abar_ev = ("dve", 0)
        for lev in range(6):
            for i in range(4):
                pvA = pe_mm(lambda eng, i=i, a=acur_sl[i], ab=abcur_sl[i]: eng.matmul(
                    pb[b0][:, i * 128:(i + 1) * 128], a, ab, start=True, stop=True),
                    [acur_ev, abar_ev] if i == 0 else [], [b0] if i == 0 else [])
            if lev < 5:
                for i in range(4):
                    pvB = pe_mm(lambda eng, i=i, a=acur_sl[i], ab=abcur_sl[i]: eng.matmul(
                        pb[b1][:, i * 128:(i + 1) * 128], ab, a, start=True, stop=True),
                        [acur_ev, abar_ev] if i == 0 else [], [b1] if i == 0 else [])
            dstA = abufs[:, lev * 512:(lev + 1) * 512]
            abar_ev = evac(lambda eng, dstA=dstA: eng.xcopy(dstA, pb[b0][:, :]), pvA, [b0])
            abcur_sl = [dstA[:, i * 128:(i + 1) * 128] for i in range(4)]
            if lev < 5:
                dstB = apows[:, (lev % 2) * 512:(lev % 2 + 1) * 512]
                acur_ev = evac(lambda eng, dstB=dstB: eng.xcopy(dstB, pb[b1][:, :]), pvB, [b1])
                acur_sl = [dstB[:, i * 128:(i + 1) * 128] for i in range(4)]
            yield

        if "applies" not in HLP:
            return
        xcur, xnext = xs[0], xs[1]
        xev = x_ev
        for j, lev in enumerate([6, 5, 4, 3, 2, 1, 0]):
            if lev == 0:
                lhs = [abar4[:, i * 128:(i + 1) * 128] for i in range(4)]
            else:
                lhs = [abufs[:, (lev - 1) * 512 + i * 128:(lev - 1) * 512 + (i + 1) * 128]
                       for i in range(4)]
            sgn = -1.0 if lev == 0 else 1.0

            for k in range(4):
                bk = b0 if k < 2 else b1
                col = (k % 2) * 192
                pv_ = pe_mm(lambda eng, lhs=lhs[k], xc=xcur, k=k, bk=bk, col=col: eng.matmul(
                    pb[bk][:, col:col + 192], lhs, xc[:, k * 192:(k + 1) * 192],
                    start=True, stop=True),
                    [xev, abar_ev] if k in (0, 2) else [], [bk] if k in (0, 2) else [])
                if k == 1:
                    pvA = pv_
            pvB = pv_

            def appev(eng, xn=xnext, xc=xcur, sgn=sgn):
                eng.scalar_tensor_tensor(xn[:, 0:384], pb[b0][:, 0:384], sgn, xc[:, 0:384],
                                         ALU.mult, ALU.add)
                return eng.scalar_tensor_tensor(xn[:, 384:768], pb[b1][:, 0:384], sgn, xc[:, 384:768],
                                                ALU.mult, ALU.add)
            S.emit("vector", appev, waits=[("pe", pvB)], inc=("dve", 1))
            xev = ("dve", S.val("dve"))
            bg_guard[b0]["dve"] = S.val("dve")
            bg_guard[b1]["dve"] = S.val("dve")
            xcur, xnext = xnext, xcur
            yield

        if "post" not in HLP:
            return
        for i in range(4):
            pvY = pe_mm(lambda eng, i=i, xc=xcur: eng.matmul(
                pb[b0][0:64, i * 128:(i + 1) * 128], kb4[:, i * 64:(i + 1) * 64],
                xc[:, i * 192:i * 192 + 128], start=True, stop=True),
                [xev] if i == 0 else [], [b0] if i == 0 else [])
        for i in range(4):
            pvN = pe_mm(lambda eng, i=i, xc=xcur: eng.matmul(
                pb[b1][0:64, i * 64:(i + 1) * 64], xc[:, i * 192 + 128:(i + 1) * 192],
                kb4[:, i * 64:(i + 1) * 64], start=True, stop=True),
                [xev] if i == 0 else [], [b1] if i == 0 else [])
        wyc = [] if styc4[st] is None else [styc4[st]]
        if yc4_pe[st]:
            wyc.append(("pe", yc4_pe[st]))
        yc_ev = evac(lambda eng: eng.xcopy(Ycs4[0:64, st * 512:(st + 1) * 512], pb[b0][0:64, 0:512]),
                     pvY, [b0], extra_waits=wyc)
        wnt = [] if stnt4[st] is None else [stnt4[st]]
        if nt4_dve[st]:
            wnt.append(("dve", nt4_dve[st]))
        nt_ev = evac(lambda eng: eng.xmul(negc4[0:64, st * 256:(st + 1) * 256], pb[b1][0:64, 0:256],
                                          -1.0), pvN, [b1], extra_waits=wnt)
        styc4[st] = dma("sync", Dyc.ap()[ch0 * 64:ch0 * 64 + 256, :].rearrange("(h p) t -> p h t", p=64),
                        Ycs4[0:64, st * 512:(st + 1) * 512].rearrange("p (h t) -> p h t", h=4),
                        waits=[yc_ev], stream=f"styc{st}")
        stnt4[st] = dma("sync", DnegT.ap()[ch0 * 64:ch0 * 64 + 256, :].rearrange("(h p) t -> p h t", p=64),
                        negc4[0:64, st * 256:(st + 1) * 256].rearrange("p (h t) -> p h t", h=4),
                        waits=[nt_ev], stream=f"stnt{st}")
        yield

        for i in range(4):
            pvO = pe_mm(lambda eng, i=i, xc=xcur: eng.matmul(
                pb[b0][:, i * 128:(i + 1) * 128], xc[:, i * 192:i * 192 + 128],
                bbar4[:, i * 128:(i + 1) * 128], start=True, stop=True),
                [xev] if i == 0 else [], [b0] if i == 0 else [])
        for i in range(4):
            pvZ = pe_mm(lambda eng, i=i, xc=xcur: eng.matmul(
                pb[b1][0:64, i * 128:(i + 1) * 128], xc[:, i * 192 + 128:(i + 1) * 192],
                bbar4[:, i * 128:(i + 1) * 128], start=True, stop=True),
                [xev] if i == 0 else [], [b1] if i == 0 else [])
        wou = [] if stou4[st] is None else [stou4[st]]
        ou_ev = evac(lambda eng: eng.xcopy(oust4[:, st * 512:(st + 1) * 512], pb[b0][:, 0:512]),
                     pvO, [b0], extra_waits=wou)

        def ztev(eng):
            last = None
            for i, hh in enumerate(heads):
                half = (hh % 2) * 64
                last = eng.scalar_tensor_tensor(ztst4[0:64, st * 512 + i * 128: st * 512 + (i + 1) * 128],
                                                pb[b1][0:64, i * 128:(i + 1) * 128], -1.0,
                                                qg4[half:half + 64, i * 128:(i + 1) * 128],
                                                ALU.mult, ALU.add)
            return last
        wzt = [] if stzt4[st] is None else [stzt4[st]]
        zt_ev = evac(ztev, pvZ, [b1], extra_waits=wzt, scalar_ok=False)
        stou4[st] = dma("sync", Dou.ap()[ch0 * 128:ch0 * 128 + 512, :].rearrange("(h p) t -> p h t", p=128),
                        oust4[:, st * 512:(st + 1) * 512].rearrange("p (h t) -> p h t", h=4),
                        waits=[ou_ev], stream=f"stou{st}")
        stzt4[st] = dma("sync", Dzt.ap()[ch0 * 64:ch0 * 64 + 256, :].rearrange("(h p) t -> p h t", p=64),
                        ztst4[0:64, st * 512:(st + 1) * 512].rearrange("p (h t) -> p h t", h=4),
                        waits=[zt_ev], stream=f"stzt{st}")
        yield

        if "compose" not in HLP:
            return
        def pcb(eng):
            last = None
            for i, hh in enumerate(heads):
                ch = cc * 16 + hh
                eng.tensor_scalar(pcs[:, i * 64:(i + 1) * 64], cst["identbf"][0:64, 0:64],
                                  glb[0:64, ch:ch + 1], None, ALU.mult)
                last = eng.tensor_tensor(pcs[:, i * 64:(i + 1) * 64], pcs[:, i * 64:(i + 1) * 64],
                                         negc4[0:64, st * 256 + i * 64: st * 256 + (i + 1) * 64], ALU.add)
            return last
        wpc = [glb_ld, nt_ev, ("pe", S.val("pe"))]
        S.emit("vector", pcb, waits=wpc, inc=("dve", 1))
        pcv = ("dve", S.val("dve"))
        nt4_dve[st] = S.val("dve")
        for half2 in range(2):
            i0_, i1_ = half2 * 2, half2 * 2 + 1

            for i in (i0_, i1_):
                def cmps1(eng, i=i, base=(i - i0_) * 192, hh=heads[i]):
                    eng.matmul(pb[b2][0:64, base:base + 192], pcs[:, i * 64:(i + 1) * 64],
                               pqacc[0:64, hh * 192:(hh + 1) * 192], start=True, stop=False)
                    return eng.matmul(pb[b2][0:64, base + 64:base + 192], cst["identbf"][0:64, 0:64],
                                      Ycs4[0:64, st * 512 + i * 128: st * 512 + (i + 1) * 128],
                                      start=False, stop=True)
                pvC = pe_mm(cmps1,
                            [pcv, yc_ev, pq_ev[heads[i0_]], pq_ev[heads[i1_]]] if i == i0_ else [],
                            [b2] if i == i0_ else [])

            def cev(eng, i0_=i0_):
                return eng.xcopy(pqacc[0:64, heads[i0_] * 192: heads[i0_] * 192 + 384],
                                 pb[b2][0:64, 0:384])
            ev = evac(cev, pvC, [b2])
            pq_ev[heads[i0_]] = ev
            pq_ev[heads[i0_] + 1] = ev
        yc4_pe[st] = S.val("pe")
        yield

    p1_nch, p1_ngrp = P1_LIMIT if P1_LIMIT else (nch, 4)
    for cc in range(p1_nch):
        par = cc % 2
        vsrc = Dvsp.ap()[cc * 2048:(cc + 1) * 2048, :].rearrange("(o t) v -> t o v", t=128)
        vwaits = [] if vld_prev[par] is None else list(vld_prev[par])
        ldv = dma("sync", vtile[:, par * 2048:(par + 1) * 2048].rearrange("t (o v) -> t o v", o=16),
                  vsrc, waits=vwaits, stream=f"vld{par}")
        qkw = [] if qk_cons[par] is None else list(qk_cons[par])
        dma("sync", qtl[:, par * 1024:(par + 1) * 1024].rearrange("p (g t) -> p g t", g=8),
            Dq.ap()[:, cc * L:(cc + 1) * L].rearrange("(g p) t -> p g t", p=128),
            waits=qkw, stream=f"qld{par}")
        ldk = dma("sync", ktl[:, par * 1024:(par + 1) * 1024].rearrange("p (g t) -> p g t", g=8),
            Dk.ap()[:, cc * L:(cc + 1) * L].rearrange("(g p) t -> p g t", p=128),
            stream=f"qld{par}")
        import os as _os
        _ilv = _os.environ.get("P1_ILV")
        if _ilv:
            for gp in range(0, p1_ngrp, 2):
                gens = [grp_stages(cc, gp, 0, ldk, ldv)]
                if gp + 1 < p1_ngrp:
                    gens.append(grp_stages(cc, gp + 1, 1, ldk, ldv))
                live = list(gens)
                while live:
                    nxt = []
                    for g in live:
                        try:
                            next(g)
                            nxt.append(g)
                        except StopIteration:
                            pass
                    live = nxt
        else:
            for gp in range(0, p1_ngrp):
                gens = [grp_stages(cc, gp, gp % 2, ldk, ldv)]
                for g in gens:
                    for _ in g:
                        pass
                    if _os.environ.get("P1_FLUSHONLY"):
                        S.flush_dma()
                    elif not _os.environ.get("P1_NOBAR"):
                        S.barrier()
        vld_prev[par] = [("pe", S.val("pe")), ("dve", S.val("dve")), ldv]
        qk_cons[par] = [("pe", S.val("pe")), ("dve", S.val("dve"))]

    pqslot_prev = [None, None]
    for hh in range(16):
        slot = hh % 2
        bfb = "bfA" if slot == 0 else "bfB"
        bfcol = 0 if slot == 0 else 1024
        pv = pe_mm(lambda eng, hh=hh, bfcol=bfcol: eng.transpose(
            pbf[:, bfcol:bfcol + 64][0:64, :], pqacc[0:64, hh * 192:hh * 192 + 64],
            cst["identbf"][0:64, 0:64]), [], [bfb])
        wst = [] if pqslot_prev[slot] is None else [pqslot_prev[slot]]
        def pqev(eng, hh=hh, slot=slot, bfcol=bfcol):
            eng.xcopy(pqst[0:64, slot * 192:slot * 192 + 64], pbf[0:64, bfcol:bfcol + 64])
            return eng.xcopy(pqst[0:64, slot * 192 + 64:(slot + 1) * 192],
                             pqacc[0:64, hh * 192 + 64:(hh + 1) * 192])
        ev = evac(pqev, pv, [bfb], extra_waits=wst)
        pqslot_prev[slot] = dma("sync", Dpq_in.ap()[hh * 64:(hh + 1) * 64, :],
                                pqst[0:64, slot * 192:(slot + 1) * 192], waits=[ev], stream=f"stpq{slot}")
    S.flush_dma()
    S.emit("gpsimd", lambda eng: eng.collective_compute(
        "AllGather", ALU.bypass, replica_groups=[list(range(NCORES))],
        ins=[Dpq_in.ap().opt()], outs=[Dpq_gath.ap().opt()]),
        waits=[pqslot_prev[0], pqslot_prev[1]], inc=("cc_pq", 1))

    barP1 = S.barrier()
    if "negT" in taps:
        dma("sync", taps["negT"].rearrange("p (a t) -> p a t", t=64),
            DnegT.ap().rearrange("(a p) t -> p a t", p=64), stream="sttap")
    if "Yc" in taps:
        dma("sync", taps["Yc"].rearrange("p (c t) -> p c t", t=128),
            Dyc.ap().rearrange("(c p) t -> p c t", p=64), stream="sttap")
    for cc_ in range(p1_nch):
        for h_ in range(p1_ngrp * 4):
            ch_ = cc_ * 16 + h_
            if "ouT" in taps:
                dma("sync", taps["ouT"][ch_ * 128:(ch_ + 1) * 128, :],
                    Dou.ap()[ch_ * 128:(ch_ + 1) * 128, :], stream="sttap")
            if "zT" in taps:
                dma("sync", taps["zT"][ch_ * 64:(ch_ + 1) * 64, :],
                    Dzt.ap()[ch_ * 64:(ch_ + 1) * 64, :], stream="sttap")
    if "pq" in taps:
        dma("sync", taps["pq"], Dpq_in.ap(), stream="sttap")

    A.free(g1)
    A.free("hm")
    A.free("constp1")
    if maxphase < 8:
        _finish(nc, S, O, ntok)
        return nc, taps

    S.phase("blend")
    S_f = A.sb("sfin", "S_f", [64, 16 * 128], F32)
    sbf = A.sb("sfin", "sbf", [64, 128], BF16)
    gb_ = "blend"
    pqs = A.sb(gb_, "pqs", [64, NCORES * 16 * 192], BF16)
    acc3 = A.sb(gb_, "acc3", [64, 3 * 16 * 192], BF16)
    tcomp = A.sb(gb_, "tcomp", [64, 128], BF16)
    ldpq = dma("sync", pqs[:, :].rearrange("p (r h c) -> p r h c", r=NCORES, h=16),
               Dpq_gath.ap().rearrange("(r h p) c -> p r h c", p=64, h=16),
               waits=[("cc_pq", 1)], stream="pq")
    S.emit("vector", lambda eng: eng.memset(acc3[0:64, :], 0.0), inc=("dve", 1))
    for j in range(3):
        for r in range(NCORES):
            S.emit("vector", lambda eng, j=j, r=r: eng.scalar_tensor_tensor(
                acc3[0:64, j * 3072:(j + 1) * 3072], pqs[0:64, r * 3072:(r + 1) * 3072],
                cst["selp"][0:64, j * 8 + r:j * 8 + r + 1], acc3[0:64, j * 3072:(j + 1) * 3072],
                ALU.mult, ALU.add),
                waits=[ldpq, ld_consts] if (j == 0 and r == 0) else [], inc=("dve", 1))
    blend_v = S.val("dve")

    def PQj(j, hh):
        base = j * 3072 + hh * 192
        return acc3[0:64, base:base + 64], acc3[0:64, base + 64:base + 192]

    for hh in range(16):
        P1T, Q1 = PQj(1, hh)
        P0T, Q0 = PQj(0, hh)
        _, Q2 = PQj(2, hh)

        def c1(eng, P1T=P1T, Q1=Q1, Q2=Q2):
            eng.matmul(pb[0][0:64, 0:128], P1T, Q2, start=True, stop=False)
            return eng.matmul(pb[0][0:64, 0:128], cst["identbf"][0:64, 0:64], Q1, start=False, stop=True)
        S.emit("tensor", c1, waits=[("dve", blend_v), ("dve", S.val("dve")), ("act", S.val("act"))],
               inc=("pe", 1))
        S.emit("vector", lambda eng: eng.tensor_copy(tcomp[0:64, 0:128], pb[0][0:64, 0:128]),
               waits=[("pe", S.val("pe"))], inc=("dve", 1))
        tv = S.val("dve")

        def c2(eng, P0T=P0T, Q0=Q0):
            eng.matmul(pb[0][0:64, 0:128], P0T, tcomp[0:64, 0:128], start=True, stop=False)
            return eng.matmul(pb[0][0:64, 0:128], cst["identbf"][0:64, 0:64], Q0, start=False, stop=True)
        S.emit("tensor", c2, waits=[("dve", tv)], inc=("pe", 1))
        S.emit("vector", lambda eng, hh=hh: eng.tensor_copy(S_f[0:64, hh * 128:(hh + 1) * 128],
                                                            pb[0][0:64, 0:128]),
               waits=[("pe", S.val("pe"))], inc=("dve", 1))
    A.free(gb_)
    if "sinit" in taps:
        dma("sync", taps["sinit"], S_f[0:64, :], waits=[("dve", S.val("dve"))], stream="sttap")
    barBL = S.barrier()

    S.phase("p2a")
    g2a_ = "p2a"
    ztt = A.sb(g2a_, "ztt", [64, 2 * 16 * 128], BF16)
    outt = A.sb(g2a_, "outt", [128, 2 * 16 * 128], BF16)
    ycc = A.sb(g2a_, "ycc", [64, 2 * 16 * 128], BF16)
    gout_t = A.sb(g2a_, "gout_t", [128, 2 * 16 * 128], BF16)
    ngc = A.sb(g2a_, "ngc", [64, 2 * 16 * 64], BF16)
    og = A.sb(g2a_, "og", [128, 2 * 16 * 128], BF16)
    o2s = A.sb(g2a_, "o2s", [128, 128], F32)
    rstdb = A.sb(g2a_, "rstdb", [128, 128], F32)
    rrow = A.sb(g2a_, "rrow", [1, 128], F32)
    identg = A.sb(g2a_, "identg", [64, 64], BF16)

    prev_ch_ld = [None, None]
    stog = [None, None]
    for cc in range(nch):
        par = cc % 2
        wts = [] if prev_ch_ld[par] is None else list(prev_ch_ld[par])
        dma("sync", ztt[0:64, par * 2048:(par + 1) * 2048].rearrange("p (h t) -> p h t", h=16),
            Dzt.ap()[cc * 1024:(cc + 1) * 1024, :].rearrange("(h p) t -> p h t", p=64),
            waits=wts, stream=f"p2z{par}")
        dma("sync", outt[:, par * 2048:(par + 1) * 2048].rearrange("p (h t) -> p h t", h=16),
            Dou.ap()[cc * 2048:(cc + 1) * 2048, :].rearrange("(h p) t -> p h t", p=128),
            stream=f"p2z{par}")
        dma("sync", ycc[0:64, par * 2048:(par + 1) * 2048].rearrange("p (h t) -> p h t", h=16),
            Dyc.ap()[cc * 1024:(cc + 1) * 1024, :].rearrange("(h p) t -> p h t", p=64),
            stream=f"p2z{par}")
        dma("sync", ngc[0:64, par * 1024:(par + 1) * 1024].rearrange("p (h t) -> p h t", h=16),
            DnegT.ap()[cc * 1024:(cc + 1) * 1024, :].rearrange("(h p) t -> p h t", p=64),
            stream=f"p2z{par}")
        chld = dma("sync", gout_t[:, par * 2048:(par + 1) * 2048].rearrange("p (h t) -> p h t", h=16),
                   Dgsp.ap()[:, cc * 128:(cc + 1) * 128].rearrange("(h p) t -> p h t", p=128),
                   stream=f"p2z{par}")
        for hh in range(16):
            ch = cc * 16 + hh
            zts = ztt[0:64, par * 2048 + hh * 128: par * 2048 + (hh + 1) * 128]
            ous = outt[:, par * 2048 + hh * 128: par * 2048 + (hh + 1) * 128]
            ycs2 = ycc[0:64, par * 2048 + hh * 128: par * 2048 + (hh + 1) * 128]
            gos = gout_t[:, par * 2048 + hh * 128: par * 2048 + (hh + 1) * 128]
            ngs = ngc[0:64, par * 1024 + hh * 64: par * 1024 + (hh + 1) * 64]

            def precast(eng, hh=hh, ch=ch):
                eng.tensor_copy(sbf[0:64, 0:128], S_f[0:64, hh * 128:(hh + 1) * 128])
                return eng.tensor_scalar(identg[0:64, 0:64], cst["identbf"][0:64, 0:64],
                                         glb[0:64, ch:ch + 1], None, ALU.mult)
            S.emit("vector", precast, waits=[("pe", S.val("pe")), glb_ld], inc=("dve", 1))
            pcv = S.val("dve")

            def outmm(eng, zts=zts, ous=ous):
                eng.matmul(pb[2][:, 0:128], sbf[0:64, 0:128], zts, start=True, stop=False)
                return eng.matmul(pb[2][:, 0:128], cst["identbf"][:, :], ous, start=False, stop=True)
            S.emit("tensor", outmm, waits=[("dve", pcv), chld,
                                           ("dve", bg_guard[2].get("dve", 0)),
                                           ("act", bg_guard[2].get("act", 0))], inc=("pe", 1))
            outpe = S.val("pe")

            def chainmm(eng, ngs=ngs, ycs2=ycs2):
                eng.matmul(pb[1][0:64, 0:128], ngs, sbf[0:64, 0:128], start=True, stop=False)
                eng.matmul(pb[1][0:64, 0:128], identg[0:64, 0:64], sbf[0:64, 0:128], start=False, stop=False)
                return eng.matmul(pb[1][0:64, 0:128], cst["identbf"][0:64, 0:64], ycs2, start=False, stop=True)
            S.emit("tensor", chainmm, waits=[chld, ("dve", bg_guard[1].get("dve", 0)),
                                             ("act", bg_guard[1].get("act", 0))], inc=("pe", 1))
            chpe = S.val("pe")
            S.emit("vector", lambda eng, hh=hh: eng.tensor_copy(S_f[0:64, hh * 128:(hh + 1) * 128],
                                                                pb[1][0:64, 0:128]),
                   waits=[("pe", chpe)], inc=("dve", 1))
            bg_guard[1]["dve"] = S.val("dve")

            S.emit("scalar", lambda eng: eng.activation(o2s[:, 0:128], pb[2][:, 0:128], AF.Square),
                   waits=[("pe", outpe)], inc=("act", 1))
            S.emit("tensor", lambda eng: eng.matmul(pb[4][0:1, 0:128], cst["ones128x1"][:, 0:1],
                                                    o2s[:, 0:128], start=True, stop=True),
                   waits=[("act", S.val("act")), ("dve", bg_guard[4].get("dve", 0)),
                          ("act", bg_guard[4].get("act", 0))], inc=("pe", 1))
            S.emit("scalar", lambda eng: eng.activation(rrow[0:1, 0:128], pb[4][0:1, 0:128], AF.Sqrt,
                                                        scale=1.0 / DV, bias=cvals[0:1, 2:3]),
                   waits=[("pe", S.val("pe"))], inc=("act", 1))
            bg_guard[4]["act"] = S.val("act")
            S.emit("vector", lambda eng: eng.reciprocal(rrow[0:1, 0:128], rrow[0:1, 0:128]),
                   waits=[("act", S.val("act"))], inc=("dve", 1))
            S.emit("tensor", lambda eng: eng.matmul(pb[5][:, 0:128], cst["onesrow"][0:1, 0:128],
                                                    rrow[0:1, 0:128], start=True, stop=True),
                   waits=[("dve", S.val("dve")), ("dve", bg_guard[5].get("dve", 0)),
                          ("act", bg_guard[5].get("act", 0))], inc=("pe", 1))
            S.emit("vector", lambda eng: eng.tensor_copy(rstdb[:, 0:128], pb[5][:, 0:128]),
                   waits=[("pe", S.val("pe"))], inc=("dve", 1))
            bg_guard[5]["dve"] = S.val("dve")
            S.emit("vector", lambda eng: eng.tensor_tensor(o2s[:, 0:128], pb[2][:, 0:128], rstdb[:, 0:128],
                                                           ALU.mult),
                   inc=("dve", 1))
            bg_guard[2]["dve"] = S.val("dve")
            wog = [chld]
            if hh == 0 and stog[par] is not None:
                wog.append(stog[par])
            S.emit("vector", lambda eng, hh=hh, gos=gos, par=par: eng.tensor_tensor(
                og[:, par * 2048 + hh * 128: par * 2048 + (hh + 1) * 128], o2s[:, 0:128], gos, ALU.mult),
                waits=wog, inc=("dve", 1))
        ogv = S.val("dve")
        prev_ch_ld[par] = [("dve", ogv), ("pe", S.val("pe"))]
        stog[par] = dma("sync", Dog.ap()[cc * 128:(cc + 1) * 128, :], og[:, par * 2048:(par + 1) * 2048],
                        waits=[("dve", ogv)], stream=f"stog{par}")
    A.free(g2a_)
    A.free("sfin")
    barP2a = S.barrier()

    S.phase("p2b")
    g2b_ = "p2b"
    owTb = A.sb(g2b_, "owTb", [128, 16 * C], BF16)
    mbc_ga = A.sb(g2b_, "mbc_ga", [128, C], BF16)
    mbc_cb = A.sb(g2b_, "mbc_cb", [128, C], BF16)
    mbc_sb = A.sb(g2b_, "mbc_sb", [128, C], BF16)
    g1bb = A.sb(g2b_, "g1bb", [128, C], F32)
    g2bb = A.sb(g2b_, "g2bb", [128, C], F32)
    og2 = A.sb(g2b_, "og2", [128, 2 * 2048], BF16)
    xtile = A.sb(g2b_, "xtile", [128, C], F32)
    x1t = A.sb(g2b_, "x1t", [128, C], F32)
    h2t = A.sb(g2b_, "h2t", [128, C], F32)
    qsc = A.sb(g2b_, "qsc", [128, C], F32)
    itok = A.sb(g2b_, "itok", [128, C], BF16)
    ibf = A.sb(g2b_, "ibf", [128, 4 * 128], BF16)
    st2 = A.sb(g2b_, "st2", [128, 8], F32)

    dma("gpsimd", mbc_ga[:, :], Dmodvec.ap()[2:3, :].partition_broadcast(128), waits=[modvec_st], sem="gmbc")
    dma("gpsimd", mbc_cb[:, :], Dmodvec.ap()[3:4, :].partition_broadcast(128), sem="gmbc")
    dma("gpsimd", mbc_sb[:, :], Dmodvec.ap()[4:5, :].partition_broadcast(128), sem="gmbc")
    mbc_gld = ("gmbc", S.val("gmbc"))
    dma("sync", g1bb[:, :], I["g1v"].partition_broadcast(128), stream="mbc")
    dma("sync", g2bb[:, :], I["g2v"].partition_broadcast(128), stream="mbc")
    g12_ld = ("q_mbc", S.val("q_mbc"))
    ldow = dma("gpsimd", owTb[:, :].rearrange("p (v o) -> p v o", v=16),
               I["owT"].rearrange("(v p) o -> p v o", p=128), sem="gow")

    ip_slot_ev = {}
    og2_pe = [0, 0]
    x1_st_prev = None
    prev_x1v = 0
    prev_act_tail = 0
    for cc in range(nch):
        par = cc % 2
        ldog = dma("sync", og2[:, par * 2048:(par + 1) * 2048], Dog.ap()[cc * 128:(cc + 1) * 128, :],
                   waits=[("pe", og2_pe[par])], stream=f"ogl{par}")
        ldx = dma("sync", xtile[:, :], I["x_sl"][3 + cc * 128:3 + (cc + 1) * 128, :],
                  waits=[("dve", prev_x1v)], stream="x2")
        for o2i in range(2):
            for half2 in range(2):
                oc = o2i * 2 + half2
                bank = 0 if half2 == 0 else 3

                def owmm(eng, oc=oc, bank=bank, par=par):
                    last = None
                    for hh in range(16):
                        last = eng.matmul(pb[bank][:, 0:512],
                                          og2[:, par * 2048 + hh * 128: par * 2048 + (hh + 1) * 128],
                                          owTb[:, hh * C + oc * 512: hh * C + oc * 512 + 512],
                                          start=(hh == 0), stop=(hh == 15))
                    return last
                S.emit("tensor", owmm, waits=[ldog, ldow,
                                              ("dve", bg_guard[bank].get("dve", 0)),
                                              ("act", bg_guard[bank].get("act", 0))], inc=("pe", 1))
                opev = S.val("pe")

                def x1ev(eng, oc=oc, bank=bank):
                    eng.tensor_tensor(qsc[:, oc * 512:(oc + 1) * 512], pb[bank][:, 0:512],
                                      mbc_ga[:, oc * 512:(oc + 1) * 512], ALU.mult)
                    return eng.tensor_add(x1t[:, oc * 512:(oc + 1) * 512], qsc[:, oc * 512:(oc + 1) * 512],
                                          xtile[:, oc * 512:(oc + 1) * 512])
                wx1e = [("pe", opev), ldx, mbc_gld, ("act", prev_act_tail)]
                if x1_st_prev is not None:
                    wx1e.append(x1_st_prev)
                S.emit("vector", x1ev, waits=wx1e, inc=("dve", 1))
                bg_guard[bank]["dve"] = S.val("dve")
        og2_pe[par] = S.val("pe")
        x1v = S.val("dve")
        wx1 = [("dve", x1v)] if x1_st_prev is None else [("dve", x1v), x1_st_prev]
        x1_st_prev = dma("sync", Dx1.ap()[cc * 128:(cc + 1) * 128, :], x1t[:, :], waits=wx1, stream="stx1")

        S.emit("vector", lambda eng: eng.reduce_sum(st2[:, 0:1], x1t[:, :], AX.X),
               waits=[("dve", x1v)], inc=("dve", 1))
        S.emit("scalar", lambda eng: eng.activation(qsc[:, :], x1t[:, :], AF.Square, accum_out=st2[:, 1:2]),
               waits=[("dve", x1v), ("act", prev_act_tail)], inc=("act", 1))

        def st2f(eng):
            eng.tensor_scalar(st2[:, 0:1], st2[:, 0:1], 1.0 / C, None, ALU.mult)
            eng.tensor_scalar(st2[:, 2:3], st2[:, 1:2], 1.0 / C, None, ALU.mult)
            eng.tensor_mul(st2[:, 3:4], st2[:, 0:1], st2[:, 0:1])
            return eng.tensor_sub(st2[:, 2:3], st2[:, 2:3], st2[:, 3:4])
        S.emit("vector", st2f, waits=[("act", S.val("act"))], inc=("dve", 1))
        S.emit("scalar", lambda eng: eng.activation(st2[:, 4:5], st2[:, 2:3], AF.Sqrt, bias=cvals[:, 0:1]),
               waits=[("dve", S.val("dve"))], inc=("act", 1))
        S.emit("vector", lambda eng: eng.reciprocal(st2[:, 5:6], st2[:, 4:5]),
               waits=[("act", S.val("act"))], inc=("dve", 1))
        S.emit("vector", lambda eng: eng.tensor_scalar(h2t[:, :], x1t[:, :], st2[:, 0:1], st2[:, 5:6],
                                                       ALU.subtract, ALU.mult),
               inc=("dve", 1))
        S.emit("vector", lambda eng: eng.tensor_mul(h2t[:, :], h2t[:, :], mbc_cb[:, :]),
               waits=[mbc_gld], inc=("dve", 1))
        S.emit("vector", lambda eng: eng.tensor_tensor(h2t[:, :], h2t[:, :], mbc_sb[:, :], ALU.add),
               inc=("dve", 1))
        h2v = S.val("dve")
        S.emit("scalar", lambda eng: eng.activation(qsc[:, :], h2t[:, :], AF.Square, accum_out=st2[:, 6:7]),
               waits=[("dve", h2v)], inc=("act", 1))
        S.emit("vector", lambda eng: eng.tensor_scalar(st2[:, 6:7], st2[:, 6:7], 1.0 / C, None, ALU.mult),
               waits=[("act", S.val("act"))], inc=("dve", 1))
        S.emit("scalar", lambda eng: eng.activation(st2[:, 7:8], st2[:, 6:7], AF.Sqrt, bias=cvals[:, 3:4]),
               waits=[("dve", S.val("dve"))], inc=("act", 1))
        S.emit("vector", lambda eng: eng.reciprocal(st2[:, 7:8], st2[:, 7:8]),
               waits=[("act", S.val("act"))], inc=("dve", 1))
        rstd2v = S.val("dve")
        prev_act_tail = S.val("act")

        for br, (gbt, sft, di, fcol) in enumerate(((g1bb, s1f, Di1, 0), (g2bb, s2f, Di2, 1))):
            S.emit("vector", lambda eng, gbt=gbt: eng.scalar_tensor_tensor(
                qsc[:, :], h2t[:, :], st2[:, 7:8], gbt[:, :], ALU.mult, ALU.mult),
                waits=[("dve", rstd2v), g12_ld, ("act", prev_act_tail)], inc=("dve", 1))
            S.emit("vector", lambda eng: eng.tensor_reduce(st2[:, 6:7], qsc[:, :], AX.X, ALU.max,
                                                           apply_absolute_value=True),
                   inc=("dve", 1))

            def qsc1(eng, sft=sft, fcol=fcol, cc=cc):
                eng.tensor_scalar(st2[:, 6:7], st2[:, 6:7], 1e-4, None, ALU.max)
                eng.tensor_scalar(sft[:, cc:cc + 1], st2[:, 6:7], fscb[:, fcol:fcol + 1], None, ALU.mult)
                return eng.reciprocal(st2[:, 6:7], st2[:, 6:7])
            S.emit("vector", qsc1, waits=[fscb_ld], inc=("dve", 1))

            def qround(eng):
                eng.tensor_scalar(qsc[:, :], qsc[:, :], st2[:, 6:7], None, ALU.mult)
                eng.tensor_scalar(qsc[:, :], qsc[:, :], 127.0, CMAGIC, ALU.mult, ALU.add)
                eng.tensor_scalar(qsc[:, :], qsc[:, :], CMAGIC - 128.0, CMAGIC + 127.0, ALU.max, ALU.min)
                return eng.tensor_scalar(itok[:, :], qsc[:, :], -CMAGIC, None, ALU.add)
            S.emit("vector", qround, waits=[("pe", S.val("pe"))], inc=("dve", 1))
            qv = S.val("dve")
            for ic in range(16):
                slot = ic % 4
                bank = slot % 2
                pcol = bank * 1024 + (slot // 2) * 128
                key = slot
                wt = [("dve", qv)]
                gkb = "bfA" if bank == 0 else "bfB"
                for snm, v in bg_guard[gkb].items():
                    if v > 0:
                        wt.append((snm, v))
                S.emit("tensor", lambda eng, ic=ic, pcol=pcol: eng.transpose(
                    pbf[:, pcol:pcol + 128], itok[:, ic * 128:(ic + 1) * 128], cst["identbf"][:, :]),
                    waits=wt, inc=("pe", 1))
                wtc = [("pe", S.val("pe"))]
                if key in ip_slot_ev:
                    wtc.append(ip_slot_ev[key])
                S.emit("vector", lambda eng, slot=slot, pcol=pcol: eng.tensor_copy(
                    ibf[:, slot * 128:(slot + 1) * 128], pbf[:, pcol:pcol + 128]),
                    waits=wtc, inc=("dve", 1))
                bg_guard[gkb]["dve"] = S.val("dve")
                ip_slot_ev[key] = dma("sync", di.ap()[(ic * nch + cc) * 128:(ic * nch + cc + 1) * 128, :],
                                      ibf[:, slot * 128:(slot + 1) * 128],
                                      waits=[("dve", S.val("dve"))], stream=f"sti{br}{slot}")
        prev_x1v = S.val("dve")
    A.free(g2b_)
    A.free("stats")
    barP2 = S.barrier()
    if "x1" in taps:
        dma("sync", taps["x1"], Dx1.ap(), stream="sttap")
    if "i1" in taps:
        dma("sync", taps["i1"], Di1.ap(), stream="sttap")

    if maxphase < 9:
        _finish(nc, S, O, ntok)
        return nc, taps

    S.phase("f1")
    gf1 = "f1"
    i1r = A.sb(gf1, "i1r", [128, 16 * nch * 128], BF16)
    i2r = A.sb(gf1, "i2r", [128, 16 * nch * 128], BF16)
    w12 = A.sb(gf1, "w12", [128, 32 * 512], BF16)
    hswt = A.sb(gf1, "hswt", [128, 512], F32)
    hbf = A.sb(gf1, "hbf", [128, 512], BF16)
    g3p = A.sb(gf1, "g3p", [128, 512], F32)
    y2t = A.sb(gf1, "y2t", [128, 512], F32)
    g3sl = A.sb(gf1, "g3sl", [128, 2 * 512], F32)

    ld1 = dma("sync", i1r[:, :].rearrange("p (a t) -> p a t", a=16 * nch),
              Di1.ap().rearrange("(a p) t -> p a t", p=128), stream="f1i1")
    ld2 = dma("sync", i2r[:, :].rearrange("p (a t) -> p a t", a=16 * nch),
              Di2.ap().rearrange("(a p) t -> p a t", p=128), stream="f1i2")
    S.emit("vector", lambda eng: (eng.memset(mx3[:, :], 0.0), eng.memset(ss3[:, :], 0.0))[-1],
           inc=("dve", 1))
    f1init = S.val("dve")

    hst_prev = None
    wcons_pe = 0
    g3_cons = [0, 0]
    prev_hbf_v = 0
    prev_ss3_v = 0
    for n in range(HID // 512):
        g3par = n % 2
        wg3 = [("dve", g3_cons[g3par])] if g3_cons[g3par] else []
        g3ld = dma("gpsimd", g3sl[:, g3par * 512:(g3par + 1) * 512],
                   I["g3v"][:, n * 512:(n + 1) * 512].partition_broadcast(128), waits=wg3,
                   sem=f"g3l{g3par}")
        ldws_by = [[], []]
        for wsel in range(2):
            base_row = 0 if wsel == 0 else 256
            for r in range(8):
                ldws_by[wsel].append(dma("sync",
                                         w12[:, (wsel * 16 + r * 2) * 512:(wsel * 16 + r * 2 + 2) * 512]
                                         .rearrange("p (a t) -> p a t", a=2),
                                         Dwq_gath.ap()[r * 768 + base_row: r * 768 + base_row + 256,
                                                       n * 512:(n + 1) * 512]
                                         .rearrange("(a p) t -> p a t", p=128),
                                         waits=[("cc_wq", 1), ("pe", wcons_pe)], stream=f"f1w{wsel}"))
        for tc in range(nch):
            pe12 = []
            for wsel, bank, ir, ldi in ((0, 0, i1r, ld1), (1, 1, i2r, ld2)):
                def f1mm(eng, tc=tc, wsel=wsel, bank=bank, ir=ir):
                    last = None
                    for ic in range(16):
                        wslot = wsel * 16 + ic
                        last = eng.matmul(pb[bank][:, 0:512],
                                          ir[:, (ic * nch + tc) * 128:(ic * nch + tc + 1) * 128],
                                          w12[:, wslot * 512:(wslot + 1) * 512],
                                          start=(ic == 0), stop=(ic == 15))
                    return last
                wt = [ldi, ldws_by[wsel][-1], ("dve", bg_guard[bank].get("dve", 0)),
                      ("act", bg_guard[bank].get("act", 0))]
                S.emit("tensor", f1mm, waits=wt, inc=("pe", 1))
                pe12.append(S.val("pe"))
            wcons_pe = S.val("pe")
            S.emit("scalar", lambda eng, tc=tc: eng.activation(hswt[:, 0:512], pb[0][:, 0:512], AF.Silu,
                                                               scale=s1f[:, tc:tc + 1]),
                   waits=[("pe", pe12[0]), ("dve", prev_hbf_v), ("dve", prev_ss3_v)], inc=("act", 1))
            bg_guard[0]["act"] = S.val("act")
            S.emit("vector", lambda eng, tc=tc: eng.tensor_scalar(y2t[:, 0:512], pb[1][:, 0:512],
                                                                  s2f[:, tc:tc + 1], None, ALU.mult),
                   waits=[("pe", pe12[1])], inc=("dve", 1))
            bg_guard[1]["dve"] = S.val("dve")
            hw = [("act", S.val("act"))]
            if hst_prev is not None:
                hw.append(hst_prev)
            S.emit("vector", lambda eng: eng.tensor_tensor(hbf[:, 0:512], hswt[:, 0:512], y2t[:, 0:512],
                                                           ALU.mult),
                   waits=hw, inc=("dve", 1))
            hv = S.val("dve")
            prev_hbf_v = hv
            S.emit("scalar", lambda eng: eng.activation(hswt[:, 0:512], hbf[:, 0:512], AF.Square,
                                                        accum_out=sc3[:, 0:1]),
                   waits=[("dve", hv), ("dve", prev_ss3_v)], inc=("act", 1))
            S.emit("vector", lambda eng, tc=tc: eng.tensor_add(ss3[:, tc:tc + 1], ss3[:, tc:tc + 1],
                                                               sc3[:, 0:1]),
                   waits=[("act", S.val("act")), ("dve", f1init)], inc=("dve", 1))
            prev_ss3_v = S.val("dve")
            S.emit("vector", lambda eng, g3par=g3par: eng.tensor_tensor(
                g3p[:, 0:512], hbf[:, 0:512], g3sl[:, g3par * 512:(g3par + 1) * 512], ALU.mult),
                waits=[g3ld], inc=("dve", 1))
            S.emit("vector", lambda eng: eng.tensor_reduce(sc3[:, 1:2], g3p[:, 0:512], AX.X, ALU.max,
                                                           apply_absolute_value=True),
                   inc=("dve", 1))
            S.emit("vector", lambda eng, tc=tc: eng.tensor_max(mx3[:, tc:tc + 1], mx3[:, tc:tc + 1],
                                                               sc3[:, 1:2]),
                   inc=("dve", 1))
            hst_prev = dma("sync", Dh.ap()[tc * 128:(tc + 1) * 128, n * 512:(n + 1) * 512], hbf[:, 0:512],
                           waits=[("dve", hv)], stream="sth")
        g3_cons[g3par] = S.val("dve")

    S.emit("vector", lambda eng: eng.tensor_scalar(ss3[:, :], ss3[:, :], 1.0 / HID, None, ALU.mult),
           waits=[("dve", S.val("dve")), hst_prev], inc=("dve", 1))
    S.emit("scalar", lambda eng: eng.activation(ss3[:, :], ss3[:, :], AF.Sqrt, bias=cvals[:, 3:4]),
           waits=[("dve", S.val("dve"))], inc=("act", 1))

    def rs3b(eng):
        eng.reciprocal(ss3[:, :], ss3[:, :])
        eng.tensor_mul(mx3[:, :], mx3[:, :], ss3[:, :])
        eng.tensor_scalar(mx3[:, :], mx3[:, :], 1e-4, None, ALU.max)
        eng.tensor_scalar(s3f[:, :], mx3[:, :], fscb[:, 2:3], None, ALU.mult)
        return eng.reciprocal(mx3[:, :], mx3[:, :])
    S.emit("vector", rs3b, waits=[("act", S.val("act")), fscb_ld], inc=("dve", 1))
    if "f2sc" in taps:
        def scdump(eng):
            for i, t in enumerate((s1f, s2f, s3f, mx3, ss3)):
                eng.tensor_copy(hswt[:, i * 16:i * 16 + t.shape[1]], t[:, :])
            return eng.tensor_copy(hswt[:, 80:82], sc3[:, 0:2])
        S.emit("vector", scdump, waits=[("dve", S.val("dve"))], inc=("dve", 1))
        dma("sync", taps["f2sc"][:, 0:96], hswt[:, 0:96], waits=[("dve", S.val("dve"))], stream="sttap")
    A.free(gf1)
    barF1 = S.barrier()
    if "hmid" in taps:
        dma("sync", taps["hmid"], Dh.ap(), stream="sttap")

    if maxphase < 10:
        _finish(nc, S, O, ntok)
        return nc, taps

    S.phase("f2")
    gf2 = "f2"
    QT = min(512, ntok)
    nblk = ntok // QT
    nch_q = QT // 128
    i3h = A.sb(gf2, "i3h", [128, 2 * 64 * QT], BF16)
    hqb = A.sb(gf2, "hqb", [128, 2 * 512], BF16)
    q32 = A.sb(gf2, "q32", [128, 512], F32)
    i3t = A.sb(gf2, "i3t", [128, 512], BF16)
    w3t = A.sb(gf2, "w3t", [128, 2 * 2048], BF16)
    x1r = A.sb(gf2, "x1r", [128, 2 * 512], F32)
    yout = A.sb(gf2, "yout", [128, 2 * 512], F32)
    g3s2 = A.sb(gf2, "g3s2", [128, 2 * 512], F32)
    mbc_gb = A.sb(gf2, "mbc_gb", [128, C], BF16)
    dma("gpsimd", mbc_gb[:, :], Dmodvec.ap()[5:6, :].partition_broadcast(128), sem="gmb2")
    gb_ld = ("gmb2", S.val("gmb2"))

    yst_prev = [None, None]
    hld_prev = [None, None]
    g3_cons2 = [0, 0]
    i3pe = [0, 0]
    i3v = [0, 0]
    banks4 = [0, 1, 3, 4]
    x1r_cons = [0, 0]
    w3slot_ev = [None, None]

    def f2_assemble(blk_i):
        par = blk_i % 2
        for n in range(HID // 512):
            g3par = n % 2
            wg3 = [("dve", g3_cons2[g3par])] if g3_cons2[g3par] else []
            g3ld2 = dma("gpsimd", g3s2[:, g3par * 512:(g3par + 1) * 512],
                        I["g3v"][:, n * 512:(n + 1) * 512].partition_broadcast(128), waits=wg3,
                        sem=f"g3m{g3par}")
            for tcl in range(nch_q):
                tc = blk_i * nch_q + tcl
                hslot = (n * nch_q + tcl) % 2
                wth = [("q_sth", S.cnt["q_sth"])]
                if hld_prev[hslot] is not None:
                    wth.append(hld_prev[hslot])
                ldh = dma("sync", hqb[:, hslot * 512:(hslot + 1) * 512],
                          Dh.ap()[tc * 128:(tc + 1) * 128, n * 512:(n + 1) * 512],
                          waits=wth, stream=f"f2h{hslot}")

                def i3q(eng, tc=tc, g3par=g3par, hslot=hslot):
                    eng.scalar_tensor_tensor(q32[:, 0:512], hqb[:, hslot * 512:(hslot + 1) * 512],
                                             ss3[:, tc:tc + 1], g3s2[:, g3par * 512:(g3par + 1) * 512],
                                             ALU.mult, ALU.mult)
                    eng.tensor_scalar(q32[:, 0:512], q32[:, 0:512], mx3[:, tc:tc + 1], None, ALU.mult)
                    eng.tensor_scalar(q32[:, 0:512], q32[:, 0:512], 127.0, CMAGIC, ALU.mult, ALU.add)
                    eng.tensor_scalar(q32[:, 0:512], q32[:, 0:512], CMAGIC - 128.0, CMAGIC + 127.0,
                                      ALU.max, ALU.min)
                    return eng.tensor_scalar(i3t[:, 0:512], q32[:, 0:512], -CMAGIC, None, ALU.add)
                S.emit("vector", i3q, waits=[ldh, g3ld2, ("pe", S.val("pe"))], inc=("dve", 1))
                hld_prev[hslot] = ("dve", S.val("dve"))
                qv = S.val("dve")
                for s4 in range(4):
                    icg = n * 4 + s4
                    bank = s4 % 2
                    pcol = bank * 1024 + (s4 // 2) * 128
                    gkb = "bfA" if bank == 0 else "bfB"
                    wt = [("dve", qv), ("pe", i3pe[par])]
                    for snm, v in bg_guard[gkb].items():
                        if v > 0:
                            wt.append((snm, v))
                    S.emit("tensor", lambda eng, s4=s4, pcol=pcol: eng.transpose(
                        pbf[:, pcol:pcol + 128], i3t[:, s4 * 128:(s4 + 1) * 128], cst["identbf"][:, :]),
                        waits=wt, inc=("pe", 1))
                    S.emit("vector", lambda eng, icg=icg, tcl=tcl, pcol=pcol, par=par: eng.tensor_copy(
                        i3h[:, par * 64 * QT + icg * QT + tcl * 128:
                            par * 64 * QT + icg * QT + (tcl + 1) * 128],
                        pbf[:, pcol:pcol + 128]),
                        waits=[("pe", S.val("pe"))], inc=("dve", 1))
                    bg_guard[gkb]["dve"] = S.val("dve")
            g3_cons2[g3par] = S.val("dve")
        i3v[par] = S.val("dve")

    def f2_matmul(blk_i):
        par = blk_i % 2
        for oc in range(4):
            for icg in range(16):
                wslot = icg % 2
                g = icg * 512
                r, j = g // 1024, g % 1024
                blk = Dwq_gath.ap()[r * 768 + 512: r * 768 + 768, :].rearrange(
                    "a (b c) -> (a b) c", b=4)
                wt = [("cc_wq", 1)]
                if w3slot_ev[wslot] is not None:
                    wt.append(w3slot_ev[wslot])
                ldw3 = dma("gpsimd", w3t[:, wslot * 2048:(wslot + 1) * 2048]
                           .rearrange("p (a c) -> p a c", a=4),
                           blk[j:j + 512, oc * 512:(oc + 1) * 512].rearrange("(a p) c -> p a c", p=128),
                           waits=wt, sem=f"gw3{wslot}")
                for tci in range(nch_q):
                    bank = banks4[tci]

                    def f2mm(eng, icg=icg, tci=tci, wslot=wslot, bank=bank, par=par):
                        last = None
                        for a in range(4):
                            ic = icg * 4 + a
                            last = eng.matmul(
                                pb[bank][:, 0:512],
                                i3h[:, par * 64 * QT + ic * QT + tci * 128:
                                    par * 64 * QT + ic * QT + (tci + 1) * 128],
                                w3t[:, wslot * 2048 + a * 512: wslot * 2048 + (a + 1) * 512],
                                start=(ic == 0), stop=(ic == 63))
                        return last
                    wtm = [ldw3, ("dve", i3v[par])]
                    if icg == 0:
                        wtm += [("dve", bg_guard[bank].get("dve", 0)),
                                ("act", bg_guard[bank].get("act", 0))]
                    S.emit("tensor", f2mm, waits=wtm, inc=("pe", 1))
                w3slot_ev[wslot] = ("pe", S.val("pe"))
            mmend = S.val("pe")
            for tci in range(nch_q):
                tc = blk_i * nch_q + tci
                bank = banks4[tci]
                xsl = tci % 2
                ldx1 = dma("sync", x1r[:, xsl * 512:(xsl + 1) * 512],
                           Dx1.ap()[tc * 128:(tc + 1) * 128, oc * 512:(oc + 1) * 512],
                           waits=[x1_st_prev, ("dve", x1r_cons[xsl])], stream=f"f2x{xsl}")

                def yev(eng, tc=tc, oc=oc, bank=bank, xsl=xsl):
                    eng.tensor_scalar(yout[:, xsl * 512:(xsl + 1) * 512], pb[bank][:, 0:512],
                                      s3f[:, tc:tc + 1], None, ALU.mult)
                    eng.tensor_tensor(yout[:, xsl * 512:(xsl + 1) * 512],
                                      yout[:, xsl * 512:(xsl + 1) * 512],
                                      mbc_gb[:, oc * 512:(oc + 1) * 512], ALU.mult)
                    return eng.tensor_add(yout[:, xsl * 512:(xsl + 1) * 512],
                                          yout[:, xsl * 512:(xsl + 1) * 512], x1r[:, xsl * 512:(xsl + 1) * 512])
                wyev = [("pe", mmend), ldx1, gb_ld]
                if yst_prev[xsl] is not None:
                    wyev.append(yst_prev[xsl])
                S.emit("vector", yev, waits=wyev, inc=("dve", 1))
                bg_guard[bank]["dve"] = S.val("dve")
                x1r_cons[xsl] = S.val("dve")
                yst_prev[xsl] = dma("sync", O["y"][tc * 128:(tc + 1) * 128, oc * 512:(oc + 1) * 512],
                                    yout[:, xsl * 512:(xsl + 1) * 512],
                                    waits=[("dve", S.val("dve"))], stream=f"sty{xsl}")
        i3pe[par] = S.val("pe")

    f2_assemble(0)
    for blk_i in range(nblk):
        if blk_i + 1 < nblk:
            f2_assemble(blk_i + 1)
        f2_matmul(blk_i)
    print(f"[alloc] peak {A.peak}B/part at {A.peak_at}")
    _finish(nc, S, O, ntok, y_done=True)
    return nc, taps


def _finish(nc, S, O, ntok, y_done=False):
    if not y_done:
        ztile = nc.sbuf_tensor("zfin", [128, C], F32).__enter__()
        S.emit("vector", lambda eng: eng.memset(ztile[:, :], 0.0), inc=("dve", 1))
        vz = S.val("dve")
        for r in range(0, ntok, 128):
            S.emit("sync", lambda eng, r=r: eng.dma_start(out=O["y"][r:r + 128, :], in_=ztile[:, :]),
                   waits=[("dve", vz)], inc=("st", 16))
    finals = [(nm, S.cnt[nm]) for nm in list(S.cnt)
              if (nm.startswith("q_st") or nm in ("st", "gst")) and S.cnt[nm] > 0]

    def fin(eng):
        for nm, v in finals:
            eng.wait_ge(S.sems[nm], v)
    S.events.append(("sync", fin, "fin"))
    with nc.Block() as block:
        S.replay(block)


_CACHE = {}


def kernel(**inputs):
    from concourse.bass_utils import run_bass_kernel_spmd
    ntok = 2048
    if "nc" not in _CACHE:
        nc, _ = build(ntok=ntok, debug_taps=(), maxphase=99)
        _CACHE["nc"] = nc
    nc = _CACHE["nc"]
    in_maps = host_inputs(inputs, ntok=ntok)
    res = run_bass_kernel_spmd(nc, in_maps, core_ids=list(range(NCORES)))
    y = np.zeros((2, 4 * ntok, C), np.float32)
    for c in range(NCORES):
        y[c // 4, (c % 4) * ntok:(c % 4 + 1) * ntok] = res.results[c]["y"]
    return y


# version 08 (final)
# speedup vs baseline: 1.0152x; 1.0060x over previous
import numpy as np
import ml_dtypes
import concourse.bass as bass
import concourse.mybir as mybir

F32 = mybir.dt.float32
BF16 = mybir.dt.bfloat16
AF = mybir.ActivationFunctionType
ALU = mybir.AluOpType
AX = mybir.AxisListType

NCORES = 8
DBG_CAPTURE = None
P1_LIMIT = None
SKIP_HEADLOOP = False
HL_PARTS = ("bld1", "bld2", "bld4", "ktr", "x0", "powers", "applies", "post", "compose")
HL_PARTS2 = None
H, DK, DV = 16, 64, 128
KEY_DIM, VAL_DIM = H * DK, H * DV
C = 2048
HID = 8192
L = 128
CMAGIC = 12582912.0
ATANH_HALF = 0.5493061443340549
EPS_ORMS = 1.1920929e-07


_DRAIN_OPS = {
    "tensor_scalar", "tensor_single_scalar", "tensor_scalar_add", "tensor_scalar_sub",
    "tensor_scalar_mul", "tensor_scalar_max", "tensor_scalar_min", "tensor_tensor",
    "tensor_add", "tensor_sub", "tensor_mul", "tensor_max", "tensor_copy", "tensor_relu",
    "scalar_tensor_tensor", "tensor_tensor_scan", "tensor_reduce", "reduce_sum",
    "reduce_max", "reciprocal", "memset", "memzero", "select", "copy_predicated",
    "activation", "copy", "add", "mul", "square", "sqrt", "sign", "transpose",
    "stream_shuffle",
}


class DrainingEngine:

    def __init__(self, eng):
        object.__setattr__(self, "_e", eng)

    def xcopy(self, out, in_):
        e = self._e
        e.drain()
        if hasattr(e, "tensor_copy"):
            return e.tensor_copy(out, in_)
        return e.copy(out, in_)

    def xmul(self, out, in_, s):
        e = self._e
        e.drain()
        if hasattr(e, "tensor_scalar_mul"):
            return e.tensor_scalar_mul(out, in_, s)
        return e.mul(out, in_, s)

    def __getattr__(self, name):
        attr = getattr(self._e, name)
        if name in _DRAIN_OPS and callable(attr):
            e = self._e

            def call(*a, **k):
                e.drain()
                return attr(*a, **k)
            return call
        return attr


class Sched:
    ENGINES = ("sync", "tensor", "vector", "scalar", "gpsimd")

    def __init__(self, nc):
        self.nc = nc
        self.events = []
        self.sems = {}
        self.cnt = {}
        self.cur_phase = "init"

    def phase(self, name):
        self.cur_phase = name

    def sem(self, name):
        if name not in self.sems:
            self.sems[name] = self.nc.semaphore(name).__enter__()
            self.cnt[name] = 0
        return self.sems[name]

    def bump(self, name, amt):
        self.sem(name)
        self.cnt[name] += amt
        return self.cnt[name]

    def val(self, name):
        self.sem(name)
        return self.cnt[name]

    def emit(self, engine, fn, waits=(), inc=None):
        assert engine in self.ENGINES, engine
        wl = [(self.sems[s], v) for (s, v) in waits if v > 0]
        ih = None
        if inc is not None:
            self.bump(inc[0], inc[1])
            ih = (self.sems[inc[0]], inc[1])

        def wrapped(eng, fn=fn, wl=wl, ih=ih):
            for (s, v) in wl:
                eng.wait_ge(s, v)
            ins = fn(eng)
            if ih is not None:
                assert ins is not None, "emit with inc: fn must return an instruction"
                ins.then_inc(ih[0], ih[1])
        self.events.append((engine, wrapped, self.cur_phase))

    def flush_dma(self):
        pend = [(nm, self.cnt[nm]) for nm in list(self.cnt)
                if nm.startswith("q_") and self.cnt[nm] > 0]

        def fl(eng, pend=pend):
            for nm, v in pend:
                eng.wait_ge(self.sems[nm], v)
        self.events.append(("sync", fl, self.cur_phase))

    def barrier(self):
        self.flush_dma()
        vals = {}
        for e in self.ENGINES:
            sn = f"bar_{e}"
            vals[sn] = self.bump(sn, 1)
            sem = self.sems[sn]

            def post(eng, sem=sem):
                eng.drain()
                return eng.nop().then_inc(sem, 1)
            self.events.append((e, post, self.cur_phase))
        allv = dict(vals)

        def waitall(eng, allv=allv):
            for sn, v in allv.items():
                eng.wait_ge(self.sems[sn], v)
        for e in self.ENGINES:
            self.events.append((e, waitall, self.cur_phase))
        return allv

    def replay(self, block):
        evs = self.events
        nc = self.nc

        def run(eng, sel, wrap=False):
            d = DrainingEngine(eng) if wrap else eng
            cur = None
            sid = None
            for (e, fn, ph) in evs:
                if e != sel:
                    continue
                if ph != cur:
                    if cur is not None:
                        nc.leave_named_scope(cur, sid, False)
                    cur = ph
                    sid, _ = nc.enter_named_scope(cur, False)
                fn(d)
            if cur is not None:
                nc.leave_named_scope(cur, sid, False)

        @block.sync
        def _(eng):
            run(eng, "sync")

        @block.tensor
        def _(eng):
            run(eng, "tensor")

        @block.vector
        def _(eng):
            run(eng, "vector", wrap=True)

        @block.scalar
        def _(eng):
            run(eng, "scalar", wrap=True)

        @block.gpsimd
        def _(eng):
            run(eng, "gpsimd", wrap=True)


class Alloc:

    def __init__(self, nc):
        self.nc = nc
        self.groups = {}

    def sb(self, group, name, shape, dt=F32):
        g = self.groups.setdefault(group, [])
        cm = self.nc.sbuf_tensor(name, list(shape), dt)
        h = cm.__enter__()
        nb = shape[1] * (2 if dt == BF16 else 4)
        if not hasattr(self, "cur"):
            self.cur, self.peak, self.peak_at = 0, 0, ""
        self.cur += nb
        if self.cur > self.peak:
            self.peak, self.peak_at = self.cur, name
        g.append((cm, nb))
        return h

    def free(self, group):
        for cm, nb in reversed(self.groups.get(group, [])):
            cm.__exit__(None, None, None)
            self.cur -= nb
        self.groups[group] = []


def host_inputs(inp, ntok=2048):
    nslice = 4
    x = inp["L0_attn_x"]
    B, T, _ = x.shape
    f32 = np.float32

    def cc(a):
        return np.ascontiguousarray(a.astype(f32))

    convall = np.concatenate([
        inp["L0_attn_q_conv_w"][:, 0, :],
        inp["L0_attn_k_conv_w"][:, 0, :],
        inp["L0_attn_v_conv_w"][:, 0, :],
    ], axis=0)
    convw = cc(convall.reshape(32, 128, 4).transpose(1, 0, 2).reshape(128, 128))
    cm = inp["L0_c"]
    c_fm = cc(cm.reshape(2, 16, 128).transpose(2, 1, 0).reshape(128, 32))
    ident = np.eye(128, dtype=f32)
    identbf = np.eye(128, dtype=ml_dtypes.bfloat16)
    ar = np.arange(L)
    mneg_s = np.where(ar[:, None] < ar[None, :], 0.0, -1e30).astype(f32)
    mneg_i = np.where(ar[:, None] <= ar[None, :], 0.0, -1e30).astype(f32)
    mneg_sT = np.ascontiguousarray(mneg_s.T)

    def mask4(m):
        return np.ascontiguousarray(np.tile(m, (1, 4)))
    headmask = np.zeros((128, 2), f32)
    headmask[:64, 0] = 1.0
    headmask[64:, 1] = 1.0
    maskT_q = np.zeros((2, 128), f32)
    maskT_q[0, :64] = DK ** -0.5
    maskT_q[1, 64:] = DK ** -0.5
    maskT_k = (maskT_q > 0).astype(f32)
    ohsel = np.zeros((16, 2048), f32)
    for hh in range(16):
        ohsel[hh, hh * 128:(hh + 1) * 128] = 1.0

    shared = {
        "qkT": cc(inp["L0_attn_qk_w"].T), "vgT": cc(inp["L0_attn_vg_w"].T),
        "owT": cc(inp["L0_attn_o_w"].T),
        "abT": cc(np.concatenate([inp["L0_attn_a_w"].T, inp["L0_attn_b_w"].T], axis=1)),
        "convw": convw, "c_fm": c_fm,
        "b_bias": cc(inp["L0_attn_b_bias"].reshape(16, 1)),
        "beta_bias": cc(inp["L0_attn_beta_bias"].reshape(16, 1)),
        "dt_bias": cc(inp["L0_attn_dt_bias"].reshape(16, 1)),
        "A_log": cc(inp["L0_attn_A_log"].reshape(16, 1)),
        "g1v": cc(inp["L0_ffn_g1"].reshape(1, C)),
        "g2v": cc(inp["L0_ffn_g2"].reshape(1, C)),
        "g3v": cc(inp["L0_ffn_g3"].reshape(1, HID)),
        "ident": ident, "identbf": identbf,
        "mneg_s4": mask4(mneg_s), "mneg_i4": mask4(mneg_i), "mneg_sT4": mask4(mneg_sT),
        "headmask": headmask, "maskT_q": maskT_q, "maskT_k": maskT_k,
        "ones128x1": np.ones((128, 1), f32), "onesrow": np.ones((16, L), f32),
        "ohsel": ohsel,
    }
    in_maps = []
    for core in range(NCORES):
        b, p = core // nslice, core % nslice
        t0 = p * ntok
        m = dict(shared)
        xs = np.zeros((ntok + 3, C), f32)
        if t0 >= 3:
            xs[:] = x[b, t0 - 3:t0 + ntok]
        else:
            xs[3 - t0:] = x[b, 0:t0 + ntok]
        m["x_sl"] = cc(xs)
        m["haloflag"] = np.array([[0.0 if t0 == 0 else 1.0]], f32)
        m["wmod_sl"] = cc(inp["L0_wmod"][:, core * 1536:(core + 1) * 1536])
        selb = np.zeros((1, 2), f32)
        selb[0, b] = 1.0
        m["selb"] = selb
        selp = np.zeros((1, 24), f32)
        for j in range(3):
            r = core - 1 - j
            if r >= b * nslice:
                selp[0, j * 8 + r] = 1.0
        m["selp"] = selp
        m["w1T_sl"] = cc(inp["L0_ffn_w1"].T[core * 256:(core + 1) * 256])
        m["w2T_sl"] = cc(inp["L0_ffn_w2"].T[core * 256:(core + 1) * 256])
        m["w3T_sl"] = cc(inp["L0_ffn_w3"].T[core * 1024:(core + 1) * 1024])
        in_maps.append(m)
    return in_maps


CONSTS_CORE = [
    ("identbf", [128, 128], BF16),
    ("ones128x1", [128, 1], F32), ("onesrow", [16, L], F32),
    ("selb", [1, 2], F32), ("headmask", [128, 2], F32),
    ("b_bias", [16, 1], F32), ("beta_bias", [16, 1], F32),
    ("dt_bias", [16, 1], F32), ("A_log", [16, 1], F32),
]
CONSTS_P1 = [
    ("convw", [128, 128], F32), ("c_fm", [128, 32], F32),
    ("ident", [128, 128], F32),
    ("mneg_s4", [128, 512], F32), ("mneg_i4", [128, 512], F32), ("mneg_sT4", [128, 512], F32),
    ("maskT_q", [2, 128], F32), ("maskT_k", [2, 128], F32),
    ("ohsel", [16, 2048], F32),
]
CONSTS = CONSTS_CORE + CONSTS_P1


def build(ntok=2048, debug_taps=(), maxphase=99):
    nch = ntok // L
    NTP = ntok + 4
    nc = bass.Bass("TRN2", target_bir_lowering=False, debug=False)

    def din(name, shape, dt=F32):
        return nc.dram_tensor(name, list(shape), dt, kind="ExternalInput").ap()

    def dout(name, shape, dt=F32):
        return nc.dram_tensor(name, list(shape), dt, kind="ExternalOutput").ap()

    I = {}
    I["x_sl"] = din("x_sl", [ntok + 3, C])
    I["qkT"] = din("qkT", [C, C])
    I["vgT"] = din("vgT", [C, 2 * VAL_DIM])
    I["owT"] = din("owT", [VAL_DIM, C])
    I["abT"] = din("abT", [C, 32])
    I["g1v"] = din("g1v", [1, C])
    I["g2v"] = din("g2v", [1, C])
    I["g3v"] = din("g3v", [1, HID])
    I["wmod_sl"] = din("wmod_sl", [C, 1536])
    I["w1T_sl"] = din("w1T_sl", [256, HID])
    I["w2T_sl"] = din("w2T_sl", [256, HID])
    I["w3T_sl"] = din("w3T_sl", [1024, C])
    for (nm, shape, dt) in CONSTS:
        I[nm] = din(nm, shape, dt)
    I["haloflag"] = din("haloflag", [1, 1])
    I["selp"] = din("selp", [1, 24])

    O = {"y": dout("y", [ntok, C])}
    taps = {}

    def mktap(name, shape, dt=F32):
        if name in debug_taps:
            taps[name] = dout("tap_" + name, list(shape), dt)

    mktap("h_fm", [128, 16 * NTP], BF16)
    mktap("modvec", [8, C])
    mktap("q_fm", [128, 8 * ntok], BF16)
    mktap("k_fm", [128, 8 * ntok], BF16)
    mktap("vsp", [nch * 16 * 128, 128], BF16)
    mktap("gsp", [VAL_DIM, ntok], BF16)
    mktap("cum", [128, nch * 16])
    mktap("beta", [128, nch * 16])
    mktap("bg", [128, nch * 16])
    mktap("kbs", [128, nch * 16])
    mktap("glrow", [1, nch * 16])
    mktap("negT", [64, nch * 16 * 64], BF16)
    mktap("Yc", [64, nch * 16 * 128], BF16)
    mktap("ouT", [nch * 16 * 128, 128], BF16)
    mktap("zT", [nch * 16 * 64, 128], BF16)
    mktap("pq", [H * 64, 192], BF16)
    mktap("sinit", [64, 16 * 128])
    mktap("x1", [ntok, C])
    mktap("i1", [16 * nch * 128, 128], BF16)
    mktap("hmid", [ntok, HID], BF16)
    mktap("f2sc", [128, 6 * 16])
    mktap("i3h", [128, 64 * 128], BF16)

    Dmodpart = nc.dram_tensor("Dmodpart", [2, 1536], F32)
    Dmodgath = nc.dram_tensor("Dmodgath", [16, 1536], F32)
    Dmodvec = nc.dram_tensor("Dmodvec", [8, C], F32)
    Dwsum_in = nc.dram_tensor("Dwsum_in", [1, 4], F32)
    Dwsum_out = nc.dram_tensor("Dwsum_out", [1, 4], F32)
    Dwq_in = nc.dram_tensor("Dwq_in", [768, HID], BF16)
    Dwq_gath = nc.dram_tensor("Dwq_gath", [NCORES * 768, HID], BF16)
    Dpq_in = nc.dram_tensor("Dpq_in", [H * 64, 192], BF16)
    Dpq_gath = nc.dram_tensor("Dpq_gath", [NCORES * H * 64, 192], BF16)
    Dvsp = nc.dram_tensor("Dvsp", [nch * 16 * 128, 128], BF16)
    Dgsp = nc.dram_tensor("Dgsp", [VAL_DIM, ntok], BF16)
    Dou = nc.dram_tensor("Dou", [nch * 16 * 128, 128], BF16)
    Dzt = nc.dram_tensor("Dzt", [nch * 16 * 64, 128], BF16)
    Dx1 = nc.dram_tensor("Dx1", [ntok, C], F32)
    Di1 = nc.dram_tensor("Di1", [16 * nch * 128, 128], BF16)
    Di2 = nc.dram_tensor("Di2", [16 * nch * 128, 128], BF16)
    Dh = nc.dram_tensor("Dh", [ntok, HID], BF16)
    Dq = nc.dram_tensor("Dq", [1024, ntok], BF16)
    Dk = nc.dram_tensor("Dk", [1024, ntok], BF16)
    DnegT = nc.dram_tensor("DnegT", [nch * 16 * 64, 64], BF16)
    Dog = nc.dram_tensor("Dog", [nch * 128, 16 * 128], BF16)

    S = Sched(nc)
    A = Alloc(nc)
    for s in ["ld", "st", "gld", "gst", "pe", "dve", "act", "pool",
              "cc_mod", "cc_wsum", "cc_wq", "cc_pq"]:
        S.sem(s)
    for e in S.ENGINES:
        S.sem(f"bar_{e}")

    cst = {nm: A.sb("const", "c_" + nm, shape, dt) for (nm, shape, dt) in CONSTS_CORE}
    cst["haloflag"] = A.sb("const", "c_haloflag", [128, 1], F32)
    cst["selp"] = A.sb("const", "c_selp", [128, 24], F32)
    cvals = A.sb("const", "cvals", [128, 8], F32)
    smallv = A.sb("const", "smallv", [16, 16], F32)
    fsc = A.sb("const", "fsc", [1, 16], F32)
    fscb = A.sb("const", "fscb", [128, 8], F32)
    thb = A.sb("const", "thb", [128, 8], F32)
    s1f = A.sb("fsc2", "s1f", [128, nch], F32)
    s2f = A.sb("fsc2", "s2f", [128, nch], F32)
    s3f = A.sb("fsc2", "s3f", [128, nch], F32)
    mx3 = A.sb("fsc2", "mx3", [128, nch], F32)
    ss3 = A.sb("fsc2", "ss3", [128, nch], F32)
    sc3 = A.sb("fsc2", "sc3", [128, 4], F32)
    st_cum = A.sb("stats", "st_cum", [128, nch * 16], F32)
    st_beta = A.sb("stats", "st_beta", [128, nch * 16], F32)
    st_bg = A.sb("stats", "st_bg", [128, nch * 16], F32)
    st_kbs = A.sb("stats", "st_kbs", [128, nch * 16], F32)
    st_cumb = A.sb("stats", "st_cumb", [128, nch * 16], F32)
    gl_row = A.sb("stats", "gl_row", [1, nch * 16], F32)
    glb = A.sb("stats", "glb", [128, nch * 16], F32)
    Ycs = A.sb("stats", "Ycs", [64, 2 * 128], BF16)
    for _nm, _shape, _dt in CONSTS_P1:
        cst[_nm] = A.sb("constp1", "c_" + _nm, _shape, _dt)

    pb = [nc.psum_tensor(f"pb{i}", [128, 512], F32).__enter__() for i in range(6)]
    pbf = nc.psum_tensor("pbf", [128, 2048], BF16).__enter__()
    bank_guard = {}

    def dma(engine, out, in_, waits=(), sem=None, stream=None):
        if stream is not None:
            sem = "q_" + stream
        if sem is None:
            sem = "gld" if engine == "gpsimd" else "ld"
        S.sem(sem)
        S.emit(engine, lambda eng, o=out, i=in_: eng.dma_start(out=o, in_=i), waits=waits, inc=(sem, 16))
        return (sem, S.val(sem))

    def cinit(eng):
        eng.memset(cvals[:, 0:1], 1e-5)
        eng.memset(cvals[:, 1:2], 1e-6)
        eng.memset(cvals[:, 2:3], EPS_ORMS)
        eng.memset(cvals[:, 3:4], 1e-8)
        return eng.memset(cvals[:, 4:5], 1.0)
    S.emit("vector", cinit, inc=("dve", 1))
    cinit_v = S.val("dve")
    for nm, t in cst.items():
        if nm in ("haloflag", "selp"):
            dma("sync", t[:, :], I[nm].partition_broadcast(128), stream="c")
        else:
            dma("sync", t[:, :], I[nm], stream="c")
    ld_consts = ("q_c", S.val("q_c"))

    S.phase("mod")
    g0 = "ph0"
    mwork = A.sb(g0, "mwork", [128, 8192], BF16)
    mfp = A.sb(g0, "mfp", [128, 8192], F32)
    cbf = A.sb(g0, "cbf", [128, 32], BF16)
    mrow = A.sb(g0, "mrow", [1, 12288], F32)
    mr2 = A.sb(g0, "mr2", [1, 2048], F32)
    mr3 = A.sb(g0, "mr3", [1, 2048], F32)
    g2t = A.sb(g0, "g2t", [2, 1536], F32)
    g2a = A.sb(g0, "g2a", [1, 1536], F32)
    g2b = A.sb(g0, "g2b", [1, 1536], F32)

    S.emit("scalar", lambda eng: eng.activation(cst["c_fm"][:, :], cst["c_fm"][:, :], AF.Silu),
           waits=[ld_consts], inc=("act", 1))
    S.emit("scalar", lambda eng: eng.copy(cbf[:, :], cst["c_fm"][:, :]), inc=("act", 1))
    silu_c = S.val("act")

    prev_pe = 0
    prev_cast = 0
    modbank_dve = [0, 0]
    for j in range(3):
        src = I["wmod_sl"][:, j * 512:(j + 1) * 512].rearrange("(ic p) o -> p ic o", p=128)
        ldv = dma("sync", mfp[:, :].rearrange("p (ic o) -> p ic o", ic=16), src,
                  waits=[("pe", prev_pe), ("act", prev_cast)], stream="mfp")
        S.emit("scalar", lambda eng: eng.copy(mwork[:, :], mfp[:, :]), waits=[ldv], inc=("act", 1))
        cv = S.val("act")
        prev_cast = cv

        def mm(eng, j=j):
            last = None
            for ic in range(16):
                last = eng.matmul(pb[j % 2][0:2, :], cbf[:, 2 * ic:2 * ic + 2],
                                  mwork[:, ic * 512:(ic + 1) * 512],
                                  start=(ic == 0), stop=(ic == 15))
            return last
        S.emit("tensor", mm, waits=[("act", cv), ("act", silu_c), ("dve", modbank_dve[j % 2])],
               inc=("pe", 1))
        prev_pe = S.val("pe")
        S.emit("vector", lambda eng, j=j: eng.tensor_copy(g2t[0:2, j * 512:(j + 1) * 512], pb[j % 2][0:2, :]),
               waits=[("pe", prev_pe)], inc=("dve", 1))
        modbank_dve[j % 2] = S.val("dve")
    mod_st = dma("sync", Dmodpart.ap(), g2t[0:2, :], waits=[("dve", S.val("dve"))], stream="stmod")
    S.emit("gpsimd", lambda eng: eng.collective_compute(
        "AllGather", ALU.bypass, replica_groups=[list(range(NCORES))],
        ins=[Dmodpart.ap().opt()], outs=[Dmodgath.ap().opt()]),
        waits=[mod_st], inc=("cc_mod", 1))
    for r in range(NCORES):
        dma("sync", g2a[0:1, :], Dmodgath.ap()[2 * r:2 * r + 1, :],
            waits=[("cc_mod", 1), ("dve", S.val("dve"))], stream="gath")
        ldv = dma("sync", g2b[0:1, :], Dmodgath.ap()[2 * r + 1:2 * r + 2, :], stream="gath")

        def blend(eng, r=r):
            eng.tensor_scalar(mr2[0:1, 0:1536], g2a[0:1, 0:1536], cst["selb"][0:1, 0:1], None, ALU.mult)
            return eng.scalar_tensor_tensor(mrow[0:1, r * 1536:(r + 1) * 1536], g2b[0:1, 0:1536],
                                            cst["selb"][0:1, 1:2], mr2[0:1, 0:1536], ALU.mult, ALU.add)
        S.emit("vector", blend, waits=[ldv], inc=("dve", 1))

    def vs(i):
        return mrow[0:1, i * 2048:(i + 1) * 2048]
    S.emit("vector", lambda eng: eng.tensor_scalar(vs(1), vs(1), 1.0, None, ALU.add), inc=("dve", 1))
    S.emit("vector", lambda eng: eng.tensor_scalar(vs(4), vs(4), 1.0, None, ALU.add), inc=("dve", 1))
    for iv in (2, 5):
        def clamp(eng, iv=iv):
            eng.tensor_scalar(mr2[0:1, :], vs(iv), 0.0, None, ALU.is_ge)
            eng.tensor_scalar(mr2[0:1, :], mr2[0:1, :], 2.0, -1.0, ALU.mult, ALU.add)
            eng.tensor_scalar(mr3[0:1, :], vs(iv), -1.0, None, ALU.mult)
            eng.tensor_tensor(mr3[0:1, :], mr3[0:1, :], vs(iv), ALU.max)
            eng.tensor_scalar(mr3[0:1, :], mr3[0:1, :], 0.25, None, ALU.max)
            return eng.tensor_mul(vs(iv), mr2[0:1, :], mr3[0:1, :])
        S.emit("vector", clamp, inc=("dve", 1))
    modproc = S.val("dve")
    for i, v in enumerate([1, 0, 2, 4, 3, 5]):
        modvec_st = dma("sync", Dmodvec.ap()[i:i + 1, :], vs(v), waits=[("dve", modproc)], stream="stmv")
    if "modvec" in taps:
        for i in range(6):
            dma("sync", taps["modvec"][i:i + 1, :], Dmodvec.ap()[i:i + 1, :], waits=[modvec_st], stream="sttap")
    A.free(g0)
    bar0 = S.barrier()

    S.phase("wq")
    Dfsc = nc.dram_tensor("Dfsc", [1, 16], F32)
    gw = "phW"
    wfq = A.sb(gw, "wfq", [128, HID], F32)
    wqo = A.sb(gw, "wqo", [128, HID], BF16)
    wsg = A.sb(gw, "wsg", [128, HID], F32)
    wcs = A.sb(gw, "wcs", [128, 4], F32)
    wrow = A.sb(gw, "wrow", [1, 32], F32)

    S.emit("vector", lambda eng: eng.memset(wcs[:, :], 0.0), inc=("dve", 1))
    wslices = [("w1T_sl", I["w1T_sl"], 0), ("w2T_sl", I["w2T_sl"], 1), ("w3T_sl", I["w3T_sl"], 2)]
    prev_red = S.val("dve")
    for nm, ap, col in wslices:
        flat = ap.rearrange("a b -> (a b)") if nm == "w3T_sl" else None
        for half in range(2):
            if nm == "w3T_sl":
                srcv = ap.rearrange("(x a) b -> x (a b)", x=2)[half:half + 1, :].rearrange(
                    "x (p c) -> (x p) c", p=128)
            else:
                srcv = ap[half * 128:(half + 1) * 128, :]
            ldv = dma("sync", wfq[:, :], srcv, waits=[("dve", prev_red)], stream="wq")
            S.emit("vector", lambda eng: eng.tensor_reduce(wcs[:, 3:4], wfq[:, :], AX.X, ALU.add,
                                                           apply_absolute_value=True),
                   waits=[ldv], inc=("dve", 1))
            S.emit("vector", lambda eng, col=col: eng.tensor_add(wcs[:, col:col + 1], wcs[:, col:col + 1],
                                                                 wcs[:, 3:4]),
                   inc=("dve", 1))
            prev_red = S.val("dve")
    S.emit("tensor", lambda eng: eng.matmul(pb[0][0:1, 0:4], cst["ones128x1"][:, 0:1], wcs[:, 0:4],
                                            start=True, stop=True),
           waits=[("dve", prev_red), ("dve", S.val("dve"))], inc=("pe", 1))
    S.emit("vector", lambda eng: eng.tensor_scalar(wrow[0:1, 0:4], pb[0][0:1, 0:4], 1.0 / (HID * C), None,
                                                   ALU.mult),
           waits=[("pe", S.val("pe"))], inc=("dve", 1))
    wsum_st = dma("sync", Dwsum_in.ap(), wrow[0:1, 0:4], waits=[("dve", S.val("dve"))], stream="stws")
    S.emit("gpsimd", lambda eng: eng.collective_compute(
        "AllReduce", ALU.add, replica_groups=[list(range(NCORES))],
        ins=[Dwsum_in.ap().opt()], outs=[Dwsum_out.ap().opt()]),
        waits=[wsum_st], inc=("cc_wsum", 1))
    ldw = dma("sync", wrow[0:1, 0:4], Dwsum_out.ap(), waits=[("cc_wsum", 1)], stream="wq")
    def wstats(eng):
        eng.tensor_scalar(wrow[0:1, 4:8], wrow[0:1, 0:4], 1e-8, ATANH_HALF, ALU.add, ALU.mult)
        eng.tensor_scalar(wrow[0:1, 8:12], wrow[0:1, 4:8], -1.0, None, ALU.mult)
        return eng.tensor_scalar(wrow[0:1, 12:16], wrow[0:1, 0:4], -1.0, None, ALU.mult)
    S.emit("vector", wstats, waits=[ldw], inc=("dve", 1))
    S.emit("scalar", lambda eng: eng.activation(wrow[0:1, 16:20], wrow[0:1, 0:4], AF.Ln, bias=1.0),
           waits=[("dve", S.val("dve"))], inc=("act", 1))
    S.emit("scalar", lambda eng: eng.activation(wrow[0:1, 20:24], wrow[0:1, 12:16], AF.Ln, bias=1.0),
           inc=("act", 1))

    def wfin(eng):
        eng.tensor_sub(wrow[0:1, 24:28], wrow[0:1, 16:20], wrow[0:1, 20:24])
        return eng.tensor_scalar(wrow[0:1, 24:28], wrow[0:1, 24:28], 0.5 / 127.0 * 0.5, None, ALU.mult)
    S.emit("vector", wfin, waits=[("act", S.val("act"))], inc=("dve", 1))
    wv = S.val("dve")
    dma("sync", Dfsc.ap()[0:1, 0:4], wrow[0:1, 24:28], waits=[("dve", wv)], stream="stfsc")
    dma("sync", Dfsc.ap()[0:1, 4:8], wrow[0:1, 8:12], waits=[("dve", wv)], stream="stfsc")
    fsc_st = dma("sync", Dfsc.ap()[0:1, 8:12], wrow[0:1, 4:8], waits=[("dve", wv)], stream="stfsc")
    dma("sync", fscb[:, 0:4], Dfsc.ap()[0:1, 0:4].partition_broadcast(128), waits=[fsc_st], stream="thld")
    ldth = dma("sync", thb[:, 0:8], Dfsc.ap()[0:1, 4:12].partition_broadcast(128), waits=[fsc_st], stream="thld")
    fscb_ld = ldth
    state_wq = {}
    prev_q = 0
    for nm, ap, col in wslices:
        for half in range(2):
            if nm == "w3T_sl":
                srcv = ap.rearrange("(x a) b -> x (a b)", x=2)[half:half + 1, :].rearrange(
                    "x (p c) -> (x p) c", p=128)
            else:
                srcv = ap[half * 128:(half + 1) * 128, :]
            ldv = dma("sync", wfq[:, :], srcv, waits=[("act", prev_q), ("dve", S.val("dve"))], stream="wq")
            S.emit("scalar", lambda eng, col=col: eng.activation(wsg[:, :], wfq[:, :], AF.Sign,
                                                                 bias=thb[:, col:col + 1]),
                   waits=[ldv, ldth], inc=("act", 1))
            S.emit("scalar", lambda eng, col=col: eng.activation(wfq[:, :], wfq[:, :], AF.Sign,
                                                                 bias=thb[:, 4 + col:4 + col + 1]),
                   inc=("act", 1))
            prev_q = S.val("act")
            wtq = [("act", prev_q)]
            if state_wq.get("st"):
                wtq.append(state_wq["st"])
            S.emit("vector", lambda eng: eng.tensor_add(wqo[:, :], wsg[:, :], wfq[:, :]),
                   waits=wtq, inc=("dve", 1))
            state_wq["st"] = dma("sync", Dwq_in.ap()[(col * 2 + half) * 128:(col * 2 + half + 1) * 128, :],
                                 wqo[:, :], waits=[("dve", S.val("dve"))], stream="stwq")
    S.emit("gpsimd", lambda eng: eng.collective_compute(
        "AllGather", ALU.bypass, replica_groups=[list(range(NCORES))],
        ins=[Dwq_in.ap().opt()], outs=[Dwq_gath.ap().opt()]),
        waits=[("q_stwq", S.val("q_stwq"))], inc=("cc_wq", 1))
    barW = S.barrier()
    A.free(gw)
    hm_cum = A.sb("hm", "hm_cum", [16, ntok], F32)
    hm_cumb = A.sb("hm", "hm_cumb", [16, ntok], F32)
    hm_expc = A.sb("hm", "hm_expc", [16, ntok], F32)

    if maxphase < 1:
        _finish(nc, S, O, ntok)
        return nc, taps

    S.phase("A1")
    h_fm = A.sb("hfm", "h_fm", [128, 16 * NTP], BF16)
    S.emit("vector", lambda eng: eng.memset(h_fm[:, :], 0.0), inc=("dve", 1))
    hfm_init = S.val("dve")
    gA = "phA1"
    xt = A.sb(gA, "xt", [128, C], F32)
    ht2 = A.sb(gA, "ht2", [128, C], F32)
    htb = A.sb(gA, "htb", [128, C], BF16)
    stt1 = A.sb(gA, "stt1", [128, 8], F32)
    mbc0 = A.sb(gA, "mbc0", [128, C], F32)
    mbc1 = A.sb(gA, "mbc1", [128, C], F32)

    dma("sync", mbc0[:, :], Dmodvec.ap()[0:1, :].partition_broadcast(128), waits=[modvec_st], stream="mbc")
    dma("sync", mbc1[:, :], Dmodvec.ap()[1:2, :].partition_broadcast(128), waits=[modvec_st], stream="mbc")
    ld_mod = ("q_mbc", S.val("q_mbc"))

    prev_evac = 0
    prev_ld = 0

    def ln_tile(rows, roff, coloff, is_halo):
        nonlocal prev_evac, prev_ld
        ldv = dma("sync", xt[0:rows, :], I["x_sl"][roff:roff + rows, :],
                  waits=[ld_mod, ("dve", prev_ld)], stream="x")
        prev_ld_stats = S.val("dve")
        S.emit("vector", lambda eng: eng.reduce_sum(stt1[0:rows, 0:1], xt[0:rows, :], AX.X),
               waits=[ldv], inc=("dve", 1))
        S.emit("scalar", lambda eng: eng.activation(ht2[0:rows, :], xt[0:rows, :], AF.Square,
                                                    accum_out=stt1[0:rows, 1:2]),
               waits=[ldv], inc=("act", 1))
        vact = S.val("act")

        def stats(eng):
            st = stt1
            eng.tensor_scalar(st[0:rows, 0:1], st[0:rows, 0:1], 1.0 / C, None, ALU.mult)
            eng.tensor_scalar(st[0:rows, 2:3], st[0:rows, 1:2], 1.0 / C, None, ALU.mult)
            eng.tensor_mul(st[0:rows, 3:4], st[0:rows, 0:1], st[0:rows, 0:1])
            return eng.tensor_sub(st[0:rows, 2:3], st[0:rows, 2:3], st[0:rows, 3:4])
        S.emit("vector", stats, waits=[("act", vact)], inc=("dve", 1))
        S.emit("scalar", lambda eng: eng.activation(stt1[0:rows, 4:5], stt1[0:rows, 2:3], AF.Sqrt,
                                                    bias=cvals[0:rows, 0:1]),
               waits=[("dve", S.val("dve")), ("dve", cinit_v)], inc=("act", 1))
        S.emit("vector", lambda eng: eng.reciprocal(stt1[0:rows, 5:6], stt1[0:rows, 4:5]),
               waits=[("act", S.val("act"))], inc=("dve", 1))
        S.emit("vector", lambda eng: eng.tensor_scalar(xt[0:rows, :], xt[0:rows, :],
                                                       stt1[0:rows, 0:1], stt1[0:rows, 5:6],
                                                       ALU.subtract, ALU.mult),
               inc=("dve", 1))
        S.emit("vector", lambda eng: eng.tensor_mul(ht2[0:rows, :], xt[0:rows, :], mbc0[0:rows, :]),
               inc=("dve", 1))
        if is_halo:
            S.emit("vector", lambda eng: eng.tensor_add(ht2[0:rows, :], ht2[0:rows, :], mbc1[0:rows, :]),
                   inc=("dve", 1))
            S.emit("vector", lambda eng: eng.tensor_scalar(htb[0:rows, :], ht2[0:rows, :],
                                                           cst["haloflag"][0:rows, 0:1], None, ALU.mult),
                   inc=("dve", 1))
        else:
            S.emit("vector", lambda eng: eng.tensor_tensor(htb[0:rows, :], ht2[0:rows, :], mbc1[0:rows, :], ALU.add),
                   inc=("dve", 1))
        vdve = S.val("dve")

        def transp(eng):
            last = None
            for ic in range(16):
                last = eng.transpose(pbf[:, ic * 128: ic * 128 + rows],
                                     htb[0:rows, ic * 128:(ic + 1) * 128], cst["identbf"][0:rows, 0:rows])
            return last
        S.emit("tensor", transp, waits=[("dve", vdve), ("dve", max(prev_evac, hfm_init))], inc=("pe", 1))
        vpe = S.val("pe")

        def evac(eng):
            last = None
            for ic in range(16):
                src = pbf[:, ic * 128: ic * 128 + rows]
                last = eng.tensor_copy(h_fm[:, ic * NTP + coloff: ic * NTP + coloff + rows], src)
            return last
        S.emit("vector", evac, waits=[("pe", vpe)], inc=("dve", 1))
        prev_evac = S.val("dve")
        prev_ld = prev_ld_stats
        prev_ld = prev_evac

    ln_tile(3, 0, 0, True)
    for t in range(ntok // 128):
        ln_tile(128, 3 + t * 128, 3 + t * 128, False)
    A.free(gA)
    barA1 = S.barrier()
    if "h_fm" in taps:
        dma("sync", taps["h_fm"], h_fm[:, :], stream="sttap")

    if maxphase < 2:
        _finish(nc, S, O, ntok)
        return nc, taps

    S.phase("A2")
    gB0 = "phA2w"
    wbf = A.sb(gB0, "wbf", [128, 2 * 2048], BF16)
    gB1 = "phA2s"
    raw = A.sb(gB1, "raw", [128, ntok + 3], F32)
    cvb = A.sb(gB1, "cvb", [128, ntok], F32)
    sqb = A.sb(gB1, "sqb", [128, 2 * 512], F32)
    nrm = A.sb(gB1, "nrm", [2, ntok], F32)
    bfnt4 = A.sb(gB1, "bfnt4", [128, 512], BF16)
    bf4 = A.sb(gB1, "bf4", [128, 512], BF16)
    qst = A.sb(gB1, "qst", [128, 2 * 512], BF16)

    ncols = ntok + 3
    grs = []
    c0 = 0
    while c0 < ncols:
        grs.append((c0, min(512, ncols - c0)))
        c0 += 512

    state = {"oc_done_dve": 0, "oc_done_pool": 0, "oc_done_st": 0, "oc_done_act": 0,
             "b01_dve": [0, 0], "vst": [], "vslot": 0, "vbank_dve": [0, 0],
             "sq_pe": [0, 0], "vtp_pe": [0, 0, 0, 0], "g_slot_st": [None, None],
             "qk_st": [None, None], "wslot": 0, "wslot_pe": [0, 0]}

    def proj_oc(kind, oc):
        wsrc, wcol = (I["qkT"], oc * 128) if kind in "qk" else (I["vgT"], oc * 128)
        if kind == "k":
            wcol += KEY_DIM
        if kind == "g":
            wcol += VAL_DIM
        src = wsrc[:, wcol:wcol + 128].rearrange("(ic p) o -> p ic o", p=128)
        wsl = state["wslot"] % 2
        state["wslot"] += 1
        ldv = dma("gpsimd", wbf[:, wsl * 2048:(wsl + 1) * 2048].rearrange("p (ic o) -> p ic o", ic=16),
                  src, waits=[("pe", state["wslot_pe"][wsl])], sem=f"wfg{wsl}")
        for gi, (g0c, w) in enumerate(grs):
            bk = gi % 2

            def mm(eng, g0c=g0c, w=w, bk=bk, wsl=wsl):
                last = None
                for ic in range(16):
                    last = eng.matmul(pb[bk][:, 0:w], wbf[:, wsl * 2048 + ic * 128: wsl * 2048 + (ic + 1) * 128],
                                      h_fm[:, ic * NTP + g0c: ic * NTP + g0c + w],
                                      start=(ic == 0), stop=(ic == 15))
                return last
            S.emit("tensor", mm, waits=[ldv, ("dve", state["b01_dve"][bk])], inc=("pe", 1))
            S.emit("vector", lambda eng, g0c=g0c, w=w, bk=bk: eng.tensor_copy(raw[:, g0c:g0c + w], pb[bk][:, 0:w]),
                   waits=[("pe", S.val("pe")), ("pool", state["oc_done_pool"]),
                          ("dve", state["oc_done_dve"])], inc=("dve", 1))
            state["b01_dve"][bk] = S.val("dve")
        state["wslot_pe"][wsl] = S.val("pe")
        rawv = S.val("dve")

        if kind == "g":
            wt_g = [("dve", rawv)]
            if state.get("last_g_st"):
                wt_g.append(state["last_g_st"])
            S.emit("scalar", lambda eng: eng.activation(cvb[:, 0:ntok], raw[:, 3:3 + ntok], AF.Silu),
                   waits=wt_g, inc=("act", 1))
            sil_g = S.val("act")
            state["oc_done_act"] = sil_g
            gslots = (bf4, bfnt4)
            for gi2, g0c in enumerate(range(0, ntok, 512)):
                w = min(512, ntok - g0c)
                gsl = gi2 % 2
                wt2 = [("act", sil_g)]
                if state["g_slot_st"][gsl] is not None:
                    wt2.append(state["g_slot_st"][gsl])
                else:
                    wt2 += state["vst"][-4:] + [("pe", S.val("pe"))]
                S.emit("vector", lambda eng, g0c=g0c, w=w, gsl=gsl: eng.tensor_copy(
                    gslots[gsl][:, 0:w], cvb[:, g0c:g0c + w]), waits=wt2, inc=("dve", 1))
                state["g_slot_st"][gsl] = dma("sync", Dgsp.ap()[oc * 128:(oc + 1) * 128, g0c:g0c + w],
                                              gslots[gsl][:, 0:w], waits=[("dve", S.val("dve"))],
                                              stream=f"stg{gsl}")
            state["oc_done_dve"] = S.val("dve")
            state["last_g_st"] = state["g_slot_st"][1] or state["g_slot_st"][0]
            return
        cg = {"q": 0, "k": 8, "v": 16}[kind] + oc

        def conv(eng, cg=cg):
            cw = cst["convw"]
            eng.tensor_scalar(cvb[:, 0:ntok], raw[:, 3:3 + ntok], cw[:, cg * 4 + 3:cg * 4 + 4], None, ALU.mult)
            for j in (2, 1, 0):
                last = eng.scalar_tensor_tensor(cvb[:, 0:ntok], raw[:, j:j + ntok],
                                                cw[:, cg * 4 + j:cg * 4 + j + 1], cvb[:, 0:ntok],
                                                ALU.mult, ALU.add)
            return last
        wt_conv = [("dve", rawv), ("act", state["oc_done_act"]), ld_consts]
        if state.get("last_g_st"):
            wt_conv.append(state["last_g_st"])
        S.emit("vector", conv, waits=wt_conv, inc=("dve", 1))
        state["oc_done_pool"] = S.val("pool")
        S.emit("scalar", lambda eng: eng.activation(cvb[:, 0:ntok], cvb[:, 0:ntok], AF.Silu),
               waits=[("dve", S.val("dve"))], inc=("act", 1))
        silv = S.val("act")
        state["oc_done_act"] = silv

        if kind == "v":
            for cch in range(nch):
                slot = state["vslot"] % 4
                state["vslot"] += 1
                bank = slot % 2
                pcol = bank * 1024 + (slot // 2) * 128
                S.emit("vector", lambda eng, cch=cch, slot=slot: eng.tensor_copy(
                    bfnt4[:, slot * 128:(slot + 1) * 128], cvb[:, cch * 128:(cch + 1) * 128]),
                    waits=[("act", silv), ("pe", state["vtp_pe"][slot])], inc=("dve", 1))
                vv = S.val("dve")
                wt = [("dve", vv), ("dve", state["b01_dve"][0]), ("dve", state["b01_dve"][1]),
                      ("dve", state["vbank_dve"][bank])]
                S.emit("tensor", lambda eng, slot=slot, pcol=pcol: eng.transpose(
                    pbf[:, pcol:pcol + 128],
                    bfnt4[:, slot * 128:(slot + 1) * 128], cst["identbf"][:, :]),
                    waits=wt, inc=("pe", 1))
                state["vtp_pe"][slot] = S.val("pe")
                wt2 = [("pe", S.val("pe"))]
                if len(state["vst"]) >= 4:
                    wt2.append(state["vst"][-4])
                S.emit("vector", lambda eng, slot=slot, pcol=pcol: eng.tensor_copy(
                    bf4[:, slot * 128:(slot + 1) * 128], pbf[:, pcol:pcol + 128]),
                    waits=wt2, inc=("dve", 1))
                state["vbank_dve"][bank] = S.val("dve")
                stv = dma("sync", Dvsp.ap()[(cch * 16 + oc) * 128:(cch * 16 + oc + 1) * 128, :],
                          bf4[:, slot * 128:(slot + 1) * 128], waits=[("dve", S.val("dve"))],
                          stream=f"stv{slot}")
                state["vst"].append(stv)
            state["oc_done_dve"] = S.val("dve")
            return
        lastn = 0
        for gi2, g0c in enumerate(range(0, ntok, 512)):
            w = min(512, ntok - g0c)
            sl2 = gi2 % 2
            S.emit("scalar", lambda eng, g0c=g0c, w=w, sl2=sl2: eng.activation(
                sqb[:, sl2 * 512:sl2 * 512 + w], cvb[:, g0c:g0c + w], AF.Square),
                waits=[("act", silv), ("pe", state["sq_pe"][sl2])], inc=("act", 1))
            vsq = S.val("act")
            S.emit("tensor", lambda eng, w=w, sl2=sl2: eng.matmul(pb[2][0:2, 0:w], cst["headmask"][:, :],
                                                                  sqb[:, sl2 * 512:sl2 * 512 + w],
                                                                  start=True, stop=True),
                   waits=[("act", vsq), ("dve", lastn)], inc=("pe", 1))
            state["sq_pe"][sl2] = S.val("pe")
            S.emit("vector", lambda eng, g0c=g0c, w=w: eng.tensor_copy(nrm[0:2, g0c:g0c + w], pb[2][0:2, 0:w]),
                   waits=[("pe", S.val("pe"))], inc=("dve", 1))
            lastn = S.val("dve")
        state["oc_done_act"] = S.val("act")
        S.emit("scalar", lambda eng: eng.activation(nrm[0:2, 0:ntok], nrm[0:2, 0:ntok], AF.Sqrt,
                                                    bias=cvals[0:2, 1:2]),
               waits=[("dve", lastn), ("dve", cinit_v)], inc=("act", 1))
        S.emit("vector", lambda eng: eng.reciprocal(nrm[0:2, 0:ntok], nrm[0:2, 0:ntok]),
               waits=[("act", S.val("act"))], inc=("dve", 1))
        vrs = S.val("dve")
        state["oc_done_act"] = S.val("act")
        maskT = cst["maskT_q"] if kind == "q" else cst["maskT_k"]
        dst_d = Dq if kind == "q" else Dk
        lastq = 0
        for gi2, g0c in enumerate(range(0, ntok, 512)):
            w = min(512, ntok - g0c)
            sl2 = gi2 % 2
            S.emit("tensor", lambda eng, g0c=g0c, w=w, maskT=maskT: eng.matmul(
                pb[3][:, 0:w], maskT[0:2, :], nrm[0:2, g0c:g0c + w], start=True, stop=True),
                waits=[("dve", vrs), ("dve", lastq)], inc=("pe", 1))
            wq2 = [("pe", S.val("pe"))]
            if state["qk_st"][sl2] is not None:
                wq2.append(state["qk_st"][sl2])
            S.emit("vector", lambda eng, g0c=g0c, w=w, sl2=sl2: eng.tensor_tensor(
                qst[:, sl2 * 512:sl2 * 512 + w], cvb[:, g0c:g0c + w], pb[3][:, 0:w], ALU.mult),
                waits=wq2, inc=("dve", 1))
            lastq = S.val("dve")
            state["qk_st"][sl2] = dma("sync", dst_d.ap()[oc * 128:(oc + 1) * 128, g0c:g0c + w],
                                      qst[:, sl2 * 512:sl2 * 512 + w], waits=[("dve", lastq)],
                                      stream=f"stqk{sl2}")
        state["oc_done_dve"] = lastq

    for oc in range(8):
        proj_oc("q", oc)
    if maxphase >= 3:
        for oc in range(8):
            proj_oc("k", oc)
    if maxphase >= 4:
        for oc in range(16):
            proj_oc("v", oc)
    if maxphase >= 5:
        for oc in range(16):
            proj_oc("g", oc)
    def qk_taps():
        for nm, dd in (("q_fm", Dq), ("k_fm", Dk)):
            if nm in taps:
                dma("sync", taps[nm][:, 0:8 * ntok].rearrange("p (g t) -> p g t", g=8),
                    dd.ap().rearrange("(g p) t -> p g t", p=128), stream="sttap")

    if maxphase < 6:
        A.free(gB1)
        A.free(gB0)
        S.barrier()
        qk_taps()
        if "vsp" in taps and maxphase >= 4:
            dma("sync", taps["vsp"], Dvsp.ap(), stream="sttap")
        if "gsp" in taps and maxphase >= 5:
            dma("sync", taps["gsp"], Dgsp.ap(), stream="sttap")
        _finish(nc, S, O, ntok)
        return nc, taps

    A.free(gB1)
    gBab = "phA2ab"
    abh = A.sb(gBab, "abh", [16, ntok], F32)
    abh2 = A.sb(gBab, "abh2", [16, ntok], F32)
    lnb = A.sb(gBab, "lnb", [16, ntok], F32)
    sc16 = A.sb(gBab, "sc16", [16, ntok], F32)
    gb1_pend = [e for e in state["qk_st"] + state["g_slot_st"] if e] + state["vst"][-4:]

    src = I["abT"].rearrange("(ic p) o -> p ic o", p=128)
    wsl_ab = state["wslot"] % 2
    ldab = dma("gpsimd", wbf[:, wsl_ab * 2048: wsl_ab * 2048 + 512].rearrange("p (ic o) -> p ic o", ic=16),
               src, waits=[("pe", state["wslot_pe"][wsl_ab])], sem=f"wfg{wsl_ab}")
    lastab = state["oc_done_dve"]
    for (wcol0, dst) in ((0, abh), (16, abh2)):
        for g0c in range(0, ntok, 512):
            w = min(512, ntok - g0c)
            def mm(eng, g0c=g0c, w=w, wcol0=wcol0, wsl_ab=wsl_ab):
                last = None
                for ic in range(16):
                    last = eng.matmul(pb[4][0:16, 0:w],
                                      wbf[:, wsl_ab * 2048 + ic * 32 + wcol0: wsl_ab * 2048 + ic * 32 + wcol0 + 16],
                                      h_fm[:, ic * NTP + 3 + g0c: ic * NTP + 3 + g0c + w],
                                      start=(ic == 0), stop=(ic == 15))
                return last
            S.emit("tensor", mm, waits=[ldab, ("dve", lastab)], inc=("pe", 1))
            S.emit("vector", lambda eng, g0c=g0c, w=w, dst=dst: eng.tensor_copy(dst[0:16, g0c:g0c + w],
                                                                                pb[4][0:16, 0:w]),
                   waits=[("pe", S.val("pe"))] + gb1_pend, inc=("dve", 1))
            gb1_pend = []
            lastab = S.val("dve")

    S.emit("vector", lambda eng: eng.tensor_add(smallv[0:16, 0:1], cst["b_bias"][0:16, 0:1],
                                                cst["beta_bias"][0:16, 0:1]),
           waits=[ld_consts, ("dve", lastab)], inc=("dve", 1))
    S.emit("scalar", lambda eng: eng.activation(abh2[0:16, 0:ntok], abh2[0:16, 0:ntok], AF.Sigmoid,
                                                bias=smallv[0:16, 0:1]),
           waits=[("dve", S.val("dve"))], inc=("act", 1))
    S.emit("scalar", lambda eng: eng.activation(lnb[0:16, 0:ntok], abh2[0:16, 0:ntok], AF.Ln, scale=2.0),
           inc=("act", 1))
    sig_act = S.val("act")
    S.emit("scalar", lambda eng: eng.activation(smallv[0:16, 1:2], cst["A_log"][0:16, 0:1], AF.Exp),
           waits=[ld_consts], inc=("act", 1))
    S.emit("vector", lambda eng: eng.tensor_scalar(smallv[0:16, 1:2], smallv[0:16, 1:2], -1.0, None, ALU.mult),
           waits=[("act", S.val("act"))], inc=("dve", 1))
    negA_v = S.val("dve")

    def gpre(eng):
        eng.tensor_scalar(abh[0:16, 0:ntok], abh[0:16, 0:ntok], cst["dt_bias"][0:16, 0:1], None, ALU.add)
        eng.tensor_scalar(sc16[0:16, 0:ntok], abh[0:16, 0:ntok], -1.0, None, ALU.mult)
        return eng.tensor_tensor(sc16[0:16, 0:ntok], sc16[0:16, 0:ntok], abh[0:16, 0:ntok], ALU.max)
    S.emit("vector", gpre, waits=[("dve", negA_v), ("act", sig_act)], inc=("dve", 1))
    S.emit("scalar", lambda eng: eng.activation(sc16[0:16, 0:ntok], sc16[0:16, 0:ntok], AF.Exp, scale=-1.0),
           waits=[("dve", S.val("dve"))], inc=("act", 1))
    S.emit("scalar", lambda eng: eng.activation(sc16[0:16, 0:ntok], sc16[0:16, 0:ntok], AF.Ln, bias=1.0),
           inc=("act", 1))
    lnact = S.val("act")

    def gfin(eng):
        eng.tensor_scalar(abh[0:16, 0:ntok], abh[0:16, 0:ntok], 0.0, None, ALU.max)
        eng.tensor_add(abh[0:16, 0:ntok], abh[0:16, 0:ntok], sc16[0:16, 0:ntok])
        return eng.tensor_scalar(abh[0:16, 0:ntok], abh[0:16, 0:ntok], smallv[0:16, 1:2], None, ALU.mult)
    S.emit("vector", gfin, waits=[("act", lnact)], inc=("dve", 1))

    def scans(eng):
        last = None
        for cch in range(nch):
            sl = slice(cch * L, (cch + 1) * L)
            last = eng.tensor_tensor_scan(hm_cum[0:16, sl], cst["onesrow"][0:16, 0:L],
                                          abh[0:16, sl], 0.0, ALU.mult, ALU.add)
        return last
    S.emit("vector", scans, inc=("dve", 1))
    S.emit("vector", lambda eng: eng.tensor_add(hm_cumb[0:16, 0:ntok], hm_cum[0:16, 0:ntok], lnb[0:16, 0:ntok]),
           inc=("dve", 1))
    S.emit("scalar", lambda eng: eng.activation(hm_expc[0:16, 0:ntok], hm_cum[0:16, 0:ntok], AF.Exp),
           waits=[("dve", S.val("dve"))], inc=("act", 1))
    S.emit("vector", lambda eng: eng.tensor_scalar(abh2[0:16, 0:ntok], abh2[0:16, 0:ntok], 2.0, None, ALU.mult),
           waits=[("act", S.val("act"))], inc=("dve", 1))
    S.emit("vector", lambda eng: eng.tensor_mul(lnb[0:16, 0:ntok], abh2[0:16, 0:ntok], hm_expc[0:16, 0:ntok]),
           inc=("dve", 1))
    bg_v = S.val("dve")

    prev_tp = 0
    for cch in range(nch):
        sl = slice(cch * L, (cch + 1) * L)
        S.emit("vector", lambda eng, cch=cch, sl=sl: eng.tensor_scalar(
            sc16[0:16, 0:L], hm_cum[0:16, sl], -1.0,
            hm_cum[0:16, cch * L + L - 1:cch * L + L], ALU.mult, ALU.add),
            waits=[("dve", bg_v), ("pe", prev_tp)], inc=("dve", 1))
        S.emit("scalar", lambda eng: eng.activation(sc16[0:16, 0:L], sc16[0:16, 0:L], AF.Exp),
               waits=[("dve", S.val("dve"))], inc=("act", 1))
        kbs_act = S.val("act")

        def tpose(eng, cch=cch, sl=sl):
            eng.transpose(pb[5][0:128, 0:16], hm_cum[0:16, sl], cst["ident"][0:16, 0:16])
            eng.transpose(pb[5][0:128, 16:32], abh2[0:16, sl], cst["ident"][0:16, 0:16])
            eng.transpose(pb[5][0:128, 32:48], lnb[0:16, sl], cst["ident"][0:16, 0:16])
            eng.transpose(pb[5][0:128, 48:64], sc16[0:16, 0:L], cst["ident"][0:16, 0:16])
            eng.transpose(pb[5][0:128, 80:96], hm_cumb[0:16, sl], cst["ident"][0:16, 0:16])
            return eng.transpose(pb[5][0:1, 64:80], hm_expc[0:16, cch * L + L - 1:cch * L + L],
                                 cst["ident"][0:16, 0:16])
        S.emit("tensor", tpose, waits=[("act", kbs_act)], inc=("pe", 1))
        prev_tp = S.val("pe")

        def tevac(eng, cch=cch):
            eng.tensor_copy(st_cum[:, cch * 16:(cch + 1) * 16], pb[5][0:128, 0:16])
            eng.tensor_copy(st_beta[:, cch * 16:(cch + 1) * 16], pb[5][0:128, 16:32])
            eng.tensor_copy(st_bg[:, cch * 16:(cch + 1) * 16], pb[5][0:128, 32:48])
            eng.tensor_copy(st_kbs[:, cch * 16:(cch + 1) * 16], pb[5][0:128, 48:64])
            eng.tensor_copy(st_cumb[:, cch * 16:(cch + 1) * 16], pb[5][0:128, 80:96])
            return eng.tensor_copy(gl_row[0:1, cch * 16:(cch + 1) * 16], pb[5][0:1, 64:80])
        S.emit("vector", tevac, waits=[("pe", prev_tp)], inc=("dve", 1))

    A.free(gBab)
    A.free(gB0)
    A.free("hfm")
    barA2 = S.barrier()
    qk_taps()
    for nm, src_, cols in (("cum", st_cum, nch * 16), ("beta", st_beta, nch * 16),
                           ("bg", st_bg, nch * 16), ("kbs", st_kbs, nch * 16)):
        if nm in taps:
            dma("sync", taps[nm][:, 0:cols], src_[:, 0:cols], stream="sttap")
    if "glrow" in taps:
        dma("sync", taps["glrow"], gl_row[0:1, :], stream="sttap")
    if "vsp" in taps:
        dma("sync", taps["vsp"], Dvsp.ap(), stream="sttap")
    if "gsp" in taps:
        dma("sync", taps["gsp"], Dgsp.ap(), stream="sttap")

    if maxphase < 7:
        _finish(nc, S, O, ntok)
        return nc, taps

    S.phase("p1")
    Dgl = nc.dram_tensor("Dgl", [1, nch * 16], F32)
    glst = dma("sync", Dgl.ap(), gl_row[0:1, :], stream="stgl")
    dma("sync", glb[:, :], Dgl.ap().partition_broadcast(128), waits=[glst], stream="mbc")
    glb_ld = ("q_mbc", S.val("q_mbc"))

    Dyc = nc.dram_tensor("Dyc", [nch * 16 * 64, 128], BF16)
    g1 = "p1"
    vtile = A.sb(g1, "vtile", [128, 2 * 16 * 128], BF16)
    em4 = A.sb(g1, "em4", [128, 2 * 512], F32)
    e4 = A.sb(g1, "e4", [128, 2 * 512], F32)
    abar4d = A.sb(g1, "abar4", [128, 2 * 512], BF16)
    a4d = A.sb(g1, "a4", [128, 2 * 512], BF16)
    bbar4d = A.sb(g1, "bbar4", [128, 2 * 512], BF16)
    qg4d = A.sb(g1, "qg4", [128, 2 * 512], F32)
    abuf = A.sb(g1, "abuf", [128, 2 * 6 * 512], BF16)
    apow = A.sb(g1, "apow", [128, 2 * 2 * 512], BF16)
    xab = A.sb(g1, "xab", [128, 2 * 2 * 768], BF16)
    kbt4 = A.sb(g1, "kbt4", [128, 2 * 256], BF16)
    pc4 = A.sb(g1, "pc4", [64, 2 * 256], BF16)
    Ycs4 = A.sb(g1, "Ycs4", [64, 2 * 512], BF16)
    negc4 = A.sb(g1, "negc4", [64, 2 * 256], BF16)
    oust4 = A.sb(g1, "oust4", [128, 2 * 512], BF16)
    ztst4 = A.sb(g1, "ztst4", [64, 2 * 512], BF16)
    pqacc = A.sb(g1, "pqacc", [64, 16 * 192], BF16)
    pqst = A.sb(g1, "pqst", [64, 2 * 192], BF16)
    qtl = A.sb(g1, "qtl", [128, 2 * 1024], BF16)
    ktl = A.sb(g1, "ktl", [128, 2 * 1024], BF16)

    pq_ev = {}
    def pqinit(eng):
        eng.memset(pqacc[0:64, :], 0.0)
        last = None
        for hh in range(16):
            last = eng.tensor_copy(pqacc[0:64, hh * 192: hh * 192 + 64], cst["identbf"][0:64, 0:64])
        return last
    S.emit("vector", pqinit, waits=[ld_consts], inc=("dve", 1))
    for hh in range(16):
        pq_ev[hh] = ("dve", S.val("dve"))

    bg_guard = {i: {} for i in range(6)}
    bg_guard["bfA"] = {}
    bg_guard["bfB"] = {}
    evac_flip = [0]

    def evac_engine():
        evac_flip[0] ^= 1
        return "vector" if evac_flip[0] else "scalar"

    def eng_sem(e):
        return "dve" if e == "vector" else "act"

    def pe_mm(fn, extra_waits, bank_keys):
        wl = list(extra_waits)
        for b in bank_keys:
            for snm, v in bg_guard[b].items():
                if v > 0:
                    wl.append((snm, v))
        S.emit("tensor", fn, waits=wl, inc=("pe", 1))
        return S.val("pe")

    def evac(fn, pev, bank_keys, extra_waits=(), scalar_ok=True):
        e = evac_engine() if scalar_ok else "vector"
        S.emit(e, fn, waits=[("pe", pev)] + list(extra_waits), inc=(eng_sem(e), 1))
        v = (eng_sem(e), S.val(eng_sem(e)))
        for b in bank_keys:
            bg_guard[b][v[0]] = v[1]
        return v

    vld_prev = [None, None]
    qk_cons = [None, None]
    styc4 = [None, None]
    stou4 = [None, None]
    stzt4 = [None, None]
    stnt4 = [None, None]
    nt4_dve = [0, 0]
    yc4_pe = [0, 0]

    def kf_ap(hh, cc):
        oc, half = hh // 2, (hh % 2) * 64
        base = (cc % 2) * 1024 + oc * L
        return ktl[half:half + 64, base:base + L]

    def qf_ap(hh, cc):
        oc, half = hh // 2, (hh % 2) * 64
        base = (cc % 2) * 1024 + oc * L
        return qtl[half:half + 64, base:base + L]

    def grp_stages(cc, grp, st, ldk, ldv):
        import os as _os
        HLP = HL_PARTS if (HL_PARTS2 is None or grp % 2 == 0) else HL_PARTS2
        par = cc % 2
        heads = list(range(grp * 4, grp * 4 + 4))
        ch0 = cc * 16 + heads[0]
        b0, b1, b2 = (0, 1, 2) if st == 0 else (3, 4, 5)
        bfk = "bfA" if st == 0 else "bfB"
        pfb = st * 1024
        sb = st * 512
        em = em4[:, sb:sb + 512]
        ex = e4[:, sb:sb + 512]
        abar4 = abar4d[:, sb:sb + 512]
        a4 = a4d[:, sb:sb + 512]
        bbar4 = bbar4d[:, sb:sb + 512]
        qg4 = qg4d[:, sb:sb + 512]
        abufs = abuf[:, st * 3072: (st + 1) * 3072]
        apows = apow[:, st * 1024: (st + 1) * 1024]
        xs = [xab[:, (st * 2 + j) * 768: (st * 2 + j + 1) * 768] for j in range(2)]
        kb4 = kbt4[:, st * 256:(st + 1) * 256]
        pcs = pc4[0:64, st * 256:(st + 1) * 256]

        fence = [("dve", S.val("dve")), ("act", S.val("act"))] if _os.environ.get("P1_FENCE") else []
        for i, hh in enumerate(heads):
            pv_g = pe_mm(lambda eng, i=i, hh=hh: eng.matmul(pb[b0][:, i * 128:(i + 1) * 128],
                                                            kf_ap(hh, cc), kf_ap(hh, cc),
                                                            start=True, stop=True),
                         [ldk] + fence if i == 0 else [ldk], [b0])
        for i, hh in enumerate(heads):
            pv_gq = pe_mm(lambda eng, i=i, hh=hh: eng.matmul(pb[b1][:, i * 128:(i + 1) * 128],
                                                             kf_ap(hh, cc), qf_ap(hh, cc),
                                                             start=True, stop=True), [ldk], [b1])
        for i, hh in enumerate(heads):
            pv_b = pe_mm(lambda eng, i=i, hh=hh: eng.matmul(pb[b2][:, i * 128:(i + 1) * 128],
                                                            cst["ohsel"][0:16, hh * 128:(hh + 1) * 128],
                                                            hm_cumb[0:16, cc * L:(cc + 1) * L],
                                                            start=True, stop=True), [], [b2])
        yield
        if "bld1" not in HLP:
            return

        def bld1(eng):
            last = None
            for i, hh in enumerate(heads):
                last = eng.scalar_tensor_tensor(em[:, i * 128:(i + 1) * 128],
                                                pb[b2][:, i * 128:(i + 1) * 128],
                                                st_cum[:, cc * 16 + hh: cc * 16 + hh + 1],
                                                cst["mneg_s4"][:, i * 128:(i + 1) * 128],
                                                ALU.subtract, ALU.add)
            return last
        em_v = evac(bld1, pv_b, [b2], scalar_ok=False)
        S.emit("scalar", lambda eng: eng.activation(ex[:, :], em[:, :], AF.Exp),
               waits=[em_v], inc=("act", 1))
        S.emit("vector", lambda eng: eng.scalar_tensor_tensor(abar4[:, :], pb[b0][:, :], 1.0,
                                                              ex[:, :], ALU.mult, ALU.mult),
               waits=[("act", S.val("act")), ("pe", pv_g)], inc=("dve", 1))
        abar_bld = ("dve", S.val("dve"))
        yield
        if "bld2" not in HLP:
            return

        for i, hh in enumerate(heads):
            pv_b2 = pe_mm(lambda eng, i=i, hh=hh: eng.matmul(pb[b2][:, i * 128:(i + 1) * 128],
                                                             cst["ohsel"][0:16, hh * 128:(hh + 1) * 128],
                                                             hm_cum[0:16, cc * L:(cc + 1) * L],
                                                             start=True, stop=True),
                          [em_v] if i == 0 else [], [b2] if i == 0 else [])

        def bld2(eng):
            for i, hh in enumerate(heads):
                eng.tensor_scalar(em[:, i * 128:(i + 1) * 128], pb[b2][:, i * 128:(i + 1) * 128],
                                  -1.0, st_cumb[:, cc * 16 + hh: cc * 16 + hh + 1],
                                  ALU.mult, ALU.add)
            return eng.tensor_add(em[:, :], em[:, :], cst["mneg_sT4"][:, :])
        em_v2 = evac(bld2, pv_b2, [], extra_waits=[abar_bld], scalar_ok=False)
        S.emit("scalar", lambda eng: eng.activation(ex[:, :], em[:, :], AF.Exp),
               waits=[em_v2], inc=("act", 1))
        S.emit("vector", lambda eng: eng.scalar_tensor_tensor(a4[:, :], pb[b0][:, :], 1.0, ex[:, :],
                                                              ALU.mult, ALU.mult),
               waits=[("act", S.val("act"))], inc=("dve", 1))
        bg_guard[b0]["dve"] = S.val("dve")

        def bld3(eng):
            last = None
            for i, hh in enumerate(heads):
                last = eng.scalar_tensor_tensor(em[:, i * 128:(i + 1) * 128],
                                                pb[b2][:, i * 128:(i + 1) * 128],
                                                st_cum[:, cc * 16 + hh: cc * 16 + hh + 1],
                                                cst["mneg_i4"][:, i * 128:(i + 1) * 128],
                                                ALU.subtract, ALU.add)
            return last
        S.emit("vector", bld3, waits=[], inc=("dve", 1))
        em_v3 = ("dve", S.val("dve"))
        bg_guard[b2]["dve"] = S.val("dve")
        S.emit("scalar", lambda eng: eng.activation(ex[:, :], em[:, :], AF.Exp),
               waits=[em_v3], inc=("act", 1))
        S.emit("vector", lambda eng: eng.scalar_tensor_tensor(bbar4[:, :], pb[b1][:, :], 1.0, ex[:, :],
                                                              ALU.mult, ALU.mult),
               waits=[("act", S.val("act")), ("pe", pv_gq)], inc=("dve", 1))
        bg_guard[b1]["dve"] = S.val("dve")
        yield
        if "bld4" not in HLP:
            return

        for i, hh in enumerate(heads):
            pv_b3 = pe_mm(lambda eng, i=i, hh=hh: eng.matmul(pb[b2][:, i * 128:(i + 1) * 128],
                                                             cst["ohsel"][0:16, hh * 128:(hh + 1) * 128],
                                                             hm_expc[0:16, cc * L:(cc + 1) * L],
                                                             start=True, stop=True),
                          [], [b2] if i == 0 else [])

        def bld4(eng):
            last = None
            for i, hh in enumerate(heads):
                half = (hh % 2) * 64
                last = eng.tensor_tensor(qg4[half:half + 64, i * 128:(i + 1) * 128],
                                         qf_ap(hh, cc), pb[b2][half:half + 64, i * 128:(i + 1) * 128],
                                         ALU.mult)
            return last
        qg_v = evac(bld4, pv_b3, [b2], scalar_ok=False)
        if "ktr" not in HLP:
            return

        for j in range(2):
            hh = heads[j * 2]
            oc = hh // 2
            base = (cc % 2) * 1024 + oc * L
            pv_kt = pe_mm(lambda eng, j=j, base=base: eng.transpose(
                pbf[:, pfb + j * 128: pfb + (j + 1) * 128], ktl[:, base:base + L],
                cst["identbf"][:, :]),
                [], [bfk] if j == 0 else [])
        yield
        if "x0" not in HLP:
            return

        def x0w(eng):
            last = None
            for i, hh in enumerate(heads):
                ch = cc * 16 + hh
                pcol = pfb + (i // 2) * 128 + (hh % 2) * 64
                eng.tensor_scalar(kb4[:, i * 64:(i + 1) * 64], pbf[:, pcol:pcol + 64],
                                  st_kbs[:, ch:ch + 1], None, ALU.mult)
                eng.tensor_scalar(xs[0][:, i * 192:i * 192 + 128],
                                  vtile[:, par * 2048 + hh * 128: par * 2048 + (hh + 1) * 128],
                                  st_beta[:, ch:ch + 1], None, ALU.mult)
                last = eng.tensor_scalar(xs[0][:, i * 192 + 128:(i + 1) * 192],
                                         pbf[:, pcol:pcol + 64],
                                         st_bg[:, ch:ch + 1], None, ALU.mult)
            return last
        x_ev = evac(x0w, pv_kt, [bfk], extra_waits=[ldv], scalar_ok=False)
        yield
        if "powers" not in HLP:
            return

        acur_sl = [a4[:, i * 128:(i + 1) * 128] for i in range(4)]
        abcur_sl = [abar4[:, i * 128:(i + 1) * 128] for i in range(4)]
        acur_ev = ("dve", 0)
        abar_ev = ("dve", 0)
        for lev in range(6):
            for i in range(4):
                pvA = pe_mm(lambda eng, i=i, a=acur_sl[i], ab=abcur_sl[i]: eng.matmul(
                    pb[b0][:, i * 128:(i + 1) * 128], a, ab, start=True, stop=True),
                    [acur_ev, abar_ev] if i == 0 else [], [b0] if i == 0 else [])
            if lev < 5:
                for i in range(4):
                    pvB = pe_mm(lambda eng, i=i, a=acur_sl[i], ab=abcur_sl[i]: eng.matmul(
                        pb[b1][:, i * 128:(i + 1) * 128], ab, a, start=True, stop=True),
                        [acur_ev, abar_ev] if i == 0 else [], [b1] if i == 0 else [])
            dstA = abufs[:, lev * 512:(lev + 1) * 512]
            abar_ev = evac(lambda eng, dstA=dstA: eng.xcopy(dstA, pb[b0][:, :]), pvA, [b0])
            abcur_sl = [dstA[:, i * 128:(i + 1) * 128] for i in range(4)]
            if lev < 5:
                dstB = apows[:, (lev % 2) * 512:(lev % 2 + 1) * 512]
                acur_ev = evac(lambda eng, dstB=dstB: eng.xcopy(dstB, pb[b1][:, :]), pvB, [b1])
                acur_sl = [dstB[:, i * 128:(i + 1) * 128] for i in range(4)]
            yield

        if "applies" not in HLP:
            return
        xcur, xnext = xs[0], xs[1]
        xev = x_ev
        for j, lev in enumerate([6, 5, 4, 3, 2, 1, 0]):
            if lev == 0:
                lhs = [abar4[:, i * 128:(i + 1) * 128] for i in range(4)]
            else:
                lhs = [abufs[:, (lev - 1) * 512 + i * 128:(lev - 1) * 512 + (i + 1) * 128]
                       for i in range(4)]
            sgn = -1.0 if lev == 0 else 1.0

            for k in range(4):
                bk = b0 if k < 2 else b1
                col = (k % 2) * 192
                pv_ = pe_mm(lambda eng, lhs=lhs[k], xc=xcur, k=k, bk=bk, col=col: eng.matmul(
                    pb[bk][:, col:col + 192], lhs, xc[:, k * 192:(k + 1) * 192],
                    start=True, stop=True),
                    [xev, abar_ev] if k in (0, 2) else [], [bk] if k in (0, 2) else [])
                if k == 1:
                    pvA = pv_
            pvB = pv_

            def appev(eng, xn=xnext, xc=xcur, sgn=sgn):
                eng.scalar_tensor_tensor(xn[:, 0:384], pb[b0][:, 0:384], sgn, xc[:, 0:384],
                                         ALU.mult, ALU.add)
                return eng.scalar_tensor_tensor(xn[:, 384:768], pb[b1][:, 0:384], sgn, xc[:, 384:768],
                                                ALU.mult, ALU.add)
            S.emit("vector", appev, waits=[("pe", pvB)], inc=("dve", 1))
            xev = ("dve", S.val("dve"))
            bg_guard[b0]["dve"] = S.val("dve")
            bg_guard[b1]["dve"] = S.val("dve")
            xcur, xnext = xnext, xcur
            yield

        if "post" not in HLP:
            return
        for i in range(4):
            pvY = pe_mm(lambda eng, i=i, xc=xcur: eng.matmul(
                pb[b0][0:64, i * 128:(i + 1) * 128], kb4[:, i * 64:(i + 1) * 64],
                xc[:, i * 192:i * 192 + 128], start=True, stop=True),
                [xev] if i == 0 else [], [b0] if i == 0 else [])
        for i in range(4):
            pvN = pe_mm(lambda eng, i=i, xc=xcur: eng.matmul(
                pb[b1][0:64, i * 64:(i + 1) * 64], xc[:, i * 192 + 128:(i + 1) * 192],
                kb4[:, i * 64:(i + 1) * 64], start=True, stop=True),
                [xev] if i == 0 else [], [b1] if i == 0 else [])
        wyc = [] if styc4[st] is None else [styc4[st]]
        if yc4_pe[st]:
            wyc.append(("pe", yc4_pe[st]))
        yc_ev = evac(lambda eng: eng.xcopy(Ycs4[0:64, st * 512:(st + 1) * 512], pb[b0][0:64, 0:512]),
                     pvY, [b0], extra_waits=wyc)
        wnt = [] if stnt4[st] is None else [stnt4[st]]
        if nt4_dve[st]:
            wnt.append(("dve", nt4_dve[st]))
        nt_ev = evac(lambda eng: eng.xmul(negc4[0:64, st * 256:(st + 1) * 256], pb[b1][0:64, 0:256],
                                          -1.0), pvN, [b1], extra_waits=wnt)
        styc4[st] = dma("sync", Dyc.ap()[ch0 * 64:ch0 * 64 + 256, :].rearrange("(h p) t -> p h t", p=64),
                        Ycs4[0:64, st * 512:(st + 1) * 512].rearrange("p (h t) -> p h t", h=4),
                        waits=[yc_ev], stream=f"styc{st}")
        stnt4[st] = dma("sync", DnegT.ap()[ch0 * 64:ch0 * 64 + 256, :].rearrange("(h p) t -> p h t", p=64),
                        negc4[0:64, st * 256:(st + 1) * 256].rearrange("p (h t) -> p h t", h=4),
                        waits=[nt_ev], stream=f"stnt{st}")
        yield

        for i in range(4):
            pvO = pe_mm(lambda eng, i=i, xc=xcur: eng.matmul(
                pb[b0][:, i * 128:(i + 1) * 128], xc[:, i * 192:i * 192 + 128],
                bbar4[:, i * 128:(i + 1) * 128], start=True, stop=True),
                [xev] if i == 0 else [], [b0] if i == 0 else [])
        for i in range(4):
            pvZ = pe_mm(lambda eng, i=i, xc=xcur: eng.matmul(
                pb[b1][0:64, i * 128:(i + 1) * 128], xc[:, i * 192 + 128:(i + 1) * 192],
                bbar4[:, i * 128:(i + 1) * 128], start=True, stop=True),
                [xev] if i == 0 else [], [b1] if i == 0 else [])
        wou = [] if stou4[st] is None else [stou4[st]]
        ou_ev = evac(lambda eng: eng.xcopy(oust4[:, st * 512:(st + 1) * 512], pb[b0][:, 0:512]),
                     pvO, [b0], extra_waits=wou)

        def ztev(eng):
            last = None
            for i, hh in enumerate(heads):
                half = (hh % 2) * 64
                last = eng.scalar_tensor_tensor(ztst4[0:64, st * 512 + i * 128: st * 512 + (i + 1) * 128],
                                                pb[b1][0:64, i * 128:(i + 1) * 128], -1.0,
                                                qg4[half:half + 64, i * 128:(i + 1) * 128],
                                                ALU.mult, ALU.add)
            return last
        wzt = [] if stzt4[st] is None else [stzt4[st]]
        zt_ev = evac(ztev, pvZ, [b1], extra_waits=wzt, scalar_ok=False)
        stou4[st] = dma("sync", Dou.ap()[ch0 * 128:ch0 * 128 + 512, :].rearrange("(h p) t -> p h t", p=128),
                        oust4[:, st * 512:(st + 1) * 512].rearrange("p (h t) -> p h t", h=4),
                        waits=[ou_ev], stream=f"stou{st}")
        stzt4[st] = dma("sync", Dzt.ap()[ch0 * 64:ch0 * 64 + 256, :].rearrange("(h p) t -> p h t", p=64),
                        ztst4[0:64, st * 512:(st + 1) * 512].rearrange("p (h t) -> p h t", h=4),
                        waits=[zt_ev], stream=f"stzt{st}")
        yield

        if "compose" not in HLP:
            return
        def pcb(eng):
            last = None
            for i, hh in enumerate(heads):
                ch = cc * 16 + hh
                eng.tensor_scalar(pcs[:, i * 64:(i + 1) * 64], cst["identbf"][0:64, 0:64],
                                  glb[0:64, ch:ch + 1], None, ALU.mult)
                last = eng.tensor_tensor(pcs[:, i * 64:(i + 1) * 64], pcs[:, i * 64:(i + 1) * 64],
                                         negc4[0:64, st * 256 + i * 64: st * 256 + (i + 1) * 64], ALU.add)
            return last
        wpc = [glb_ld, nt_ev, ("pe", S.val("pe"))]
        S.emit("vector", pcb, waits=wpc, inc=("dve", 1))
        pcv = ("dve", S.val("dve"))
        nt4_dve[st] = S.val("dve")
        for half2 in range(2):
            i0_, i1_ = half2 * 2, half2 * 2 + 1

            for i in (i0_, i1_):
                def cmps1(eng, i=i, base=(i - i0_) * 192, hh=heads[i]):
                    eng.matmul(pb[b2][0:64, base:base + 192], pcs[:, i * 64:(i + 1) * 64],
                               pqacc[0:64, hh * 192:(hh + 1) * 192], start=True, stop=False)
                    return eng.matmul(pb[b2][0:64, base + 64:base + 192], cst["identbf"][0:64, 0:64],
                                      Ycs4[0:64, st * 512 + i * 128: st * 512 + (i + 1) * 128],
                                      start=False, stop=True)
                pvC = pe_mm(cmps1,
                            [pcv, yc_ev, pq_ev[heads[i0_]], pq_ev[heads[i1_]]] if i == i0_ else [],
                            [b2] if i == i0_ else [])

            def cev(eng, i0_=i0_):
                return eng.xcopy(pqacc[0:64, heads[i0_] * 192: heads[i0_] * 192 + 384],
                                 pb[b2][0:64, 0:384])
            ev = evac(cev, pvC, [b2])
            pq_ev[heads[i0_]] = ev
            pq_ev[heads[i0_] + 1] = ev
        yc4_pe[st] = S.val("pe")
        yield

    p1_nch, p1_ngrp = P1_LIMIT if P1_LIMIT else (nch, 4)
    for cc in range(p1_nch):
        par = cc % 2
        vsrc = Dvsp.ap()[cc * 2048:(cc + 1) * 2048, :].rearrange("(o t) v -> t o v", t=128)
        vwaits = [] if vld_prev[par] is None else list(vld_prev[par])
        ldv = dma("sync", vtile[:, par * 2048:(par + 1) * 2048].rearrange("t (o v) -> t o v", o=16),
                  vsrc, waits=vwaits, stream=f"vld{par}")
        qkw = [] if qk_cons[par] is None else list(qk_cons[par])
        dma("sync", qtl[:, par * 1024:(par + 1) * 1024].rearrange("p (g t) -> p g t", g=8),
            Dq.ap()[:, cc * L:(cc + 1) * L].rearrange("(g p) t -> p g t", p=128),
            waits=qkw, stream=f"qld{par}")
        ldk = dma("sync", ktl[:, par * 1024:(par + 1) * 1024].rearrange("p (g t) -> p g t", g=8),
            Dk.ap()[:, cc * L:(cc + 1) * L].rearrange("(g p) t -> p g t", p=128),
            stream=f"qld{par}")
        import os as _os
        _ilv = _os.environ.get("P1_ILV")
        if _ilv:
            for gp in range(0, p1_ngrp, 2):
                gens = [grp_stages(cc, gp, 0, ldk, ldv)]
                if gp + 1 < p1_ngrp:
                    gens.append(grp_stages(cc, gp + 1, 1, ldk, ldv))
                live = list(gens)
                while live:
                    nxt = []
                    for g in live:
                        try:
                            next(g)
                            nxt.append(g)
                        except StopIteration:
                            pass
                    live = nxt
        else:
            for gp in range(0, p1_ngrp):
                gens = [grp_stages(cc, gp, gp % 2, ldk, ldv)]
                for g in gens:
                    for _ in g:
                        pass
                    if _os.environ.get("P1_FLUSHONLY"):
                        S.flush_dma()
                    elif not _os.environ.get("P1_NOBAR"):
                        S.barrier()
        vld_prev[par] = [("pe", S.val("pe")), ("dve", S.val("dve")), ldv]
        qk_cons[par] = [("pe", S.val("pe")), ("dve", S.val("dve"))]

    pqslot_prev = [None, None]
    for hh in range(16):
        slot = hh % 2
        bfb = "bfA" if slot == 0 else "bfB"
        bfcol = 0 if slot == 0 else 1024
        pv = pe_mm(lambda eng, hh=hh, bfcol=bfcol: eng.transpose(
            pbf[:, bfcol:bfcol + 64][0:64, :], pqacc[0:64, hh * 192:hh * 192 + 64],
            cst["identbf"][0:64, 0:64]), [], [bfb])
        wst = [] if pqslot_prev[slot] is None else [pqslot_prev[slot]]
        def pqev(eng, hh=hh, slot=slot, bfcol=bfcol):
            eng.xcopy(pqst[0:64, slot * 192:slot * 192 + 64], pbf[0:64, bfcol:bfcol + 64])
            return eng.xcopy(pqst[0:64, slot * 192 + 64:(slot + 1) * 192],
                             pqacc[0:64, hh * 192 + 64:(hh + 1) * 192])
        ev = evac(pqev, pv, [bfb], extra_waits=wst)
        pqslot_prev[slot] = dma("sync", Dpq_in.ap()[hh * 64:(hh + 1) * 64, :],
                                pqst[0:64, slot * 192:(slot + 1) * 192], waits=[ev], stream=f"stpq{slot}")
    S.flush_dma()
    S.emit("gpsimd", lambda eng: eng.collective_compute(
        "AllGather", ALU.bypass, replica_groups=[list(range(NCORES))],
        ins=[Dpq_in.ap().opt()], outs=[Dpq_gath.ap().opt()]),
        waits=[pqslot_prev[0], pqslot_prev[1]], inc=("cc_pq", 1))

    barP1 = S.barrier()
    if "negT" in taps:
        dma("sync", taps["negT"].rearrange("p (a t) -> p a t", t=64),
            DnegT.ap().rearrange("(a p) t -> p a t", p=64), stream="sttap")
    if "Yc" in taps:
        dma("sync", taps["Yc"].rearrange("p (c t) -> p c t", t=128),
            Dyc.ap().rearrange("(c p) t -> p c t", p=64), stream="sttap")
    for cc_ in range(p1_nch):
        for h_ in range(p1_ngrp * 4):
            ch_ = cc_ * 16 + h_
            if "ouT" in taps:
                dma("sync", taps["ouT"][ch_ * 128:(ch_ + 1) * 128, :],
                    Dou.ap()[ch_ * 128:(ch_ + 1) * 128, :], stream="sttap")
            if "zT" in taps:
                dma("sync", taps["zT"][ch_ * 64:(ch_ + 1) * 64, :],
                    Dzt.ap()[ch_ * 64:(ch_ + 1) * 64, :], stream="sttap")
    if "pq" in taps:
        dma("sync", taps["pq"], Dpq_in.ap(), stream="sttap")

    A.free(g1)
    A.free("hm")
    A.free("constp1")
    if maxphase < 8:
        _finish(nc, S, O, ntok)
        return nc, taps

    S.phase("blend")
    S_f = A.sb("sfin", "S_f", [64, 16 * 128], F32)
    sbf = A.sb("sfin", "sbf", [64, 128], BF16)
    gb_ = "blend"
    pqs = A.sb(gb_, "pqs", [64, NCORES * 16 * 192], BF16)
    acc3 = A.sb(gb_, "acc3", [64, 3 * 16 * 192], BF16)
    tcomp = A.sb(gb_, "tcomp", [64, 128], BF16)
    ldpq = dma("sync", pqs[:, :].rearrange("p (r h c) -> p r h c", r=NCORES, h=16),
               Dpq_gath.ap().rearrange("(r h p) c -> p r h c", p=64, h=16),
               waits=[("cc_pq", 1)], stream="pq")
    S.emit("vector", lambda eng: eng.memset(acc3[0:64, :], 0.0), inc=("dve", 1))
    for j in range(3):
        for r in range(NCORES):
            S.emit("vector", lambda eng, j=j, r=r: eng.scalar_tensor_tensor(
                acc3[0:64, j * 3072:(j + 1) * 3072], pqs[0:64, r * 3072:(r + 1) * 3072],
                cst["selp"][0:64, j * 8 + r:j * 8 + r + 1], acc3[0:64, j * 3072:(j + 1) * 3072],
                ALU.mult, ALU.add),
                waits=[ldpq, ld_consts] if (j == 0 and r == 0) else [], inc=("dve", 1))
    blend_v = S.val("dve")

    def PQj(j, hh):
        base = j * 3072 + hh * 192
        return acc3[0:64, base:base + 64], acc3[0:64, base + 64:base + 192]

    for hh in range(16):
        P1T, Q1 = PQj(1, hh)
        P0T, Q0 = PQj(0, hh)
        _, Q2 = PQj(2, hh)

        def c1(eng, P1T=P1T, Q1=Q1, Q2=Q2):
            eng.matmul(pb[0][0:64, 0:128], P1T, Q2, start=True, stop=False)
            return eng.matmul(pb[0][0:64, 0:128], cst["identbf"][0:64, 0:64], Q1, start=False, stop=True)
        S.emit("tensor", c1, waits=[("dve", blend_v), ("dve", S.val("dve")), ("act", S.val("act"))],
               inc=("pe", 1))
        S.emit("vector", lambda eng: eng.tensor_copy(tcomp[0:64, 0:128], pb[0][0:64, 0:128]),
               waits=[("pe", S.val("pe"))], inc=("dve", 1))
        tv = S.val("dve")

        def c2(eng, P0T=P0T, Q0=Q0):
            eng.matmul(pb[0][0:64, 0:128], P0T, tcomp[0:64, 0:128], start=True, stop=False)
            return eng.matmul(pb[0][0:64, 0:128], cst["identbf"][0:64, 0:64], Q0, start=False, stop=True)
        S.emit("tensor", c2, waits=[("dve", tv)], inc=("pe", 1))
        S.emit("vector", lambda eng, hh=hh: eng.tensor_copy(S_f[0:64, hh * 128:(hh + 1) * 128],
                                                            pb[0][0:64, 0:128]),
               waits=[("pe", S.val("pe"))], inc=("dve", 1))
    A.free(gb_)
    if "sinit" in taps:
        dma("sync", taps["sinit"], S_f[0:64, :], waits=[("dve", S.val("dve"))], stream="sttap")
    barBL = S.barrier()

    S.phase("p2a")
    g2a_ = "p2a"
    ztt = A.sb(g2a_, "ztt", [64, 2 * 16 * 128], BF16)
    outt = A.sb(g2a_, "outt", [128, 2 * 16 * 128], BF16)
    ycc = A.sb(g2a_, "ycc", [64, 2 * 16 * 128], BF16)
    gout_t = A.sb(g2a_, "gout_t", [128, 2 * 16 * 128], BF16)
    ngc = A.sb(g2a_, "ngc", [64, 2 * 16 * 64], BF16)
    og = A.sb(g2a_, "og", [128, 2 * 16 * 128], BF16)
    o2s = A.sb(g2a_, "o2s", [128, 128], F32)
    rstdb = A.sb(g2a_, "rstdb", [128, 128], F32)
    rrow = A.sb(g2a_, "rrow", [1, 128], F32)
    identg = A.sb(g2a_, "identg", [64, 64], BF16)

    prev_ch_ld = [None, None]
    stog = [None, None]
    for cc in range(nch):
        par = cc % 2
        wts = [] if prev_ch_ld[par] is None else list(prev_ch_ld[par])
        dma("sync", ztt[0:64, par * 2048:(par + 1) * 2048].rearrange("p (h t) -> p h t", h=16),
            Dzt.ap()[cc * 1024:(cc + 1) * 1024, :].rearrange("(h p) t -> p h t", p=64),
            waits=wts, stream=f"p2z{par}")
        dma("sync", outt[:, par * 2048:(par + 1) * 2048].rearrange("p (h t) -> p h t", h=16),
            Dou.ap()[cc * 2048:(cc + 1) * 2048, :].rearrange("(h p) t -> p h t", p=128),
            stream=f"p2z{par}")
        dma("sync", ycc[0:64, par * 2048:(par + 1) * 2048].rearrange("p (h t) -> p h t", h=16),
            Dyc.ap()[cc * 1024:(cc + 1) * 1024, :].rearrange("(h p) t -> p h t", p=64),
            stream=f"p2z{par}")
        dma("sync", ngc[0:64, par * 1024:(par + 1) * 1024].rearrange("p (h t) -> p h t", h=16),
            DnegT.ap()[cc * 1024:(cc + 1) * 1024, :].rearrange("(h p) t -> p h t", p=64),
            stream=f"p2z{par}")
        chld = dma("sync", gout_t[:, par * 2048:(par + 1) * 2048].rearrange("p (h t) -> p h t", h=16),
                   Dgsp.ap()[:, cc * 128:(cc + 1) * 128].rearrange("(h p) t -> p h t", p=128),
                   stream=f"p2z{par}")
        for hh in range(16):
            ch = cc * 16 + hh
            zts = ztt[0:64, par * 2048 + hh * 128: par * 2048 + (hh + 1) * 128]
            ous = outt[:, par * 2048 + hh * 128: par * 2048 + (hh + 1) * 128]
            ycs2 = ycc[0:64, par * 2048 + hh * 128: par * 2048 + (hh + 1) * 128]
            gos = gout_t[:, par * 2048 + hh * 128: par * 2048 + (hh + 1) * 128]
            ngs = ngc[0:64, par * 1024 + hh * 64: par * 1024 + (hh + 1) * 64]

            def precast(eng, hh=hh, ch=ch):
                eng.tensor_copy(sbf[0:64, 0:128], S_f[0:64, hh * 128:(hh + 1) * 128])
                return eng.tensor_scalar(identg[0:64, 0:64], cst["identbf"][0:64, 0:64],
                                         glb[0:64, ch:ch + 1], None, ALU.mult)
            S.emit("vector", precast, waits=[("pe", S.val("pe")), glb_ld], inc=("dve", 1))
            pcv = S.val("dve")

            def outmm(eng, zts=zts, ous=ous):
                eng.matmul(pb[2][:, 0:128], sbf[0:64, 0:128], zts, start=True, stop=False)
                return eng.matmul(pb[2][:, 0:128], cst["identbf"][:, :], ous, start=False, stop=True)
            S.emit("tensor", outmm, waits=[("dve", pcv), chld,
                                           ("dve", bg_guard[2].get("dve", 0)),
                                           ("act", bg_guard[2].get("act", 0))], inc=("pe", 1))
            outpe = S.val("pe")

            def chainmm(eng, ngs=ngs, ycs2=ycs2):
                eng.matmul(pb[1][0:64, 0:128], ngs, sbf[0:64, 0:128], start=True, stop=False)
                eng.matmul(pb[1][0:64, 0:128], identg[0:64, 0:64], sbf[0:64, 0:128], start=False, stop=False)
                return eng.matmul(pb[1][0:64, 0:128], cst["identbf"][0:64, 0:64], ycs2, start=False, stop=True)
            S.emit("tensor", chainmm, waits=[chld, ("dve", bg_guard[1].get("dve", 0)),
                                             ("act", bg_guard[1].get("act", 0))], inc=("pe", 1))
            chpe = S.val("pe")
            S.emit("vector", lambda eng, hh=hh: eng.tensor_copy(S_f[0:64, hh * 128:(hh + 1) * 128],
                                                                pb[1][0:64, 0:128]),
                   waits=[("pe", chpe)], inc=("dve", 1))
            bg_guard[1]["dve"] = S.val("dve")

            S.emit("scalar", lambda eng: eng.activation(o2s[:, 0:128], pb[2][:, 0:128], AF.Square),
                   waits=[("pe", outpe)], inc=("act", 1))
            S.emit("tensor", lambda eng: eng.matmul(pb[4][0:1, 0:128], cst["ones128x1"][:, 0:1],
                                                    o2s[:, 0:128], start=True, stop=True),
                   waits=[("act", S.val("act")), ("dve", bg_guard[4].get("dve", 0)),
                          ("act", bg_guard[4].get("act", 0))], inc=("pe", 1))
            S.emit("scalar", lambda eng: eng.activation(rrow[0:1, 0:128], pb[4][0:1, 0:128], AF.Sqrt,
                                                        scale=1.0 / DV, bias=cvals[0:1, 2:3]),
                   waits=[("pe", S.val("pe"))], inc=("act", 1))
            bg_guard[4]["act"] = S.val("act")
            S.emit("vector", lambda eng: eng.reciprocal(rrow[0:1, 0:128], rrow[0:1, 0:128]),
                   waits=[("act", S.val("act"))], inc=("dve", 1))
            S.emit("tensor", lambda eng: eng.matmul(pb[5][:, 0:128], cst["onesrow"][0:1, 0:128],
                                                    rrow[0:1, 0:128], start=True, stop=True),
                   waits=[("dve", S.val("dve")), ("dve", bg_guard[5].get("dve", 0)),
                          ("act", bg_guard[5].get("act", 0))], inc=("pe", 1))
            S.emit("vector", lambda eng: eng.tensor_copy(rstdb[:, 0:128], pb[5][:, 0:128]),
                   waits=[("pe", S.val("pe"))], inc=("dve", 1))
            bg_guard[5]["dve"] = S.val("dve")
            S.emit("vector", lambda eng: eng.tensor_tensor(o2s[:, 0:128], pb[2][:, 0:128], rstdb[:, 0:128],
                                                           ALU.mult),
                   inc=("dve", 1))
            bg_guard[2]["dve"] = S.val("dve")
            wog = [chld]
            if hh == 0 and stog[par] is not None:
                wog.append(stog[par])
            S.emit("vector", lambda eng, hh=hh, gos=gos, par=par: eng.tensor_tensor(
                og[:, par * 2048 + hh * 128: par * 2048 + (hh + 1) * 128], o2s[:, 0:128], gos, ALU.mult),
                waits=wog, inc=("dve", 1))
        ogv = S.val("dve")
        prev_ch_ld[par] = [("dve", ogv), ("pe", S.val("pe"))]
        stog[par] = dma("sync", Dog.ap()[cc * 128:(cc + 1) * 128, :], og[:, par * 2048:(par + 1) * 2048],
                        waits=[("dve", ogv)], stream=f"stog{par}")
    A.free(g2a_)
    A.free("sfin")
    barP2a = S.barrier()

    S.phase("p2b")
    g2b_ = "p2b"
    owTb = A.sb(g2b_, "owTb", [128, 16 * C], BF16)
    mbc_ga = A.sb(g2b_, "mbc_ga", [128, C], BF16)
    mbc_cb = A.sb(g2b_, "mbc_cb", [128, C], BF16)
    mbc_sb = A.sb(g2b_, "mbc_sb", [128, C], BF16)
    g1bb = A.sb(g2b_, "g1bb", [128, C], F32)
    g2bb = A.sb(g2b_, "g2bb", [128, C], F32)
    og2 = A.sb(g2b_, "og2", [128, 2 * 2048], BF16)
    xtile = A.sb(g2b_, "xtile", [128, C], F32)
    x1t = A.sb(g2b_, "x1t", [128, C], F32)
    h2t = A.sb(g2b_, "h2t", [128, C], F32)
    qsc = A.sb(g2b_, "qsc", [128, C], F32)
    itok = A.sb(g2b_, "itok", [128, C], BF16)
    ibf = A.sb(g2b_, "ibf", [128, 4 * 128], BF16)
    st2 = A.sb(g2b_, "st2", [128, 8], F32)

    dma("gpsimd", mbc_ga[:, :], Dmodvec.ap()[2:3, :].partition_broadcast(128), waits=[modvec_st], sem="gmbc")
    dma("gpsimd", mbc_cb[:, :], Dmodvec.ap()[3:4, :].partition_broadcast(128), sem="gmbc")
    dma("gpsimd", mbc_sb[:, :], Dmodvec.ap()[4:5, :].partition_broadcast(128), sem="gmbc")
    mbc_gld = ("gmbc", S.val("gmbc"))
    dma("sync", g1bb[:, :], I["g1v"].partition_broadcast(128), stream="mbc")
    dma("sync", g2bb[:, :], I["g2v"].partition_broadcast(128), stream="mbc")
    g12_ld = ("q_mbc", S.val("q_mbc"))
    ldow = dma("gpsimd", owTb[:, :].rearrange("p (v o) -> p v o", v=16),
               I["owT"].rearrange("(v p) o -> p v o", p=128), sem="gow")

    ip_slot_ev = {}
    og2_pe = [0, 0]
    x1_st_prev = None
    prev_x1v = 0
    prev_act_tail = 0
    for cc in range(nch):
        par = cc % 2
        ldog = dma("gpsimd", og2[:, par * 2048:(par + 1) * 2048], Dog.ap()[cc * 128:(cc + 1) * 128, :],
                   waits=[("pe", og2_pe[par])], sem=f"gog{par}")
        ldx = dma("sync", xtile[:, :], I["x_sl"][3 + cc * 128:3 + (cc + 1) * 128, :],
                  waits=[("dve", prev_x1v)], stream="x2")
        for o2i in range(2):
            for half2 in range(2):
                oc = o2i * 2 + half2
                bank = 0 if half2 == 0 else 3

                def owmm(eng, oc=oc, bank=bank, par=par):
                    last = None
                    for hh in range(16):
                        last = eng.matmul(pb[bank][:, 0:512],
                                          og2[:, par * 2048 + hh * 128: par * 2048 + (hh + 1) * 128],
                                          owTb[:, hh * C + oc * 512: hh * C + oc * 512 + 512],
                                          start=(hh == 0), stop=(hh == 15))
                    return last
                S.emit("tensor", owmm, waits=[ldog, ldow,
                                              ("dve", bg_guard[bank].get("dve", 0)),
                                              ("act", bg_guard[bank].get("act", 0))], inc=("pe", 1))
                opev = S.val("pe")

                def x1ev(eng, oc=oc, bank=bank):
                    eng.tensor_tensor(qsc[:, oc * 512:(oc + 1) * 512], pb[bank][:, 0:512],
                                      mbc_ga[:, oc * 512:(oc + 1) * 512], ALU.mult)
                    return eng.tensor_add(x1t[:, oc * 512:(oc + 1) * 512], qsc[:, oc * 512:(oc + 1) * 512],
                                          xtile[:, oc * 512:(oc + 1) * 512])
                wx1e = [("pe", opev), ldx, mbc_gld, ("act", prev_act_tail)]
                if x1_st_prev is not None:
                    wx1e.append(x1_st_prev)
                S.emit("vector", x1ev, waits=wx1e, inc=("dve", 1))
                bg_guard[bank]["dve"] = S.val("dve")
        og2_pe[par] = S.val("pe")
        x1v = S.val("dve")
        wx1 = [("dve", x1v)] if x1_st_prev is None else [("dve", x1v), x1_st_prev]
        x1_st_prev = dma("sync", Dx1.ap()[cc * 128:(cc + 1) * 128, :], x1t[:, :], waits=wx1, stream="stx1")

        S.emit("vector", lambda eng: eng.reduce_sum(st2[:, 0:1], x1t[:, :], AX.X),
               waits=[("dve", x1v)], inc=("dve", 1))
        S.emit("scalar", lambda eng: eng.activation(qsc[:, :], x1t[:, :], AF.Square, accum_out=st2[:, 1:2]),
               waits=[("dve", x1v), ("act", prev_act_tail)], inc=("act", 1))

        def st2f(eng):
            eng.tensor_scalar(st2[:, 0:1], st2[:, 0:1], 1.0 / C, None, ALU.mult)
            eng.tensor_scalar(st2[:, 2:3], st2[:, 1:2], 1.0 / C, None, ALU.mult)
            eng.tensor_mul(st2[:, 3:4], st2[:, 0:1], st2[:, 0:1])
            return eng.tensor_sub(st2[:, 2:3], st2[:, 2:3], st2[:, 3:4])
        S.emit("vector", st2f, waits=[("act", S.val("act"))], inc=("dve", 1))
        S.emit("scalar", lambda eng: eng.activation(st2[:, 4:5], st2[:, 2:3], AF.Sqrt, bias=cvals[:, 0:1]),
               waits=[("dve", S.val("dve"))], inc=("act", 1))
        S.emit("vector", lambda eng: eng.reciprocal(st2[:, 5:6], st2[:, 4:5]),
               waits=[("act", S.val("act"))], inc=("dve", 1))
        S.emit("vector", lambda eng: eng.tensor_scalar(h2t[:, :], x1t[:, :], st2[:, 0:1], st2[:, 5:6],
                                                       ALU.subtract, ALU.mult),
               inc=("dve", 1))
        S.emit("vector", lambda eng: eng.tensor_mul(h2t[:, :], h2t[:, :], mbc_cb[:, :]),
               waits=[mbc_gld], inc=("dve", 1))
        S.emit("vector", lambda eng: eng.tensor_tensor(h2t[:, :], h2t[:, :], mbc_sb[:, :], ALU.add),
               inc=("dve", 1))
        h2v = S.val("dve")
        S.emit("scalar", lambda eng: eng.activation(qsc[:, :], h2t[:, :], AF.Square, accum_out=st2[:, 6:7]),
               waits=[("dve", h2v)], inc=("act", 1))
        S.emit("vector", lambda eng: eng.tensor_scalar(st2[:, 6:7], st2[:, 6:7], 1.0 / C, None, ALU.mult),
               waits=[("act", S.val("act"))], inc=("dve", 1))
        S.emit("scalar", lambda eng: eng.activation(st2[:, 7:8], st2[:, 6:7], AF.Sqrt, bias=cvals[:, 3:4]),
               waits=[("dve", S.val("dve"))], inc=("act", 1))
        S.emit("vector", lambda eng: eng.reciprocal(st2[:, 7:8], st2[:, 7:8]),
               waits=[("act", S.val("act"))], inc=("dve", 1))
        rstd2v = S.val("dve")
        prev_act_tail = S.val("act")

        for br, (gbt, sft, di, fcol) in enumerate(((g1bb, s1f, Di1, 0), (g2bb, s2f, Di2, 1))):
            S.emit("vector", lambda eng, gbt=gbt: eng.scalar_tensor_tensor(
                qsc[:, :], h2t[:, :], st2[:, 7:8], gbt[:, :], ALU.mult, ALU.mult),
                waits=[("dve", rstd2v), g12_ld, ("act", prev_act_tail)], inc=("dve", 1))
            S.emit("vector", lambda eng: eng.tensor_reduce(st2[:, 6:7], qsc[:, :], AX.X, ALU.max,
                                                           apply_absolute_value=True),
                   inc=("dve", 1))

            def qsc1(eng, sft=sft, fcol=fcol, cc=cc):
                eng.tensor_scalar(st2[:, 6:7], st2[:, 6:7], 1e-4, None, ALU.max)
                eng.tensor_scalar(sft[:, cc:cc + 1], st2[:, 6:7], fscb[:, fcol:fcol + 1], None, ALU.mult)
                return eng.reciprocal(st2[:, 6:7], st2[:, 6:7])
            S.emit("vector", qsc1, waits=[fscb_ld], inc=("dve", 1))

            def qround(eng):
                eng.tensor_scalar(qsc[:, :], qsc[:, :], st2[:, 6:7], None, ALU.mult)
                eng.tensor_scalar(qsc[:, :], qsc[:, :], 127.0, CMAGIC, ALU.mult, ALU.add)
                eng.tensor_scalar(qsc[:, :], qsc[:, :], CMAGIC - 128.0, CMAGIC + 127.0, ALU.max, ALU.min)
                return eng.tensor_scalar(itok[:, :], qsc[:, :], -CMAGIC, None, ALU.add)
            S.emit("vector", qround, waits=[("pe", S.val("pe"))], inc=("dve", 1))
            qv = S.val("dve")
            for ic in range(16):
                slot = ic % 4
                bank = slot % 2
                pcol = bank * 1024 + (slot // 2) * 128
                key = slot
                wt = [("dve", qv)]
                gkb = "bfA" if bank == 0 else "bfB"
                for snm, v in bg_guard[gkb].items():
                    if v > 0:
                        wt.append((snm, v))
                S.emit("tensor", lambda eng, ic=ic, pcol=pcol: eng.transpose(
                    pbf[:, pcol:pcol + 128], itok[:, ic * 128:(ic + 1) * 128], cst["identbf"][:, :]),
                    waits=wt, inc=("pe", 1))
                wtc = [("pe", S.val("pe"))]
                if key in ip_slot_ev:
                    wtc.append(ip_slot_ev[key])
                S.emit("vector", lambda eng, slot=slot, pcol=pcol: eng.tensor_copy(
                    ibf[:, slot * 128:(slot + 1) * 128], pbf[:, pcol:pcol + 128]),
                    waits=wtc, inc=("dve", 1))
                bg_guard[gkb]["dve"] = S.val("dve")
                ip_slot_ev[key] = dma("sync", di.ap()[(ic * nch + cc) * 128:(ic * nch + cc + 1) * 128, :],
                                      ibf[:, slot * 128:(slot + 1) * 128],
                                      waits=[("dve", S.val("dve"))], stream=f"sti{br}{slot}")
        prev_x1v = S.val("dve")
    A.free(g2b_)
    A.free("stats")
    barP2 = S.barrier()
    if "x1" in taps:
        dma("sync", taps["x1"], Dx1.ap(), stream="sttap")
    if "i1" in taps:
        dma("sync", taps["i1"], Di1.ap(), stream="sttap")

    if maxphase < 9:
        _finish(nc, S, O, ntok)
        return nc, taps

    S.phase("f1")
    gf1 = "f1"
    i1r = A.sb(gf1, "i1r", [128, 16 * nch * 128], BF16)
    i2r = A.sb(gf1, "i2r", [128, 16 * nch * 128], BF16)
    w12 = A.sb(gf1, "w12", [128, 32 * 512], BF16)
    hswt = A.sb(gf1, "hswt", [128, 512], F32)
    hbf = A.sb(gf1, "hbf", [128, 512], BF16)
    g3p = A.sb(gf1, "g3p", [128, 512], F32)
    y2t = A.sb(gf1, "y2t", [128, 512], F32)
    g3sl = A.sb(gf1, "g3sl", [128, 2 * 512], F32)

    ld1 = dma("sync", i1r[:, :].rearrange("p (a t) -> p a t", a=16 * nch),
              Di1.ap().rearrange("(a p) t -> p a t", p=128), stream="f1i1")
    ld2 = dma("sync", i2r[:, :].rearrange("p (a t) -> p a t", a=16 * nch),
              Di2.ap().rearrange("(a p) t -> p a t", p=128), stream="f1i2")
    S.emit("vector", lambda eng: (eng.memset(mx3[:, :], 0.0), eng.memset(ss3[:, :], 0.0))[-1],
           inc=("dve", 1))
    f1init = S.val("dve")

    hst_prev = None
    wcons_pe = 0
    g3_cons = [0, 0]
    prev_hbf_v = 0
    prev_ss3_v = 0
    for n in range(HID // 512):
        g3par = n % 2
        wg3 = [("dve", g3_cons[g3par])] if g3_cons[g3par] else []
        g3ld = dma("gpsimd", g3sl[:, g3par * 512:(g3par + 1) * 512],
                   I["g3v"][:, n * 512:(n + 1) * 512].partition_broadcast(128), waits=wg3,
                   sem=f"g3l{g3par}")
        ldws_by = [[], []]
        for wsel in range(2):
            base_row = 0 if wsel == 0 else 256
            for r in range(8):
                ldws_by[wsel].append(dma("sync",
                                         w12[:, (wsel * 16 + r * 2) * 512:(wsel * 16 + r * 2 + 2) * 512]
                                         .rearrange("p (a t) -> p a t", a=2),
                                         Dwq_gath.ap()[r * 768 + base_row: r * 768 + base_row + 256,
                                                       n * 512:(n + 1) * 512]
                                         .rearrange("(a p) t -> p a t", p=128),
                                         waits=[("cc_wq", 1), ("pe", wcons_pe)], stream=f"f1w{wsel}"))
        for tc in range(nch):
            pe12 = []
            for wsel, bank, ir, ldi in ((0, 0, i1r, ld1), (1, 1, i2r, ld2)):
                def f1mm(eng, tc=tc, wsel=wsel, bank=bank, ir=ir):
                    last = None
                    for ic in range(16):
                        wslot = wsel * 16 + ic
                        last = eng.matmul(pb[bank][:, 0:512],
                                          ir[:, (ic * nch + tc) * 128:(ic * nch + tc + 1) * 128],
                                          w12[:, wslot * 512:(wslot + 1) * 512],
                                          start=(ic == 0), stop=(ic == 15))
                    return last
                wt = [ldi, ldws_by[wsel][-1], ("dve", bg_guard[bank].get("dve", 0)),
                      ("act", bg_guard[bank].get("act", 0))]
                S.emit("tensor", f1mm, waits=wt, inc=("pe", 1))
                pe12.append(S.val("pe"))
            wcons_pe = S.val("pe")
            S.emit("scalar", lambda eng, tc=tc: eng.activation(hswt[:, 0:512], pb[0][:, 0:512], AF.Silu,
                                                               scale=s1f[:, tc:tc + 1]),
                   waits=[("pe", pe12[0]), ("dve", prev_hbf_v), ("dve", prev_ss3_v)], inc=("act", 1))
            bg_guard[0]["act"] = S.val("act")
            S.emit("vector", lambda eng, tc=tc: eng.tensor_scalar(y2t[:, 0:512], pb[1][:, 0:512],
                                                                  s2f[:, tc:tc + 1], None, ALU.mult),
                   waits=[("pe", pe12[1])], inc=("dve", 1))
            bg_guard[1]["dve"] = S.val("dve")
            hw = [("act", S.val("act"))]
            if hst_prev is not None:
                hw.append(hst_prev)
            S.emit("vector", lambda eng: eng.tensor_tensor(hbf[:, 0:512], hswt[:, 0:512], y2t[:, 0:512],
                                                           ALU.mult),
                   waits=hw, inc=("dve", 1))
            hv = S.val("dve")
            prev_hbf_v = hv
            S.emit("scalar", lambda eng: eng.activation(hswt[:, 0:512], hbf[:, 0:512], AF.Square,
                                                        accum_out=sc3[:, 0:1]),
                   waits=[("dve", hv), ("dve", prev_ss3_v)], inc=("act", 1))
            S.emit("vector", lambda eng, tc=tc: eng.tensor_add(ss3[:, tc:tc + 1], ss3[:, tc:tc + 1],
                                                               sc3[:, 0:1]),
                   waits=[("act", S.val("act")), ("dve", f1init)], inc=("dve", 1))
            prev_ss3_v = S.val("dve")
            S.emit("vector", lambda eng, g3par=g3par: eng.tensor_tensor(
                g3p[:, 0:512], hbf[:, 0:512], g3sl[:, g3par * 512:(g3par + 1) * 512], ALU.mult),
                waits=[g3ld], inc=("dve", 1))
            S.emit("vector", lambda eng: eng.tensor_reduce(sc3[:, 1:2], g3p[:, 0:512], AX.X, ALU.max,
                                                           apply_absolute_value=True),
                   inc=("dve", 1))
            S.emit("vector", lambda eng, tc=tc: eng.tensor_max(mx3[:, tc:tc + 1], mx3[:, tc:tc + 1],
                                                               sc3[:, 1:2]),
                   inc=("dve", 1))
            hst_prev = dma("sync", Dh.ap()[tc * 128:(tc + 1) * 128, n * 512:(n + 1) * 512], hbf[:, 0:512],
                           waits=[("dve", hv)], stream="sth")
        g3_cons[g3par] = S.val("dve")

    S.emit("vector", lambda eng: eng.tensor_scalar(ss3[:, :], ss3[:, :], 1.0 / HID, None, ALU.mult),
           waits=[("dve", S.val("dve")), hst_prev], inc=("dve", 1))
    S.emit("scalar", lambda eng: eng.activation(ss3[:, :], ss3[:, :], AF.Sqrt, bias=cvals[:, 3:4]),
           waits=[("dve", S.val("dve"))], inc=("act", 1))

    def rs3b(eng):
        eng.reciprocal(ss3[:, :], ss3[:, :])
        eng.tensor_mul(mx3[:, :], mx3[:, :], ss3[:, :])
        eng.tensor_scalar(mx3[:, :], mx3[:, :], 1e-4, None, ALU.max)
        eng.tensor_scalar(s3f[:, :], mx3[:, :], fscb[:, 2:3], None, ALU.mult)
        return eng.reciprocal(mx3[:, :], mx3[:, :])
    S.emit("vector", rs3b, waits=[("act", S.val("act")), fscb_ld], inc=("dve", 1))
    if "f2sc" in taps:
        def scdump(eng):
            for i, t in enumerate((s1f, s2f, s3f, mx3, ss3)):
                eng.tensor_copy(hswt[:, i * 16:i * 16 + t.shape[1]], t[:, :])
            return eng.tensor_copy(hswt[:, 80:82], sc3[:, 0:2])
        S.emit("vector", scdump, waits=[("dve", S.val("dve"))], inc=("dve", 1))
        dma("sync", taps["f2sc"][:, 0:96], hswt[:, 0:96], waits=[("dve", S.val("dve"))], stream="sttap")
    A.free(gf1)
    barF1 = S.barrier()
    if "hmid" in taps:
        dma("sync", taps["hmid"], Dh.ap(), stream="sttap")

    if maxphase < 10:
        _finish(nc, S, O, ntok)
        return nc, taps

    S.phase("f2")
    gf2 = "f2"
    QT = min(512, ntok)
    nblk = ntok // QT
    nch_q = QT // 128
    i3h = A.sb(gf2, "i3h", [128, 2 * 64 * QT], BF16)
    hqb = A.sb(gf2, "hqb", [128, 2 * 512], BF16)
    q32 = A.sb(gf2, "q32", [128, 512], F32)
    i3t = A.sb(gf2, "i3t", [128, 512], BF16)
    w3t = A.sb(gf2, "w3t", [128, 2 * 2048], BF16)
    x1r = A.sb(gf2, "x1r", [128, 2 * 512], F32)
    yout = A.sb(gf2, "yout", [128, 2 * 512], F32)
    g3s2 = A.sb(gf2, "g3s2", [128, 2 * 512], F32)
    mbc_gb = A.sb(gf2, "mbc_gb", [128, C], BF16)
    dma("gpsimd", mbc_gb[:, :], Dmodvec.ap()[5:6, :].partition_broadcast(128), sem="gmb2")
    gb_ld = ("gmb2", S.val("gmb2"))

    yst_prev = [None, None]
    hld_prev = [None, None]
    g3_cons2 = [0, 0]
    i3pe = [0, 0]
    i3v = [0, 0]
    banks4 = [0, 1, 3, 4]
    x1r_cons = [0, 0]
    w3slot_ev = [None, None]

    def f2_assemble(blk_i):
        par = blk_i % 2
        for n in range(HID // 512):
            g3par = n % 2
            wg3 = [("dve", g3_cons2[g3par])] if g3_cons2[g3par] else []
            g3ld2 = dma("gpsimd", g3s2[:, g3par * 512:(g3par + 1) * 512],
                        I["g3v"][:, n * 512:(n + 1) * 512].partition_broadcast(128), waits=wg3,
                        sem=f"g3m{g3par}")
            for tcl in range(nch_q):
                tc = blk_i * nch_q + tcl
                hslot = (n * nch_q + tcl) % 2
                wth = [("q_sth", S.cnt["q_sth"])]
                if hld_prev[hslot] is not None:
                    wth.append(hld_prev[hslot])
                ldh = dma("sync", hqb[:, hslot * 512:(hslot + 1) * 512],
                          Dh.ap()[tc * 128:(tc + 1) * 128, n * 512:(n + 1) * 512],
                          waits=wth, stream=f"f2h{hslot}")

                def i3q(eng, tc=tc, g3par=g3par, hslot=hslot):
                    eng.scalar_tensor_tensor(q32[:, 0:512], hqb[:, hslot * 512:(hslot + 1) * 512],
                                             ss3[:, tc:tc + 1], g3s2[:, g3par * 512:(g3par + 1) * 512],
                                             ALU.mult, ALU.mult)
                    eng.tensor_scalar(q32[:, 0:512], q32[:, 0:512], mx3[:, tc:tc + 1], None, ALU.mult)
                    eng.tensor_scalar(q32[:, 0:512], q32[:, 0:512], 127.0, CMAGIC, ALU.mult, ALU.add)
                    eng.tensor_scalar(q32[:, 0:512], q32[:, 0:512], CMAGIC - 128.0, CMAGIC + 127.0,
                                      ALU.max, ALU.min)
                    return eng.tensor_scalar(i3t[:, 0:512], q32[:, 0:512], -CMAGIC, None, ALU.add)
                S.emit("vector", i3q, waits=[ldh, g3ld2, ("pe", S.val("pe"))], inc=("dve", 1))
                hld_prev[hslot] = ("dve", S.val("dve"))
                qv = S.val("dve")
                for s4 in range(4):
                    icg = n * 4 + s4
                    bank = s4 % 2
                    pcol = bank * 1024 + (s4 // 2) * 128
                    gkb = "bfA" if bank == 0 else "bfB"
                    wt = [("dve", qv), ("pe", i3pe[par])]
                    for snm, v in bg_guard[gkb].items():
                        if v > 0:
                            wt.append((snm, v))
                    S.emit("tensor", lambda eng, s4=s4, pcol=pcol: eng.transpose(
                        pbf[:, pcol:pcol + 128], i3t[:, s4 * 128:(s4 + 1) * 128], cst["identbf"][:, :]),
                        waits=wt, inc=("pe", 1))
                    S.emit("vector", lambda eng, icg=icg, tcl=tcl, pcol=pcol, par=par: eng.tensor_copy(
                        i3h[:, par * 64 * QT + icg * QT + tcl * 128:
                            par * 64 * QT + icg * QT + (tcl + 1) * 128],
                        pbf[:, pcol:pcol + 128]),
                        waits=[("pe", S.val("pe"))], inc=("dve", 1))
                    bg_guard[gkb]["dve"] = S.val("dve")
            g3_cons2[g3par] = S.val("dve")
        i3v[par] = S.val("dve")

    def f2_matmul(blk_i):
        par = blk_i % 2
        for oc in range(4):
            for icg in range(16):
                wslot = icg % 2
                g = icg * 512
                r, j = g // 1024, g % 1024
                blk = Dwq_gath.ap()[r * 768 + 512: r * 768 + 768, :].rearrange(
                    "a (b c) -> (a b) c", b=4)
                wt = [("cc_wq", 1)]
                if w3slot_ev[wslot] is not None:
                    wt.append(w3slot_ev[wslot])
                ldw3 = dma("gpsimd", w3t[:, wslot * 2048:(wslot + 1) * 2048]
                           .rearrange("p (a c) -> p a c", a=4),
                           blk[j:j + 512, oc * 512:(oc + 1) * 512].rearrange("(a p) c -> p a c", p=128),
                           waits=wt, sem=f"gw3{wslot}")
                for tci in range(nch_q):
                    bank = banks4[tci]

                    def f2mm(eng, icg=icg, tci=tci, wslot=wslot, bank=bank, par=par):
                        last = None
                        for a in range(4):
                            ic = icg * 4 + a
                            last = eng.matmul(
                                pb[bank][:, 0:512],
                                i3h[:, par * 64 * QT + ic * QT + tci * 128:
                                    par * 64 * QT + ic * QT + (tci + 1) * 128],
                                w3t[:, wslot * 2048 + a * 512: wslot * 2048 + (a + 1) * 512],
                                start=(ic == 0), stop=(ic == 63))
                        return last
                    wtm = [ldw3, ("dve", i3v[par])]
                    if icg == 0:
                        wtm += [("dve", bg_guard[bank].get("dve", 0)),
                                ("act", bg_guard[bank].get("act", 0))]
                    S.emit("tensor", f2mm, waits=wtm, inc=("pe", 1))
                w3slot_ev[wslot] = ("pe", S.val("pe"))
            mmend = S.val("pe")
            for tci in range(nch_q):
                tc = blk_i * nch_q + tci
                bank = banks4[tci]
                xsl = tci % 2
                ldx1 = dma("sync", x1r[:, xsl * 512:(xsl + 1) * 512],
                           Dx1.ap()[tc * 128:(tc + 1) * 128, oc * 512:(oc + 1) * 512],
                           waits=[x1_st_prev, ("dve", x1r_cons[xsl])], stream=f"f2x{xsl}")

                def yev(eng, tc=tc, oc=oc, bank=bank, xsl=xsl):
                    eng.tensor_scalar(yout[:, xsl * 512:(xsl + 1) * 512], pb[bank][:, 0:512],
                                      s3f[:, tc:tc + 1], None, ALU.mult)
                    eng.tensor_tensor(yout[:, xsl * 512:(xsl + 1) * 512],
                                      yout[:, xsl * 512:(xsl + 1) * 512],
                                      mbc_gb[:, oc * 512:(oc + 1) * 512], ALU.mult)
                    return eng.tensor_add(yout[:, xsl * 512:(xsl + 1) * 512],
                                          yout[:, xsl * 512:(xsl + 1) * 512], x1r[:, xsl * 512:(xsl + 1) * 512])
                wyev = [("pe", mmend), ldx1, gb_ld]
                if yst_prev[xsl] is not None:
                    wyev.append(yst_prev[xsl])
                S.emit("vector", yev, waits=wyev, inc=("dve", 1))
                bg_guard[bank]["dve"] = S.val("dve")
                x1r_cons[xsl] = S.val("dve")
                yst_prev[xsl] = dma("sync", O["y"][tc * 128:(tc + 1) * 128, oc * 512:(oc + 1) * 512],
                                    yout[:, xsl * 512:(xsl + 1) * 512],
                                    waits=[("dve", S.val("dve"))], stream=f"sty{xsl}")
        i3pe[par] = S.val("pe")

    f2_assemble(0)
    for blk_i in range(nblk):
        if blk_i + 1 < nblk:
            f2_assemble(blk_i + 1)
        f2_matmul(blk_i)
    print(f"[alloc] peak {A.peak}B/part at {A.peak_at}")
    _finish(nc, S, O, ntok, y_done=True)
    return nc, taps


def _finish(nc, S, O, ntok, y_done=False):
    if not y_done:
        ztile = nc.sbuf_tensor("zfin", [128, C], F32).__enter__()
        S.emit("vector", lambda eng: eng.memset(ztile[:, :], 0.0), inc=("dve", 1))
        vz = S.val("dve")
        for r in range(0, ntok, 128):
            S.emit("sync", lambda eng, r=r: eng.dma_start(out=O["y"][r:r + 128, :], in_=ztile[:, :]),
                   waits=[("dve", vz)], inc=("st", 16))
    finals = [(nm, S.cnt[nm]) for nm in list(S.cnt)
              if (nm.startswith("q_st") or nm in ("st", "gst")) and S.cnt[nm] > 0]

    def fin(eng):
        for nm, v in finals:
            eng.wait_ge(S.sems[nm], v)
    S.events.append(("sync", fin, "fin"))
    with nc.Block() as block:
        S.replay(block)


_CACHE = {}


def kernel(**inputs):
    from concourse.bass_utils import run_bass_kernel_spmd
    ntok = 2048
    if "nc" not in _CACHE:
        nc, _ = build(ntok=ntok, debug_taps=(), maxphase=99)
        _CACHE["nc"] = nc
    nc = _CACHE["nc"]
    in_maps = host_inputs(inputs, ntok=ntok)
    res = run_bass_kernel_spmd(nc, in_maps, core_ids=list(range(NCORES)))
    y = np.zeros((2, 4 * ntok, C), np.float32)
    for c in range(NCORES):
        y[c // 4, (c % 4) * ntok:(c % 4 + 1) * ntok] = res.results[c]["y"]
    return y
